# version 05
# speedup vs baseline: 1.3768x; 1.0888x over previous
import numpy as np
import ml_dtypes
from contextlib import ExitStack
import concourse.bass as bass
import concourse.mybir as mybir
from concourse.bass_utils import run_bass_kernel_spmd

F32 = mybir.dt.float32
BF16 = mybir.dt.bfloat16
AF = mybir.ActivationFunctionType
ALU = mybir.AluOpType
AX = mybir.AxisListType
BF = ml_dtypes.bfloat16

D = 1024
DC = 8
TCORE = 2048
NTC = TCORE // 128
EPS_LN = 1e-5


ENGINES = ("tensor", "vector", "scalar", "gpsimd", "sync")

class PB:

    def __init__(self, nc, es):
        self.nc = nc
        self.es = es
        self.sems = {}
        self.cnt = {}
        for s in ENGINES + tuple("d" + e for e in ENGINES) + ("coll",):
            self.sems[s] = es.enter_context(nc.semaphore("S_" + s))
            self.cnt[s] = 0
        self.q = {e: [] for e in ENGINES}
        self.hwm = {}
        self.tail = {}

    def _waits(self, eng, deps, strict):
        need = {}
        for (s, v) in deps:
            if v > need.get(s, 0):
                need[s] = v
        ws = []
        for s, v in need.items():
            k = (eng, s)
            if self.hwm.get(k, 0) >= v:
                continue
            self.hwm[k] = v
            ws.append((s, v))
        return ws

    def emit(self, eng, fn, deps=(), strict=False, dma=False, coll=False):
        ws = self._waits(eng, deps, strict)
        if coll:
            sem = "coll"
        elif dma:
            sem = "d" + eng
        else:
            sem = eng
        self.cnt[sem] += 16 if (dma or coll) else 1
        tok = (sem, self.cnt[sem])
        self.q[eng].append((fn, ws, sem, 16 if (dma or coll) else 1))
        return tok

    def barrier_tok(self, eng):
        return (eng, self.cnt[eng])

    def all_toks(self):
        return [(s, c) for s, c in self.cnt.items() if c > 0]

    def dma(self, eng, out, in_, deps=(), strict=False):
        deps = list(deps) + [("d" + eng, self.cnt["d" + eng])]
        return self.emit(eng, lambda e, o=out, i=in_: e.dma_start(out=o, in_=i),
                         deps=deps, strict=strict, dma=True)

    def build(self):
        block = self.es.enter_context(self.nc.Block())
        for eng in ENGINES:
            items = self.q[eng]
            if not items:
                continue
            def runner(e, items=items):
                for (fn, ws, sem, amt) in items:
                    for (s, v) in ws:
                        e.wait_ge(self.sems[s], v)
                    inst = fn(e)
                    inst.then_inc(self.sems[sem], amt)
            getattr(block, eng)(runner)
        return block


def new_nc():
    return bass.Bass("TRN2", target_bir_lowering=False)



class Buf:
    __slots__ = ("t",)
    def __init__(self, t):
        self.t = t
    def __getitem__(self, idx):
        return self.t[idx]


def sb(pb, name, shape, dtype, side=None):
    return pb.es.enter_context(pb.nc.sbuf_tensor(name, list(shape), dtype, side=side))

def ps(pb, name, shape, dtype=F32):
    shape = list(shape)
    min_cols = 512 if dtype == F32 else 1024
    shape[1] = max(shape[1], min_cols)
    return pb.es.enter_context(pb.nc.psum_tensor(name, shape, dtype))



def emit_ln_tm(pb, x, xn_bf, stats, sq_scr, n_tc=NTC, tok_x=(), affine_wb=None,
               gamma_bc=None, beta_bc=None, out_extra=None):
    n = n_tc
    toks = []
    t_red = []
    for tc in range(n):
        xs = x[:, tc * D:(tc + 1) * D]
        t1 = pb.emit("scalar", lambda e, xs=xs, tc=tc: e.activation(
            sq_scr[:, 0:D], xs, AF.Square, accum_out=stats[:, n + tc:n + tc + 1]),
            deps=tok_x)
        t2 = pb.emit("vector", lambda e, xs=xs, tc=tc: e.reduce_sum(
            stats[:, tc:tc + 1], xs, axis=AX.X), deps=tok_x)
        t_red += [t1, t2]
    tmu = pb.emit("vector", lambda e: e.tensor_scalar(
        stats[:, 2 * n:3 * n], stats[:, 0:n], 1.0 / D, None, ALU.mult, ALU.bypass),
        deps=t_red, strict=True)
    tm2 = pb.emit("vector", lambda e: e.tensor_tensor(
        stats[:, 0:n], stats[:, 2 * n:3 * n], stats[:, 2 * n:3 * n], ALU.mult),
        deps=[tmu], strict=True)
    tvar = pb.emit("vector", lambda e: e.scalar_tensor_tensor(
        stats[:, n:2 * n], stats[:, n:2 * n], 1.0 / D, stats[:, 0:n],
        ALU.mult, ALU.subtract), deps=[tm2], strict=True)
    tve = pb.emit("vector", lambda e: e.tensor_scalar(
        stats[:, n:2 * n], stats[:, n:2 * n], EPS_LN, None, ALU.add, ALU.bypass),
        deps=[tvar], strict=True)
    tln = pb.emit("scalar", lambda e: e.activation(
        stats[:, n:2 * n], stats[:, n:2 * n], AF.Ln, bias=0.0, scale=1.0),
        deps=[tve], strict=True)
    trstd = pb.emit("scalar", lambda e: e.activation(
        stats[:, 3 * n:4 * n], stats[:, n:2 * n], AF.Exp, bias=0.0, scale=-0.5),
        deps=[tln], strict=True)
    for tc in range(n):
        xs = x[:, tc * D:(tc + 1) * D]
        os = xn_bf[:, tc * D:(tc + 1) * D]
        if affine_wb is None:
            tt = pb.emit("vector", lambda e, xs=xs, os=os, tc=tc: e.tensor_scalar(
                os, xs, stats[:, 2 * n + tc:2 * n + tc + 1],
                stats[:, 3 * n + tc:3 * n + tc + 1], ALU.subtract, ALU.mult),
                deps=[trstd], strict=True)
        else:
            g_bc, b_bc, scr = affine_wb
            t0 = pb.emit("vector", lambda e, xs=xs, tc=tc, scr=scr: e.tensor_scalar(
                scr[:, 0:D], xs, stats[:, 2 * n + tc:2 * n + tc + 1],
                stats[:, 3 * n + tc:3 * n + tc + 1], ALU.subtract, ALU.mult),
                deps=[trstd] + ldep + list(prev_aff), strict=True)
            t0b = pb.emit("vector", lambda e, scr=scr: e.tensor_tensor(
                scr[:, 0:D], scr[:, 0:D], g_bc[:, 0:D], ALU.mult),
                deps=[t0], strict=True)
            tt = pb.emit("vector", lambda e, os=os, scr=scr: e.tensor_tensor(
                os, scr[:, 0:D], b_bc[:, 0:D], ALU.add), deps=[t0b], strict=True)
        toks.append(tt)
    return toks, toks[-1]


def emit_tm_to_fm(pb, src_bf, dst_fm, ident_bf, pt, n_tc=NTC, deps_per_tile=None,
                  src_cols=D, dst_cols=None, extra_deps=()):
    if dst_cols is None:
        dst_cols = n_tc * 128
    ndc = src_cols // 128
    toks = [None] * ndc
    prev_evac = {0: (), 1: ()}
    k = 0
    for tc in range(n_tc):
        dep = list(deps_per_tile[tc]) if deps_per_tile else []
        dep += list(extra_deps)
        for dc in range(ndc):
            half = k % 2
            k += 1
            po = pt[half][:, 0:128]
            ts = pb.emit("tensor", lambda e, tc=tc, dc=dc, po=po: e.transpose(
                po, src_bf[:, tc * src_cols + dc * 128: tc * src_cols + dc * 128 + 128],
                ident_bf[:, 0:128]), deps=dep + list(prev_evac[half]))
            tv = pb.emit("scalar", lambda e, tc=tc, dc=dc, po=po: e.copy(
                dst_fm[:, dc * dst_cols + tc * 128: dc * dst_cols + tc * 128 + 128],
                po), deps=[ts])
            prev_evac[half] = (tv,)
            toks[dc] = tv
    return toks


def emit_weight_load(pb, dram_ap, stage, wbf, nchunks, cols, deps=(), eng_dma="sync",
                     cast_eng="scalar"):
    toks = []
    key = ("stage", stage.name if hasattr(stage, "name") else id(stage))
    prev_cast = pb.tail.get(key)
    for c in range(nchunks):
        sl = stage[:, 0:cols]
        dep = list(deps)
        if prev_cast is not None:
            dep.append(prev_cast)
        td = pb.dma(eng_dma, sl, dram_ap[c], deps=dep)
        prev_cast = pb.emit(cast_eng, lambda e, sl=sl, c=c: e.copy(
            wbf[:, c * cols:(c + 1) * cols], sl), deps=[td])
        toks.append(prev_cast)
    pb.tail[key] = prev_cast
    return toks, toks[-1]



def emit_routing_l0(pb, psum_tok, psum_r, routed_tm, rscr, tc, renorm=True):
    lg = rscr[:, 0:8]
    ex = rscr[:, 8:16]
    rw = rscr[:, 16:24]
    rw2 = rscr[:, 24:32]
    m = rscr[:, 32:33]
    nm = rscr[:, 33:34]
    s = rscr[:, 34:35]
    rs = rscr[:, 35:36]
    m1 = rscr[:, 36:37]
    m2 = rscr[:, 37:38]
    a = rscr[:, 38:39]
    d1 = rscr[:, 39:40]
    s2 = rscr[:, 40:41]
    d2 = rscr[:, 41:42]
    den = rscr[:, 42:43]
    coef = rscr[:, 43:44]
    mask1 = rscr[:, 44:52]
    mask2 = rscr[:, 52:60]
    tl = pb.emit("vector", lambda e: e.tensor_copy(lg, psum_r[:, 0:8]), deps=[psum_tok])
    t1 = pb.emit("vector", lambda e: e.reduce_max(m, lg, axis=AX.X), deps=[tl], strict=True)
    t2 = pb.emit("vector", lambda e: e.tensor_scalar(nm, m, -1.0, None, ALU.mult, ALU.bypass), deps=[t1], strict=True)
    t3 = pb.emit("scalar", lambda e: e.activation(ex, lg, AF.Exp, bias=nm[:, 0:1], scale=1.0), deps=[t2], strict=True)
    t4 = pb.emit("vector", lambda e: e.reduce_sum(s, ex, axis=AX.X), deps=[t3], strict=True)
    t5 = pb.emit("vector", lambda e: e.reciprocal(rs, s), deps=[t4], strict=True)
    t6 = pb.emit("vector", lambda e: e.tensor_scalar(rw, ex, rs[:, 0:1], None, ALU.mult, ALU.bypass), deps=[t5], strict=True)
    t7 = pb.emit("vector", lambda e: e.reduce_max(m1, rw, axis=AX.X), deps=[t6], strict=True)
    t8 = pb.emit("vector", lambda e: e.tensor_scalar(mask1, rw, m1[:, 0:1], None, ALU.is_ge, ALU.bypass), deps=[t7], strict=True)
    t9 = pb.emit("vector", lambda e: e.scalar_tensor_tensor(rw2, mask1, -1.0, rw, ALU.mult, ALU.add), deps=[t8], strict=True)
    t10 = pb.emit("vector", lambda e: e.reduce_max(m2, rw2, axis=AX.X), deps=[t9], strict=True)
    t11 = pb.emit("vector", lambda e: e.tensor_scalar(mask2, rw2, m2[:, 0:1], None, ALU.is_ge, ALU.bypass), deps=[t10], strict=True)
    if renorm:
        t12 = pb.emit("vector", lambda e: e.tensor_tensor(a, m1, m2, ALU.add), deps=[t11], strict=True)
        t13 = pb.emit("vector", lambda e: e.tensor_scalar(d1, a, 1e-9, None, ALU.add, ALU.bypass), deps=[t12], strict=True)
        rd1 = rscr[:, 60:61]
        t13b = pb.emit("vector", lambda e: e.reciprocal(rd1, d1), deps=[t13], strict=True)
        t14 = pb.emit("vector", lambda e: e.tensor_tensor(s2, a, rd1, ALU.mult), deps=[t13b], strict=True)
        t15 = pb.emit("vector", lambda e: e.tensor_scalar(d2, s2, 1e-9, None, ALU.add, ALU.bypass), deps=[t14], strict=True)
        t16 = pb.emit("vector", lambda e: e.tensor_tensor(den, d1, d2, ALU.mult), deps=[t15], strict=True)
        t17 = pb.emit("vector", lambda e: e.reciprocal(coef, den), deps=[t16], strict=True)
    else:
        t17 = pb.emit("vector", lambda e: e.memset(coef, 1.0), deps=[t11], strict=True)
    ta = pb.emit("vector", lambda e: e.tensor_scalar(mask1, mask1, m1[:, 0:1], coef[:, 0:1], ALU.mult, ALU.mult), deps=[t17], strict=True)
    tb = pb.emit("vector", lambda e: e.tensor_scalar(mask2, mask2, m2[:, 0:1], coef[:, 0:1], ALU.mult, ALU.mult), deps=[ta], strict=True)
    td = pb.emit("vector", lambda e: e.tensor_tensor(
        routed_tm[:, tc * 8:(tc + 1) * 8], mask1, mask2, ALU.add), deps=[tb], strict=True)
    return td




def emit_moe_l0(pb, bufs, wap, n_tc=NTC, fm_toks=None, renorm=True,
                shared=False, l1=False):
    x = bufs["x"]; xn = bufs["xn_fm"]
    TT_ = n_tc * 128
    fm_dep = ([fm_toks[-1]] if fm_toks else [])
    pr = bufs["ps_r"]
    prev_route = None
    for tc in range(n_tc):
        mm_toks = []
        for dc in range(DC):
            dep = list(fm_dep)
            if prev_route is not None and dc == 0:
                dep.append(prev_route)
            tmm = pb.emit("tensor", lambda e, tc=tc, dc=dc: e.matmul(
                pr[:, 0:8], xn[:, dc * TT_ + tc * 128: dc * TT_ + tc * 128 + 128],
                bufs["gwT"][:, dc * 8:(dc + 1) * 8],
                start=(dc == 0), stop=(dc == DC - 1)), deps=dep)
            mm_toks.append(tmm)
        if l1:
            prev_route = emit_routing_l1(pb, mm_toks[-1], pr, bufs["routed_tm"],
                                         bufs["rscr"], tc, bufs["bias_bc"],
                                         bias_tok=bufs.get("bias_tok", ("sync", 0)))
        else:
            prev_route = emit_routing_l0(pb, mm_toks[-1], pr, bufs["routed_tm"],
                                         bufs["rscr"], tc, renorm=renorm)
    route_done = prev_route

    n_exp = 8
    gu_bufs = [bufs["gu_bf0"], bufs["gu_bf1"]]
    dn_bufs = [bufs["dn_bf0"], bufs.get("dn_bf1", bufs["dn_bf0"])]
    h_bufs = [bufs["hbuf0"], bufs["hbuf1"]]
    w_free = [(), ()]
    h_free = {0: (), 1: ()}
    x_tok = {}
    psg = bufs["ps_g"]; psu = bufs["ps_u"]; psd = bufs["ps_d"]
    prev_g_use = ()
    prev_u_use = ()
    prev_sg_use = ()
    prev_d_use = {0: (), 1: ()}
    out_toks = []
    hf_major = bufs.get("hf_major", False)
    for e_i in range(n_exp + (1 if shared else 0)):
        is_shared = (e_i == n_exp)
        half = e_i % 2
        gub = gu_bufs[half]; dnb = dn_bufs[half]
        gu_src = wap["sgu"] if is_shared else wap["gu"][e_i]
        dn_src = wap["sdown"] if is_shared else wap["down"][e_i]
        if hf_major:
            gu_toks = emit_weight_load_mq(pb, gu_src, bufs["stages"], gub, 1024,
                                          deps=list(w_free[0]) + list(w_free[1]),
                                          order=[hf + half8 for hf in range(8)
                                                 for half8 in (0, 8)])
            dn_toks = emit_weight_load_mq(pb, dn_src, bufs["stages2"], dnb, 1024,
                                          deps=list(w_free[0]) + list(w_free[1]))
            tgu = gu_toks[-1]
            tdn = dn_toks[-1]
        else:
            _, tgu = emit_weight_load(pb, gu_src, bufs["stage"], gub, len(gu_src), 1024,
                                      deps=list(w_free[0]) + list(w_free[1]), eng_dma="sync")
            _, tdn = emit_weight_load(pb, dn_src, bufs["stage2"], dnb, len(dn_src), 1024,
                                      deps=list(w_free[0]) + list(w_free[1]), eng_dma="gpsimd")
            gu_toks = dn_toks = None
        last_use = []
        n_hb = 1 if h_bufs[1] is h_bufs[0] else 2
        for tt in range(n_tc // 4):
            hh = tt % n_hb
            hb = h_bufs[hh]
            h_ready = []
            for hf in range(8):
                gmm = umm = None
                if hf_major:
                    gdep = [gu_toks[hf]]
                    udep = [gu_toks[8 + hf]]
                    goff = lambda dc, hf=hf: hf * 1024 + dc * 128
                    uoff = lambda dc, hf=hf: (8 + hf) * 1024 + dc * 128
                else:
                    gdep = [tgu]
                    udep = [tgu]
                    goff = lambda dc, hf=hf: dc * 2048 + hf * 128
                    uoff = lambda dc, hf=hf: dc * 2048 + 1024 + hf * 128
                for dc in range(DC):
                    dep = gdep + (list(prev_g_use) if dc == 0 else []) + fm_dep
                    gmm = pb.emit("tensor", lambda e, gub=gub, tt=tt, hf=hf, dc=dc, goff=goff: e.matmul(
                        psg[:, 0:512],
                        gub[:, goff(dc): goff(dc) + 128],
                        xn[:, dc * TT_ + tt * 512: dc * TT_ + tt * 512 + 512],
                        start=(dc == 0), stop=(dc == DC - 1)), deps=dep)
                for dc in range(DC):
                    dep = udep + (list(prev_u_use) if dc == 0 else []) + fm_dep
                    umm = pb.emit("tensor", lambda e, gub=gub, tt=tt, hf=hf, dc=dc, uoff=uoff: e.matmul(
                        psu[:, 0:512],
                        gub[:, uoff(dc): uoff(dc) + 128],
                        xn[:, dc * TT_ + tt * 512: dc * TT_ + tt * 512 + 512],
                        start=(dc == 0), stop=(dc == DC - 1)), deps=dep)
                tsg = pb.emit("scalar", lambda e: e.activation(
                    bufs["sg"][:, 0:512], psg[:, 0:512], AF.Silu),
                    deps=[gmm] + list(prev_sg_use))
                th = pb.emit("vector", lambda e, hb=hb, hf=hf: e.tensor_tensor(
                    hb[:, hf * 512:(hf + 1) * 512], bufs["sg"][:, 0:512],
                    psu[:, 0:512], ALU.mult),
                    deps=[tsg, umm] + list(h_free[hh]), strict=True)
                prev_g_use = (tsg,)
                prev_u_use = (th,)
                prev_sg_use = (th,)
                h_ready.append(th)
            d_toks = []
            for tc2 in range(4):
                tcg = tt * 4 + tc2
                for dc2 in range(2):
                    pd = psd[:, dc2 * 512:(dc2 + 1) * 512]
                    dmm = None
                    for hf in range(8):
                        dep = [h_ready[hf], dn_toks[hf] if hf_major else tdn]
                        if hf == 0:
                            dep = dep + list(prev_d_use[dc2])
                        dmm = pb.emit("tensor", lambda e, hb=hb, dnb=dnb, tc2=tc2, dc2=dc2, hf=hf, pd=pd: e.matmul(
                            pd,
                            hb[:, hf * 512 + tc2 * 128: hf * 512 + tc2 * 128 + 128],
                            dnb[:, hf * 1024 + dc2 * 512: hf * 1024 + dc2 * 512 + 512],
                            start=(hf == 0), stop=(hf == 7)), deps=dep)
                    xs = x[:, tcg * D + dc2 * 512: tcg * D + dc2 * 512 + 512]
                    xdep = [dmm, x_tok.get((tcg, dc2), ("vector", 0))]
                    if is_shared:
                        tacc = pb.emit("vector", lambda e, xs=xs, pd=pd: e.tensor_tensor(
                            xs, pd, xs, ALU.add), deps=xdep, strict=True)
                    else:
                        rcol = bufs["routed_tm"][:, tcg * 8 + e_i: tcg * 8 + e_i + 1]
                        tacc = pb.emit("vector", lambda e, xs=xs, pd=pd, rcol=rcol: e.scalar_tensor_tensor(
                            xs, pd, rcol, xs, ALU.mult, ALU.add),
                            deps=xdep + [route_done], strict=True)
                    x_tok[(tcg, dc2)] = tacc
                    prev_d_use[dc2] = (tacc,)
                    d_toks.append(tacc)
                    last_use.append(tacc)
            h_free[hh] = (d_toks[-1],)
        w_free[half] = (last_use[-1],)
    return [x_tok[k] for k in x_tok]


def emit_ln_fm(pb, x, xn_fm, stats, sq_scr, tmp_bf, ident_bf, pt, n_tc=NTC,
               tok_x=(), affine_wb=None, n_dc=DC, dst_off=0, dst_stride=None,
               src_dram=None, dram_scr=None, dma_eng="gpsimd"):
    n = n_tc
    t_red = []
    prev_sq = None
    scr_free = {0: (), 1: ()}
    def tile_src(tc):
        half = tc % 2
        sl = dram_scr[:, half * D:(half + 1) * D]
        dep = list(tok_x) + list(scr_free[half])
        t = pb.dma(dma_eng, sl, src_dram[tc], deps=dep)
        return sl, [t], half
    for tc in range(n):
        if src_dram is None:
            xs = x[:, tc * D:(tc + 1) * D]
            dep0 = list(tok_x)
            half = None
        else:
            xs, dep0, half = tile_src(tc)
        dep1 = dep0 + ([prev_sq] if prev_sq is not None else [])
        t1 = pb.emit("scalar", lambda e, xs=xs, tc=tc: e.activation(
            sq_scr[:, 0:D], xs, AF.Square, accum_out=stats[:, n + tc:n + tc + 1]),
            deps=dep1, strict=True)
        prev_sq = t1
        t2 = pb.emit("vector", lambda e, xs=xs, tc=tc: e.reduce_sum(
            stats[:, tc:tc + 1], xs, axis=AX.X), deps=dep0)
        t_red += [t1, t2]
        if src_dram is not None:
            scr_free[half] = (t1, t2)
    tmu = pb.emit("vector", lambda e: e.tensor_scalar(
        stats[:, 2 * n:3 * n], stats[:, 0:n], 1.0 / D, None, ALU.mult, ALU.bypass),
        deps=t_red, strict=True)
    tm2 = pb.emit("vector", lambda e: e.tensor_tensor(
        stats[:, 0:n], stats[:, 2 * n:3 * n], stats[:, 2 * n:3 * n], ALU.mult),
        deps=[tmu], strict=True)
    tvar = pb.emit("vector", lambda e: e.scalar_tensor_tensor(
        stats[:, n:2 * n], stats[:, n:2 * n], 1.0 / D, stats[:, 0:n],
        ALU.mult, ALU.subtract), deps=[tm2], strict=True)
    tve = pb.emit("vector", lambda e: e.tensor_scalar(
        stats[:, n:2 * n], stats[:, n:2 * n], EPS_LN, None, ALU.add, ALU.bypass),
        deps=[tvar], strict=True)
    tln = pb.emit("scalar", lambda e: e.activation(
        stats[:, n:2 * n], stats[:, n:2 * n], AF.Ln, bias=0.0, scale=1.0),
        deps=[tve], strict=True)
    trstd = pb.emit("scalar", lambda e: e.activation(
        stats[:, 3 * n:4 * n], stats[:, n:2 * n], AF.Exp, bias=0.0, scale=-0.5),
        deps=[tln], strict=True)
    TT_ = dst_stride if dst_stride is not None else n * 128
    nh = tmp_bf.shape[1] // D if hasattr(tmp_bf, "shape") else 1
    nh = max(1, nh)
    prev_tr = {i: () for i in range(nh)}
    prev_pt = {0: (), 1: ()}
    prev_aff = ()
    last = None
    k = 0
    for tc in range(n):
        half = tc % nh
        if src_dram is None:
            xs = x[:, tc * D:(tc + 1) * D]
            ldep = []
            shalf = None
        else:
            xs, ldep, shalf = tile_src(tc)
        os = tmp_bf[:, half * D:(half + 1) * D]
        if affine_wb is None:
            tt = pb.emit("vector", lambda e, xs=xs, os=os, tc=tc: e.tensor_scalar(
                os, xs, stats[:, 2 * n + tc:2 * n + tc + 1],
                stats[:, 3 * n + tc:3 * n + tc + 1], ALU.subtract, ALU.mult),
                deps=[trstd] + list(prev_tr[half]) + ldep, strict=True)
            if shalf is not None:
                scr_free[shalf] = (tt,)
        else:
            g_bc, b_bc, scr = affine_wb
            t0 = pb.emit("vector", lambda e, xs=xs, tc=tc, scr=scr: e.tensor_scalar(
                scr[:, 0:D], xs, stats[:, 2 * n + tc:2 * n + tc + 1],
                stats[:, 3 * n + tc:3 * n + tc + 1], ALU.subtract, ALU.mult),
                deps=[trstd] + ldep + list(prev_aff), strict=True)
            t0b = pb.emit("vector", lambda e, scr=scr: e.tensor_tensor(
                scr[:, 0:D], scr[:, 0:D], g_bc[:, 0:D], ALU.mult),
                deps=[t0], strict=True)
            tt = pb.emit("vector", lambda e, os=os, scr=scr: e.tensor_tensor(
                os, scr[:, 0:D], b_bc[:, 0:D], ALU.add),
                deps=[t0b] + list(prev_tr[half]), strict=True)
            prev_aff = (tt,)
            if shalf is not None:
                scr_free[shalf] = (tt,)
        trs = []
        for dc in range(n_dc):
            ph = k % 2
            k += 1
            po = pt[ph][:, 0:128]
            ts = pb.emit("tensor", lambda e, os=os, dc=dc, po=po: e.transpose(
                po, os[:, dc * 128:dc * 128 + 128], ident_bf[:, 0:128]),
                deps=[tt] + list(prev_pt[ph]))
            tv = pb.emit("scalar", lambda e, tc=tc, dc=dc, po=po: e.copy(
                xn_fm[:, dc * TT_ + dst_off + tc * 128: dc * TT_ + dst_off + tc * 128 + 128], po),
                deps=[ts])
            prev_pt[ph] = (tv,)
            trs.append(tv)
            last = tv
        prev_tr[half] = (trs[-1],)
    return last



def chunk_w(w):
    w = np.ascontiguousarray(np.asarray(w, np.float32))
    return w.reshape(w.shape[0] // 128, 128, w.shape[1])

IDENT_BF = np.eye(128, dtype=BF)


def build_moe_l0_test(n_tc=NTC):
    nc = new_nc()
    es = ExitStack()
    T = n_tc * 128
    xin = nc.dram_tensor("xin", [n_tc, 128, D], F32, kind="ExternalInput").ap()
    gwT = nc.dram_tensor("gwT", [128, DC * 8], F32, kind="ExternalInput").ap()
    gu = nc.dram_tensor("gu", [8, DC, 128, 2048], F32, kind="ExternalInput").ap()
    down = nc.dram_tensor("down", [8, DC, 128, 1024], F32, kind="ExternalInput").ap()
    ident = nc.dram_tensor("ident", [128, 128], BF16, kind="ExternalInput").ap()
    xout = nc.dram_tensor("xout", [n_tc, 128, D], F32, kind="ExternalOutput").ap()
    with es:
        pb = PB(nc, es)
        bufs = dict(
            x=sb(pb, "x", [128, n_tc * D], F32),
            xn_fm=sb(pb, "xn_fm", [128, DC * T], BF16),
            tmp_bf=sb(pb, "tmp_bf", [128, D], BF16),
            stats=sb(pb, "stats", [128, 4 * n_tc], F32),
            identb=sb(pb, "identb", [128, 128], BF16),
            gwT=sb(pb, "gwTb", [128, DC * 8], BF16),
            stage=sb(pb, "stage", [128, 1024], F32),
            stage2=sb(pb, "stage2", [128, 1024], F32),
            gu_bf0=sb(pb, "gu_bf0", [128, DC * 2048], BF16),
            gu_bf1=sb(pb, "gu_bf1", [128, DC * 2048], BF16),
            dn_bf0=sb(pb, "dn_bf0", [128, DC * 1024], BF16),
            hbuf0=sb(pb, "hbuf0", [128, 4096], BF16),
            hbuf1=sb(pb, "hbuf1", [128, 4096], BF16),
            sg=sb(pb, "sg", [128, 512], F32),
            routed_tm=sb(pb, "routed_tm", [128, n_tc * 8], F32),
            rscr=sb(pb, "rscr", [128, 64], F32),
            ps_r=ps(pb, "ps_r", [128, 8]),
            ps_g=ps(pb, "ps_g", [128, 512]),
            ps_u=ps(pb, "ps_u", [128, 512]),
            ps_d=ps(pb, "ps_d", [128, 1024]),
            pt=(ps(pb, "pt0", [128, 128], BF16), ps(pb, "pt1", [128, 128], BF16)),
        )
        tid = pb.dma("sync", bufs["identb"][:, :], ident)
        x_toks = []
        for g in range(0, n_tc, 4):
            gn = min(4, n_tc - g)
            x_toks.append(pb.dma(
                "sync",
                bufs["x"][:, g * D:(g + gn) * D].rearrange("p (t d) -> p t d", t=gn),
                xin[g:g + gn].rearrange("t p d -> p t d")))
        tgw = pb.dma("sync", bufs["rscr"][:, 0:64], gwT)
        tgwb = pb.emit("vector", lambda e: e.tensor_copy(bufs["gwT"][:, :], bufs["rscr"][:, 0:64]), deps=[tgw])
        fm_last = emit_ln_fm(pb, bufs["x"], bufs["xn_fm"], bufs["stats"],
                             bufs["tmp_bf"][:, 0:D],
                             bufs["tmp_bf"], bufs["identb"], bufs["pt"], n_tc=n_tc,
                             tok_x=x_toks + [tid, tgwb])
        class W:
            pass
        wap = {
            "gu": [[gu[e_i, c // 2][:, (c % 2) * 1024:(c % 2) * 1024 + 1024] for c in range(DC * 2)] for e_i in range(8)],
            "down": [[down[e_i, c] for c in range(DC)] for e_i in range(8)],
        }
        moe_toks = emit_moe_l0(pb, bufs, wap, n_tc=n_tc, fm_toks=[fm_last], renorm=True)
        st_toks = []
        for g in range(0, n_tc, 4):
            gn = min(4, n_tc - g)
            st_toks.append(pb.dma(
                "gpsimd", xout[g:g + gn].rearrange("t p d -> p t d"),
                bufs["x"][:, g * D:(g + gn) * D].rearrange("p (t d) -> p t d", t=gn),
                deps=moe_toks))
        pb.emit("sync", lambda e: e.nop(), deps=st_toks)
        pb.build()
    return nc



def emit_routing_l1(pb, psum_tok, psum_r, routed_tm, rscr, tc, bias_bc, bias_tok=("sync", 0)):
    lg = rscr[:, 0:8]
    ex = rscr[:, 8:16]
    rw = rscr[:, 16:24]
    sp = rscr[:, 24:32]
    m = rscr[:, 32:33]
    nm = rscr[:, 33:34]
    s = rscr[:, 34:35]
    rs = rscr[:, 35:36]
    m1 = rscr[:, 36:37]
    m2 = rscr[:, 37:38]
    mask1 = rscr[:, 44:52]
    mask2 = rscr[:, 52:60]
    sp2 = rscr[:, 24:32]
    tl = pb.emit("vector", lambda e: e.tensor_copy(lg, psum_r[:, 0:8]), deps=[psum_tok])
    t1 = pb.emit("vector", lambda e: e.reduce_max(m, lg, axis=AX.X), deps=[tl], strict=True)
    t2 = pb.emit("vector", lambda e: e.tensor_scalar(nm, m, -1.0, None, ALU.mult, ALU.bypass), deps=[t1], strict=True)
    t3 = pb.emit("scalar", lambda e: e.activation(ex, lg, AF.Exp, bias=nm[:, 0:1], scale=1.0), deps=[t2], strict=True)
    t4 = pb.emit("vector", lambda e: e.reduce_sum(s, ex, axis=AX.X), deps=[t3], strict=True)
    t5 = pb.emit("vector", lambda e: e.reciprocal(rs, s), deps=[t4], strict=True)
    t6 = pb.emit("vector", lambda e: e.tensor_scalar(rw, ex, rs[:, 0:1], None, ALU.mult, ALU.bypass), deps=[t5], strict=True)
    t7 = pb.emit("vector", lambda e: e.tensor_tensor(sp, rw, bias_bc[:, 0:8], ALU.add), deps=[t6, bias_tok], strict=True)
    t8 = pb.emit("vector", lambda e: e.reduce_max(m1, sp, axis=AX.X), deps=[t7], strict=True)
    t9 = pb.emit("vector", lambda e: e.tensor_scalar(mask1, sp, m1[:, 0:1], None, ALU.is_ge, ALU.bypass), deps=[t8], strict=True)
    t10 = pb.emit("vector", lambda e: e.scalar_tensor_tensor(sp2, mask1, -1e30, sp, ALU.mult, ALU.add), deps=[t9], strict=True)
    t11 = pb.emit("vector", lambda e: e.reduce_max(m2, sp2, axis=AX.X), deps=[t10], strict=True)
    t12 = pb.emit("vector", lambda e: e.tensor_scalar(mask2, sp2, m2[:, 0:1], None, ALU.is_ge, ALU.bypass), deps=[t11], strict=True)
    t13 = pb.emit("vector", lambda e: e.tensor_tensor(mask1, mask1, mask2, ALU.add), deps=[t12], strict=True)
    td = pb.emit("vector", lambda e: e.tensor_tensor(
        routed_tm[:, tc * 8:(tc + 1) * 8], rw, mask1, ALU.mult), deps=[t13], strict=True)
    return td



def emit_weight_load_mq(pb, chunks, stages, wbf, cols, deps=(), queues=("sync", "gpsimd"),
                        cast_engs=("scalar", "vector"), order=None):
    nq_ = len(queues)
    skey = [("stage", s.name if hasattr(s, "name") else id(s)) for s in stages]
    prev_cast = [pb.tail.get(k) for k in skey]
    toks = [None] * len(chunks)
    for oi, c in enumerate(order if order is not None else range(len(chunks))):
        ap = chunks[c]
        qi = oi % nq_
        sl = stages[qi][:, 0:cols]
        dep = list(deps)
        if prev_cast[qi] is not None:
            dep.append(prev_cast[qi])
        td = pb.dma(queues[qi], sl, ap, deps=dep)
        ce = cast_engs[oi % len(cast_engs)]
        if ce == "scalar":
            prev_cast[qi] = pb.emit(ce, lambda e, sl=sl, c=c: e.copy(
                wbf[:, c * cols:(c + 1) * cols], sl), deps=[td], strict=True)
        else:
            prev_cast[qi] = pb.emit(ce, lambda e, sl=sl, c=c: e.tensor_copy(
                wbf[:, c * cols:(c + 1) * cols], sl), deps=[td], strict=True)
        toks[c] = prev_cast[qi]
    for qi in range(nq_):
        pb.tail[skey[qi]] = prev_cast[qi]
    return toks


def emit_weight_load_off(pb, chunks, stage, wbf, cols, col_off, deps=(), eng_dma="sync",
                         cast_eng="scalar"):
    key = ("stage", stage.name if hasattr(stage, "name") else id(stage))
    prev_cast = pb.tail.get(key)
    for c, ap in enumerate(chunks):
        sl = stage[:, 0:cols]
        dep = list(deps)
        if prev_cast is not None:
            dep.append(prev_cast)
        td = pb.dma(eng_dma, sl, ap, deps=dep)
        prev_cast = pb.emit(cast_eng, lambda e, sl=sl, c=c: e.copy(
            wbf[:, col_off + c * cols: col_off + (c + 1) * cols], sl), deps=[td])
    pb.tail[key] = prev_cast
    return prev_cast


def emit_proj_residual(pb, bufs, src_fm, w_bf, n_tc, out_x, act=None, bias_row=None,
                       src_tok=(), w_tok=(), x_tok=None, scale_cols=None,
                       x_dram=None, xo_dram=None):
    T = n_tc * 128
    pd = bufs["ps_q"]
    scr = bufs.get("projscr")
    prev_use = pb.tail.get(("ps_proj",))
    scr_use = {}
    toks = []
    for tc in range(n_tc):
        if x_dram is not None:
            txl = pb.dma("gpsimd", out_x[:, 0:D], x_dram[tc],
                         deps=list(pb.tail.get(("prj_x",), ())))
        for dc2 in range(2):
            mm = None
            for hf in range(DC):
                dep = list(src_tok) + list(w_tok) + ([prev_use] if (hf == 0 and prev_use is not None) else [])
                if hf == 0 and prev_use is not None:
                    dep = dep + [prev_use]
                mm = pb.emit("tensor", lambda e, tc=tc, dc2=dc2, hf=hf: e.matmul(
                    pd[:, 0:512],
                    src_fm[:, hf * T + tc * 128: hf * T + tc * 128 + 128],
                    w_bf[:, hf * 1024 + dc2 * 512: hf * 1024 + dc2 * 512 + 512],
                    start=(hf == 0), stop=(bias_row is None and hf == DC - 1)),
                    deps=dep)
            if bias_row is not None:
                mm = pb.emit("tensor", lambda e, dc2=dc2: e.matmul(
                    pd[:, 0:512], bufs["onescol"][0:1, 0:128],
                    bias_row[0:1, dc2 * 512:dc2 * 512 + 512],
                    start=False, stop=True), deps=[])
            if x_dram is not None:
                xs = out_x[:, dc2 * 512: dc2 * 512 + 512]
            else:
                xs = out_x[:, tc * D + dc2 * 512: tc * D + dc2 * 512 + 512]
            xdep = [mm]
            if x_dram is not None:
                xdep.append(txl)
            if x_tok is not None:
                xdep.append(x_tok.get((tc, dc2), ("vector", 0)))
            if act is not None:
                ta = pb.emit("scalar", lambda e, dc2=dc2: e.activation(
                    scr[:, dc2 * 512:dc2 * 512 + 512], pd[:, 0:512], act),
                    deps=[mm] + scr_use.get(dc2, []))
                tacc = pb.emit("vector", lambda e, xs=xs, dc2=dc2: e.tensor_tensor(
                    xs, scr[:, dc2 * 512:dc2 * 512 + 512], xs, ALU.add),
                    deps=[ta] + xdep[1:], strict=True)
                scr_use[dc2] = [tacc]
                prev_use = ta
            else:
                tacc = pb.emit("vector", lambda e, xs=xs: e.tensor_tensor(
                    xs, pd[:, 0:512], xs, ALU.add), deps=xdep, strict=True)
                prev_use = tacc
            if x_tok is not None:
                x_tok[(tc, dc2)] = tacc
            toks.append(tacc)
        if x_dram is not None:
            tst = pb.dma("gpsimd", xo_dram[tc], out_x[:, 0:D], deps=toks[-2:])
            pb.tail[("prj_x",)] = (tst,)
            toks.append(tst)
    pb.tail[("ps_proj",)] = prev_use
    return toks


def emit_attn(pb, bufs, cfg):
    nq = cfg["nq_tc"]
    prefix = cfg.get("prefix", True)
    nkv = 2 * nq if prefix else nq
    T = nq * 128; KVT = nkv * 128
    QOFF = KVT // 2 if prefix else 0
    H = cfg["heads"]; dh = cfg["dh"]
    xkv = bufs["xkv_fm"]
    tanh = cfg.get("tanh", False)
    alibi = cfg.get("alibi", False)
    lora = cfg.get("lora", False)
    AFp = AF.Tanh if tanh else AF.Copy
    ps_q = bufs["ps_q"]
    ps_s = [bufs["ps_s0"], bufs["ps_s1"]] + ([bufs["ps_s2"]] if "ps_s2" in bufs else [])
    ps_o = bufs["ps_o"]
    pbuf = [bufs["pb0"], bufs["pb1"]] + ([bufs["pb2"]] if "pb2" in bufs else [])
    NSL = len(ps_s)
    PV_DELAY = NSL - 1
    ctx = bufs["ctx_tm"]
    HDall = H * dh
    xkv_tok = bufs["xkv_tok"]
    const_tok = bufs["const_tok"]

    psq_use = None
    t1_tok = ()
    if lora:
        t1 = {n: bufs["t1_" + n] for n in ("q", "k", "v")}
        last = None
        for name in ("q", "k", "v"):
            bT = bufs["bT_" + name]
            for tt in range(KVT // 512):
                mm = None
                for dc in range(DC):
                    dep = ([xkv_tok, const_tok] + list(bufs.get("lw_tok", ())) +
                           ([psq_use] if (psq_use and dc == 0) else []))
                    mm = pb.emit("tensor", lambda e, name=name, tt=tt, dc=dc, bT=bT: e.matmul(
                        ps_q[0:8, 0:512], bT[:, dc * 8:(dc + 1) * 8],
                        xkv[:, dc * KVT + tt * 512: dc * KVT + tt * 512 + 512],
                        start=(dc == 0), stop=(dc == DC - 1)), deps=dep)
                last = pb.emit("scalar", lambda e, name=name, tt=tt: e.activation(
                    t1[name][0:8, tt * 512:tt * 512 + 512], ps_q[0:8, 0:512],
                    AF.Copy, bias=0.0, scale=2.0), deps=[mm])
                psq_use = last
        t1_tok = (last,)

    qh = [bufs["qh0"], bufs["qh1"]]
    kh = [bufs["kh0"], bufs["kh1"]]
    vh = [bufs["vh0"], bufs["vh1"]]
    wstage = bufs["wstage"]
    whb = [bufs["wh0"], bufs["wh1"]]
    biasc = [bufs["bias0"], bufs["bias1"]]
    e0 = tuple(bufs.get("entry_tok", ()))
    head_free = [e0, e0]
    pexp = [None] * NSL
    ppv = [()] * NSL
    pso_use = ()
    x_toks = []
    single = cfg.get("single_hbuf", False)
    for h in range(H):
        hf = 0 if single else h % 2
        Q, K, V, W = qh[hf], kh[hf], vh[hf], whb[hf]
        wt = {}
        for wi, name in enumerate(("wq", "wk", "wv")):
            wt[name] = emit_weight_load_off(
                pb, cfg[name][h], wstage, W, 64, wi * 512,
                deps=head_free[hf], eng_dma="sync")
        if alibi:
            tb = pb.emit("vector", lambda e, h=h, hf=hf: e.scalar_tensor_tensor(
                biasc[hf][:, 0:nkv], bufs["kpos"][:, 0:nkv], -cfg["slopes"][h],
                bufs["kmask"][:, 0:nkv], ALU.mult, ALU.add),
                deps=[const_tok] + list(head_free[hf]))
        else:
            tb = pb.emit("vector", lambda e, hf=hf: e.tensor_copy(
                biasc[hf][:, 0:nkv], bufs["kmask"][:, 0:nkv]),
                deps=[const_tok] + list(head_free[hf]))
        q_toks = []
        k_toks = []
        v_toks = []
        for tt in range(T // 512):
            mm = None
            for dc in range(DC):
                dep = ([wt["wq"], xkv_tok] + list(head_free[hf]) +
                       ([psq_use] if (psq_use and dc == 0) else []))
                mm = pb.emit("tensor", lambda e, W=W, tt=tt, dc=dc: e.matmul(
                    ps_q[0:64, 0:512], W[:, dc * 64: dc * 64 + 64],
                    xkv[:, dc * KVT + QOFF + tt * 512: dc * KVT + QOFF + tt * 512 + 512],
                    start=(dc == 0), stop=(not lora and dc == DC - 1)), deps=dep)
            if lora:
                mm = pb.emit("tensor", lambda e, h=h, tt=tt: e.matmul(
                    ps_q[0:64, 0:512], bufs["aT_q"][0:8, h * 64:(h + 1) * 64],
                    bufs["t1_q"][0:8, QOFF + tt * 512: QOFF + tt * 512 + 512],
                    start=False, stop=True), deps=list(t1_tok) + list(bufs.get("lw_tok", ())))
            tq = pb.emit("scalar", lambda e, Q=Q, tt=tt, h=h: e.activation(
                Q[0:64, tt * 512:tt * 512 + 512], ps_q[0:64, 0:512], AFp,
                bias=(bufs["bqk"][0:64, 2 * h:2 * h + 1] if tanh else 0.0),
                scale=1.0), deps=[mm, bufs.get("bqk_tok", ("scalar", 0))], strict=True)
            psq_use = tq
            q_toks.append(tq)
        for tt in range(KVT // 512):
            mm = None
            for dc in range(DC):
                dep = [wt["wk"], xkv_tok] + list(head_free[hf]) + ([psq_use] if dc == 0 else [])
                mm = pb.emit("tensor", lambda e, W=W, tt=tt, dc=dc: e.matmul(
                    ps_q[0:64, 0:512], W[:, 512 + dc * 64: 512 + dc * 64 + 64],
                    xkv[:, dc * KVT + tt * 512: dc * KVT + tt * 512 + 512],
                    start=(dc == 0), stop=(not lora and dc == DC - 1)), deps=dep)
            if lora:
                mm = pb.emit("tensor", lambda e, h=h, tt=tt: e.matmul(
                    ps_q[0:64, 0:512], bufs["aT_k"][0:8, h * 64:(h + 1) * 64],
                    bufs["t1_k"][0:8, tt * 512: tt * 512 + 512],
                    start=False, stop=True), deps=list(t1_tok))
            tk = pb.emit("scalar", lambda e, K=K, tt=tt, h=h: e.activation(
                K[0:64, tt * 512:tt * 512 + 512], ps_q[0:64, 0:512], AFp,
                bias=(bufs["bqk"][0:64, 2 * h + 1:2 * h + 2] if tanh else 0.0),
                scale=1.0), deps=[mm, bufs.get("bqk_tok", ("scalar", 0))], strict=True)
            psq_use = tk
            k_toks.append(tk)
        tmset = pb.emit("vector", lambda e, V=V: e.memset(V[:, :], 1.0),
                        deps=list(head_free[hf]))
        for kc in range(nkv):
            mm = None
            for dc in range(DC):
                dep = [wt["wv"], xkv_tok, tmset] + ([psq_use] if dc == 0 else [])
                mm = pb.emit("tensor", lambda e, W=W, kc=kc, dc=dc: e.matmul(
                    ps_q[0:128, 0:64],
                    xkv[:, dc * KVT + kc * 128: dc * KVT + kc * 128 + 128],
                    W[:, 1024 + dc * 64: 1024 + dc * 64 + 64],
                    start=(dc == 0), stop=(not lora and not tanh and dc == DC - 1)), deps=dep)
            if lora:
                mm = pb.emit("tensor", lambda e, h=h, kc=kc: e.matmul(
                    ps_q[0:128, 0:64], bufs["t1_v"][0:8, kc * 128: kc * 128 + 128],
                    bufs["aT_v"][0:8, h * 64:(h + 1) * 64],
                    start=False, stop=(not tanh)), deps=list(t1_tok))
            if tanh:
                mm = pb.emit("tensor", lambda e, h=h: e.matmul(
                    ps_q[0:128, 0:64], bufs["onescol"][0:1, 0:128],
                    bufs["bv_rows"][0:1, h * 64:(h + 1) * 64],
                    start=False, stop=True), deps=[const_tok])
            elif not lora:
                pass
            tv = pb.emit("scalar", lambda e, V=V, kc=kc: e.activation(
                V[:, kc * 65: kc * 65 + 64], ps_q[0:128, 0:64], AFp), deps=[mm, tmset],
                strict=True)
            psq_use = tv
            v_toks.append(tv)
        qkv_tok = q_toks[-1]
        kv_done = [k_toks[-1], v_toks[-1], qkv_tok, tb]
        last_s = ()
        last_pv = ()
        qt0, qt1 = cfg.get("qt_range", (0, T // 512))
        ctx_base = cfg.get("ctx_base", 0)
        for qt in range(qt0, qt1):
            nkc = (nq if prefix else 0) + 4 * qt + 4
            pend = []
            pend_pv = []

            def emit_pv(kc, te, nkc=nkc):
                sl = kc % NSL
                for qs in range(4):
                    dep = [te]
                    if kc == 0:
                        dep += list(pso_use)
                    tpv = pb.emit("tensor", lambda e, V=V, sl=sl, qs=qs, kc=kc, nkc=nkc: e.matmul(
                        ps_o[qs][:, 0:65],
                        pbuf[sl][:, qs * 128:(qs + 1) * 128],
                        V[:, kc * 65: kc * 65 + 65],
                        start=(kc == 0), stop=(kc == nkc - 1)), deps=dep)
                    pend.append(tpv)
                ppv[sl] = (pend[-1],)

            for kc in range(nkc):
                sl = kc % NSL
                dep = [pexp[sl]] if pexp[sl] else []
                ts = pb.emit("tensor", lambda e, K=K, Q=Q, kc=kc, qt=qt, sl=sl: e.matmul(
                    ps_s[sl][:, 0:512], K[0:64, kc * 128: kc * 128 + 128],
                    Q[0:64, qt * 512: qt * 512 + 512],
                    start=True, stop=True), deps=dep + kv_done)
                band = kc - ((nq if prefix else 0) + 4 * qt)
                tm_ = ts
                if band >= 0:
                    tm_ = pb.emit("vector", lambda e, sl=sl, band=band: e.tensor_tensor(
                        ps_s[sl][:, 0:512], ps_s[sl][:, 0:512],
                        bufs["cmask"][:, band * 512:(band + 1) * 512], ALU.add),
                        deps=[ts], strict=True)
                te = pb.emit("scalar", lambda e, sl=sl, kc=kc, hf=hf: e.activation(
                    pbuf[sl][:, 0:512], ps_s[sl][:, 0:512], AF.Exp,
                    bias=biasc[hf][:, kc:kc + 1], scale=cfg.get("sscale", 0.125)),
                    deps=[tm_, tb] + list(ppv[sl]), strict=True)
                pexp[sl] = te
                pend_pv.append((kc, te))
                if len(pend_pv) > PV_DELAY:
                    kc_, te_ = pend_pv.pop(0)
                    emit_pv(kc_, te_)
            for kc_, te_ in pend_pv:
                emit_pv(kc_, te_)
            ev = []
            for qs in range(4):
                tcg = qt * 4 + qs
                tr = pb.emit("vector", lambda e, qs=qs: e.reciprocal(
                    bufs["rcol"][:, qs:qs + 1], ps_o[qs][:, 64:65]),
                    deps=[pend[-1]], strict=True)
                tn = pb.emit("vector", lambda e, qs=qs, tcg=tcg, h=h: e.tensor_scalar(
                    ctx[:, (tcg - ctx_base) * HDall + h * dh: (tcg - ctx_base) * HDall + (h + 1) * dh],
                    ps_o[qs][:, 0:64],
                    bufs["rcol"][:, qs:qs + 1], None, ALU.mult, ALU.bypass),
                    deps=[tr], strict=True)
                ev.append(tn)
            pso_use = tuple(ev)
            last_pv = pend[-1]
            x_toks.extend(ev)
        head_free[hf] = (last_pv,) + tuple(pso_use) + tuple(t for t in pexp if t)
    return x_toks


def build_attn_test(nq_tc=NTC, style="tanh"):
    nc = new_nc()
    es = ExitStack()
    T = nq_tc * 128
    KVT = 2 * T
    nkv = 2 * nq_tc
    H, dh = 16, 64
    tanh = (style == "tanh")
    lora = (style == "alibi")
    xin = nc.dram_tensor("xin", [nq_tc, 128, D], F32, kind="ExternalInput").ap()
    xprev = nc.dram_tensor("xprev", [nq_tc, 128, D], F32, kind="ExternalInput").ap()
    wq_d = nc.dram_tensor("wq", [H, DC, 128, dh], F32, kind="ExternalInput").ap()
    wk_d = nc.dram_tensor("wk", [H, DC, 128, dh], F32, kind="ExternalInput").ap()
    wv_d = nc.dram_tensor("wv", [H, DC, 128, dh], F32, kind="ExternalInput").ap()
    wo_d = nc.dram_tensor("wo", [DC, 128, D], F32, kind="ExternalInput").ap()
    ident = nc.dram_tensor("ident", [128, 128], BF16, kind="ExternalInput").ap()
    cmask_d = nc.dram_tensor("cmask", [128, 4 * 512], F32, kind="ExternalInput").ap()
    kmask_d = nc.dram_tensor("kmask", [128, nkv], F32, kind="ExternalInput").ap()
    kpos_d = nc.dram_tensor("kpos", [128, nkv], F32, kind="ExternalInput").ap()
    ones_d = nc.dram_tensor("onesc", [1, 128], BF16, kind="ExternalInput").ap()
    if tanh:
        bqk_d = nc.dram_tensor("bqk", [64, 2 * H], F32, kind="ExternalInput").ap()
        bvo_d = nc.dram_tensor("bvo", [1, 2 * D], BF16, kind="ExternalInput").ap()
    if lora:
        bT_d = {n2: nc.dram_tensor("bT" + n2, [DC, 128, 8], F32, kind="ExternalInput").ap()
                for n2 in ("q", "k", "v")}
        aT_d = {n2: nc.dram_tensor("aT" + n2, [8, D], F32, kind="ExternalInput").ap()
                for n2 in ("q", "k", "v")}
    xout = nc.dram_tensor("xout", [nq_tc, 128, D], F32, kind="ExternalOutput").ap()
    slopes = [2.0 ** (-0.5 - 0.5 * i) for i in range(H)] if lora else None

    with es:
        pb = PB(nc, es)
        stats = sb(pb, "stats", [128, 4 * nq_tc], F32)
        tmp_bf = sb(pb, "tmp_bf", [128, D], BF16)
        identb = sb(pb, "identb", [128, 128], BF16)
        onescol = sb(pb, "onescol", [1, 128], BF16)
        wstage = sb(pb, "wstage", [128, 1024], F32)
        rcol = sb(pb, "rcol", [128, 8], F32)
        kposb = sb(pb, "kposb", [128, nkv], F32)
        kmaskb = sb(pb, "kmaskb", [128, nkv], F32)
        cmaskb = sb(pb, "cmaskb", [128, 4 * 512], F32)
        ps_q = ps(pb, "ps_q", [128, 512])
        esKV = ExitStack()
        c_toks = [pb.dma("sync", identb[:, :], ident),
                  pb.dma("sync", onescol[0:1, :], ones_d),
                  pb.dma("sync", kposb[:, :], kpos_d),
                  pb.dma("sync", kmaskb[:, :], kmask_d),
                  pb.dma("sync", cmaskb[:, :], cmask_d)]
        bufs = dict(ps_q=ps_q,
                    onescol=onescol, wstage=wstage, rcol=rcol,
                    kpos=kposb, kmask=kmaskb, cmask=cmaskb)
        if tanh:
            bqk = sb(pb, "bqkb", [64, 2 * H], F32)
            bvo = sb(pb, "bvob", [1, 2 * D], BF16)
            c_toks += [pb.dma("sync", bqk[0:64, :], bqk_d),
                       pb.dma("sync", bvo[0:1, :], bvo_d)]
            bufs["bqk"] = bqk
            bufs["bqk_tok"] = c_toks[-2]
            bufs["bv_rows"] = bvo[0:1, 0:D]
        saved0 = pb.es
        pb.es = esKV
        xkv_fm = sb(pb, "xkv_fm", [128, DC * KVT], BF16)
        pb.es = saved0
        bufs["xkv_fm"] = xkv_fm
        esA = ExitStack()
        with esA:
            pbA_saved = pb.es
            pb.es = esA
            x = sb(pb, "x", [128, nq_tc * D], F32)
            dram_scr = sb(pb, "dram_scr", [128, 2 * D], F32)
            pt = (ps(pb, "ptA0", [128, 128], BF16), ps(pb, "ptA1", [128, 128], BF16))
            pb.es = pbA_saved
            x_toks = []
            for g in range(0, nq_tc, 4):
                gn = min(4, nq_tc - g)
                x_toks.append(pb.dma(
                    "gpsimd", x[:, g * D:(g + gn) * D].rearrange("p (t d) -> p t d", t=gn),
                    xin[g:g + gn].rearrange("t p d -> p t d")))
            ln1 = emit_ln_fm(pb, x, xkv_fm, stats, tmp_bf[:, 0:D], tmp_bf, identb, pt,
                             n_tc=nq_tc, tok_x=x_toks + [c_toks[0]],
                             dst_off=KVT // 2, dst_stride=KVT)
            ln2 = emit_ln_fm(pb, None, xkv_fm, stats, tmp_bf[:, 0:D], tmp_bf, identb, pt,
                             n_tc=nq_tc, tok_x=[ln1],
                             dst_off=0, dst_stride=KVT,
                             src_dram=[xprev[t] for t in range(nq_tc)],
                             dram_scr=dram_scr, dma_eng="gpsimd")
        entryB = pb.all_toks()
        esB = ExitStack()
        ctx_tm = sb(pb, "ctx_tm", [128, nq_tc * H * dh], BF16, side="right")
        with esB:
            saved = pb.es
            pb.es = esB
            ps_s0 = ps(pb, "ps_s0", [128, 512])
            ps_s1 = ps(pb, "ps_s1", [128, 512])
            ps_o = [ps(pb, "ps_o%d" % i, [128, 65]) for i in range(4)]
            bufs["ps_s0"] = ps_s0; bufs["ps_s1"] = ps_s1; bufs["ps_o"] = ps_o
            b2 = dict(
                qh0=sb(pb, "qh0", [64, T], BF16), qh1=sb(pb, "qh1", [64, T], BF16),
                kh0=sb(pb, "kh0", [64, KVT], BF16), kh1=sb(pb, "kh1", [64, KVT], BF16),
                vh0=sb(pb, "vh0", [128, nkv * 65], BF16), vh1=sb(pb, "vh1", [128, nkv * 65], BF16),
                wh0=sb(pb, "wh0", [128, 3 * 512], BF16), wh1=sb(pb, "wh1", [128, 3 * 512], BF16),
                pb0=sb(pb, "pb0", [128, 512], BF16), pb1=sb(pb, "pb1", [128, 512], BF16),
                bias0=sb(pb, "bias0", [128, nkv], F32), bias1=sb(pb, "bias1", [128, nkv], F32),
            )
            if lora:
                for n2 in ("q", "k", "v"):
                    b2["t1_" + n2] = sb(pb, "t1b" + n2, [8, KVT], BF16)
                    b2["bT_" + n2] = sb(pb, "bTb" + n2, [128, 64], BF16)
                    b2["aT_" + n2] = sb(pb, "aTb" + n2, [8, D], BF16)
            pb.es = saved
            bufs.update(b2)
            bufs["ctx_tm"] = ctx_tm
            bufs["xkv_tok"] = ln2
            bufs["const_tok"] = c_toks[-1]
            lw_toks = list(entryB)
            if lora:
                for n2 in ("q", "k", "v"):
                    tb1 = emit_weight_load_off(pb, [bT_d[n2][c] for c in range(DC)],
                                               wstage, bufs["bT_" + n2], 8, 0,
                                               deps=entryB, eng_dma="sync")
                    ta1 = pb.dma("sync", wstage[0:8, 0:D], aT_d[n2], deps=[tb1])
                    ta2 = pb.emit("scalar", lambda e, n2=n2: e.copy(
                        bufs["aT_" + n2][0:8, 0:D], wstage[0:8, 0:D]), deps=[ta1])
                    pb.tail[("stage", wstage.name)] = ta2
                    lw_toks.append(ta2)
            cfg = dict(nq_tc=nq_tc, heads=H, dh=dh, tanh=tanh, alibi=lora, lora=lora,
                       slopes=slopes,
                       wq=[[wq_d[h, c] for c in range(DC)] for h in range(H)],
                       wk=[[wk_d[h, c] for c in range(DC)] for h in range(H)],
                       wv=[[wv_d[h, c] for c in range(DC)] for h in range(H)])
            bufs["const_tok"] = c_toks[-1]
            bufs["lw_tok"] = tuple(lw_toks[len(entryB):]) if lora else ()
            attn_toks = emit_attn(pb, bufs, cfg)
        esKV.close()
        entryC = pb.all_toks()
        esC = ExitStack()
        with esC:
            saved = pb.es
            pb.es = esC
            pt = (ps(pb, "ptC0", [128, 128], BF16), ps(pb, "ptC1", [128, 128], BF16))
            ctx_fm = sb(pb, "ctx_fm", [128, DC * T], BF16)
            wo_bf = sb(pb, "wo_bf", [128, DC * 1024], BF16)
            projscr = sb(pb, "projscr", [128, 1024], F32)
            xr = sb(pb, "xr", [128, nq_tc * D], F32)
            pb.es = saved
            bufs["projscr"] = projscr
            xr_toks = []
            for g in range(0, nq_tc, 4):
                gn = min(4, nq_tc - g)
                xr_toks.append(pb.dma(
                    "gpsimd", xr[:, g * D:(g + gn) * D].rearrange("p (t d) -> p t d", t=gn),
                    xin[g:g + gn].rearrange("t p d -> p t d"), deps=entryC))
            wo_toks, wo_last = emit_weight_load(pb, [wo_d[c] for c in range(DC)],
                                                wstage, wo_bf, DC, 1024,
                                                deps=entryC, eng_dma="sync")
            fm_toks = emit_tm_to_fm(pb, ctx_tm, ctx_fm, identb, pt, n_tc=nq_tc,
                                    deps_per_tile=None, src_cols=H * dh,
                                    dst_cols=T, extra_deps=[attn_toks[-1]] + list(entryC))
            x_tok = {}
            for tc in range(nq_tc):
                for dc2 in range(2):
                    x_tok[(tc, dc2)] = xr_toks[tc // 4]
            pr_toks = emit_proj_residual(
                pb, bufs, ctx_fm, wo_bf, nq_tc, xr,
                act=(AF.Tanh if tanh else None),
                bias_row=(bvo[0:1, D:2 * D] if tanh else None),
                src_tok=[fm_toks[-1]], w_tok=[wo_last],
                x_tok=x_tok)
            st_toks = []
            for g in range(0, nq_tc, 4):
                gn = min(4, nq_tc - g)
                st_toks.append(pb.dma(
                    "gpsimd", xout[g:g + gn].rearrange("t p d -> p t d"),
                    xr[:, g * D:(g + gn) * D].rearrange("p (t d) -> p t d", t=gn),
                    deps=pr_toks))
            pb.emit("sync", lambda e: e.nop(), deps=st_toks)
        pb.build()
    return nc


def build_moe_l1_test(n_tc=NTC):
    nc = new_nc()
    es = ExitStack()
    T = n_tc * 128
    xin = nc.dram_tensor("xin", [n_tc, 128, D], F32, kind="ExternalInput").ap()
    gwT = nc.dram_tensor("gwT", [128, DC * 8], F32, kind="ExternalInput").ap()
    bias_d = nc.dram_tensor("ebias", [128, 8], F32, kind="ExternalInput").ap()
    gu = nc.dram_tensor("gu", [8, DC, 128, 2048], F32, kind="ExternalInput").ap()
    down = nc.dram_tensor("down", [8, DC, 128, 1024], F32, kind="ExternalInput").ap()
    sgu = nc.dram_tensor("sgu", [DC, 128, 2048], F32, kind="ExternalInput").ap()
    sdown = nc.dram_tensor("sdown", [DC, 128, 1024], F32, kind="ExternalInput").ap()
    ident = nc.dram_tensor("ident", [128, 128], BF16, kind="ExternalInput").ap()
    xout = nc.dram_tensor("xout", [n_tc, 128, D], F32, kind="ExternalOutput").ap()
    with es:
        pb = PB(nc, es)
        bufs = dict(
            x=sb(pb, "x", [128, n_tc * D], F32),
            xn_fm=sb(pb, "xn_fm", [128, DC * T], BF16),
            tmp_bf=sb(pb, "tmp_bf", [128, D], BF16),
            stats=sb(pb, "stats", [128, 4 * n_tc], F32),
            identb=sb(pb, "identb", [128, 128], BF16),
            gwT=sb(pb, "gwTb", [128, DC * 8], BF16),
            bias_bc=sb(pb, "bias_bc", [128, 8], F32),
            stage=sb(pb, "stage", [128, 1024], F32),
            stage2=sb(pb, "stage2", [128, 1024], F32),
            gu_bf0=sb(pb, "gu_bf0", [128, DC * 2048], BF16),
            gu_bf1=sb(pb, "gu_bf1", [128, DC * 2048], BF16),
            dn_bf0=sb(pb, "dn_bf0", [128, DC * 1024], BF16),
            hbuf0=sb(pb, "hbuf0", [128, 4096], BF16),
            hbuf1=sb(pb, "hbuf1", [128, 4096], BF16),
            sg=sb(pb, "sg", [128, 512], F32),
            routed_tm=sb(pb, "routed_tm", [128, n_tc * 8], F32),
            rscr=sb(pb, "rscr", [128, 64], F32),
            ps_r=ps(pb, "ps_r", [128, 8]),
            ps_g=ps(pb, "ps_g", [128, 512]),
            ps_u=ps(pb, "ps_u", [128, 512]),
            ps_d=ps(pb, "ps_d", [128, 1024]),
            pt=(ps(pb, "pt0", [128, 128], BF16), ps(pb, "pt1", [128, 128], BF16)),
        )
        tid = pb.dma("sync", bufs["identb"][:, :], ident)
        tbias = pb.dma("sync", bufs["bias_bc"][:, :], bias_d)
        bufs["bias_tok"] = tbias
        x_toks = []
        for g in range(0, n_tc, 4):
            gn = min(4, n_tc - g)
            x_toks.append(pb.dma(
                "sync",
                bufs["x"][:, g * D:(g + gn) * D].rearrange("p (t d) -> p t d", t=gn),
                xin[g:g + gn].rearrange("t p d -> p t d")))
        tgw = pb.dma("sync", bufs["rscr"][:, 0:64], gwT)
        tgwb = pb.emit("vector", lambda e: e.tensor_copy(bufs["gwT"][:, :], bufs["rscr"][:, 0:64]), deps=[tgw])
        fm_last = emit_ln_fm(pb, bufs["x"], bufs["xn_fm"], bufs["stats"],
                             bufs["tmp_bf"][:, 0:D],
                             bufs["tmp_bf"], bufs["identb"], bufs["pt"], n_tc=n_tc,
                             tok_x=x_toks + [tid, tgwb, tbias])
        wap = {
            "gu": [[gu[e_i, c // 2][:, (c % 2) * 1024:(c % 2) * 1024 + 1024] for c in range(DC * 2)] for e_i in range(8)],
            "down": [[down[e_i, c] for c in range(DC)] for e_i in range(8)],
            "sgu": [sgu[c // 2][:, (c % 2) * 1024:(c % 2) * 1024 + 1024] for c in range(DC * 2)],
            "sdown": [sdown[c] for c in range(DC)],
        }
        moe_toks = emit_moe_l0(pb, bufs, wap, n_tc=n_tc, fm_toks=[fm_last],
                               renorm=False, shared=True, l1=True)
        st_toks = []
        for g in range(0, n_tc, 4):
            gn = min(4, n_tc - g)
            st_toks.append(pb.dma(
                "gpsimd", xout[g:g + gn].rearrange("t p d -> p t d"),
                bufs["x"][:, g * D:(g + gn) * D].rearrange("p (t d) -> p t d", t=gn),
                deps=moe_toks))
        pb.emit("sync", lambda e: e.nop(), deps=st_toks)
        pb.build()
    return nc



def emit_ln_tm_f32(pb, x, out_tm, stats, sq_scr, n_tc=NTC, tok_x=()):
    n = n_tc
    t_red = []
    prev_sq = None
    for tc in range(n):
        xs = x[:, tc * D:(tc + 1) * D]
        dep1 = list(tok_x) + ([prev_sq] if prev_sq is not None else [])
        t1 = pb.emit("scalar", lambda e, xs=xs, tc=tc: e.activation(
            sq_scr[:, 0:D], xs, AF.Square, accum_out=stats[:, n + tc:n + tc + 1]),
            deps=dep1, strict=True)
        prev_sq = t1
        t2 = pb.emit("vector", lambda e, xs=xs, tc=tc: e.reduce_sum(
            stats[:, tc:tc + 1], xs, axis=AX.X), deps=tok_x)
        t_red += [t1, t2]
    tmu = pb.emit("vector", lambda e: e.tensor_scalar(
        stats[:, 2 * n:3 * n], stats[:, 0:n], 1.0 / D, None, ALU.mult, ALU.bypass),
        deps=t_red, strict=True)
    tm2 = pb.emit("vector", lambda e: e.tensor_tensor(
        stats[:, 0:n], stats[:, 2 * n:3 * n], stats[:, 2 * n:3 * n], ALU.mult),
        deps=[tmu], strict=True)
    tvar = pb.emit("vector", lambda e: e.scalar_tensor_tensor(
        stats[:, n:2 * n], stats[:, n:2 * n], 1.0 / D, stats[:, 0:n],
        ALU.mult, ALU.subtract), deps=[tm2], strict=True)
    tve = pb.emit("vector", lambda e: e.tensor_scalar(
        stats[:, n:2 * n], stats[:, n:2 * n], EPS_LN, None, ALU.add, ALU.bypass),
        deps=[tvar], strict=True)
    tln = pb.emit("scalar", lambda e: e.activation(
        stats[:, n:2 * n], stats[:, n:2 * n], AF.Ln, bias=0.0, scale=1.0),
        deps=[tve], strict=True)
    trstd = pb.emit("scalar", lambda e: e.activation(
        stats[:, 3 * n:4 * n], stats[:, n:2 * n], AF.Exp, bias=0.0, scale=-0.5),
        deps=[tln], strict=True)
    toks = []
    for tc in range(n):
        xs = x[:, tc * D:(tc + 1) * D]
        os = out_tm[:, tc * D:(tc + 1) * D]
        tt = pb.emit("vector", lambda e, xs=xs, os=os, tc=tc: e.tensor_scalar(
            os, xs, stats[:, 2 * n + tc:2 * n + tc + 1],
            stats[:, 3 * n + tc:3 * n + tc + 1], ALU.subtract, ALU.mult),
            deps=[trstd], strict=True)
        toks.append(tt)
    return toks


def emit_mingru(pb, bufs, cfg):
    n_tc = cfg["n_tc"]
    T = n_tc * 128
    L = cfg["layers"]
    x_mg = bufs["x_mg"]
    xn_fm = bufs["xn_fm"]
    W = bufs["w_bf"]
    psg, psu = bufs["ps_g"], bufs["ps_u"]
    ptF = bufs["ptF"]
    cbuf, ubuf, hbuf = bufs["cbuf"], bufs["ubuf"], bufs["hbuf"]
    zscr = bufs["zscr"]
    r1 = bufs["r1scr"]
    s1 = bufs["s1scr"]
    nb = bufs["nbcols"]
    bcols = bufs["bcols"]
    init = bufs["init"]
    hend = bufs["hend"]
    gb_tok = bufs["gb_tok"]
    x_toks = list(bufs["x_tok"])
    for l in range(L):
        g_bc, b_bc = bufs["g_bc"], bufs["b_bc"]
        bc_toks = []
        prev_bc = pb.tail.get(("mg_bc",), None)
        for which, dstt in ((0, g_bc), (1, b_bc)):
            for half in range(2):
                tm1 = pb.emit("tensor", lambda e, which=which, l=l, half=half, dstt=dstt: e.matmul(
                    bufs["ps_bc"][:, 0:512], bufs["onescol"][0:1, 0:128],
                    bufs["gbrows"][0:1, (which * L + l) * D + half * 512:
                                   (which * L + l) * D + half * 512 + 512],
                    start=True, stop=True),
                    deps=[gb_tok] + ([prev_bc] if prev_bc else []))
                prev_bc = pb.emit("scalar", lambda e, dstt=dstt, half=half: e.copy(
                    dstt[:, half * 512:half * 512 + 512], bufs["ps_bc"][:, 0:512]),
                    deps=[tm1])
                bc_toks.append(prev_bc)
        pb.tail[("mg_bc",)] = prev_bc
        lnt = emit_ln_fm(pb, x_mg, xn_fm, bufs["stats"], bufs["tmp_bf"][:, 0:D],
                         bufs["tmp_bf"], bufs["identb"], bufs["pt"], n_tc=n_tc,
                         tok_x=x_toks + bc_toks,
                         affine_wb=(g_bc, b_bc, bufs["lnscr"]))
        wl_tok = emit_weight_load_off(pb, cfg["w"][l], bufs["stage"], W, 1024, 0,
                                      deps=pb.tail.get(("mg_wfree",), ()) or (),
                                      eng_dma="sync")
        x_toks2 = []
        scan_toks = []
        for cc in range(DC):
            cu_toks = []
            for tt in range(n_tc // 4):
                mmg = mmh = None
                for dc in range(DC):
                    dep = [wl_tok, lnt]
                    if dc == 0 and ("mg_pg",) in pb.tail:
                        dep = dep + [pb.tail[("mg_pg",)]]
                    mmg = pb.emit("tensor", lambda e, l=l, cc=cc, tt=tt, dc=dc: e.matmul(
                        psg[:, 0:512],
                        W[:, dc * 2048 + cc * 128: dc * 2048 + cc * 128 + 128],
                        xn_fm[:, dc * T + tt * 512: dc * T + tt * 512 + 512],
                        start=(dc == 0), stop=(dc == DC - 1)), deps=dep)
                for dc in range(DC):
                    dep = [wl_tok, lnt]
                    if dc == 0 and ("mg_ph",) in pb.tail:
                        dep = dep + [pb.tail[("mg_ph",)]]
                    mmh = pb.emit("tensor", lambda e, l=l, cc=cc, tt=tt, dc=dc: e.matmul(
                        psu[:, 0:512],
                        W[:, dc * 2048 + 1024 + cc * 128: dc * 2048 + 1024 + cc * 128 + 128],
                        xn_fm[:, dc * T + tt * 512: dc * T + tt * 512 + 512],
                        start=(dc == 0), stop=(dc == DC - 1)), deps=dep)
                bcol = bcols[:, l * 16 + cc: l * 16 + cc + 1]
                nbcol = nb[:, l * 16 + cc: l * 16 + cc + 1]
                bcol2 = bcols[:, l * 16 + 8 + cc: l * 16 + 8 + cc + 1]
                tc1 = pb.emit("scalar", lambda e, cc=cc, tt=tt, nbcol=nbcol: e.activation(
                    cbuf[:, tt * 512:tt * 512 + 512], psg[:, 0:512], AF.Sigmoid,
                    bias=nbcol, scale=-1.0),
                    deps=[mmg, bufs["nb_tok"]] + list(pb.tail.get(("mg_cu",), ()))
                         + list(pb.tail.get(("mg_scan",), ())), strict=True)
                tz = pb.emit("scalar", lambda e, cc=cc, tt=tt, bcol=bcol: e.activation(
                    zscr[:, 0:512], psg[:, 0:512], AF.Sigmoid, bias=bcol, scale=1.0),
                    deps=[tc1, bufs["bc_tok"]], strict=True)
                pb.tail[("mg_pg",)] = tz
                tr1 = pb.emit("scalar", lambda e, bcol2=bcol2: e.activation(
                    r1[:, 0:512], psu[:, 0:512], AF.Relu, bias=bcol2, scale=1.0),
                    deps=[mmh, bufs["bc_tok"], tz], strict=True)
                ts1 = pb.emit("scalar", lambda e, bcol2=bcol2: e.activation(
                    s1[:, 0:512], psu[:, 0:512], AF.Sigmoid, bias=bcol2, scale=1.0),
                    deps=[tr1], strict=True)
                pb.tail[("mg_ph",)] = ts1
                tm1_ = pb.emit("vector", lambda e: e.tensor_scalar_min(
                    s1[:, 0:512], s1[:, 0:512], 0.5), deps=[ts1], strict=True)
                tg = pb.emit("vector", lambda e: e.tensor_tensor(
                    r1[:, 0:512], r1[:, 0:512], s1[:, 0:512], ALU.add),
                    deps=[tm1_, tr1], strict=True)
                tu = pb.emit("vector", lambda e, tt=tt: e.tensor_tensor(
                    ubuf[:, tt * 512:tt * 512 + 512], zscr[:, 0:512], r1[:, 0:512],
                    ALU.mult), deps=[tg, tz] + list(pb.tail.get(("mg_scan",), ())),
                    strict=True)
                pb.tail[("mg_cu",)] = (tu,)
                cu_toks.append(tu)
                cu_toks.append(tc1)
            icol = init[:, l * 8 + cc: l * 8 + cc + 1]
            tscan = pb.emit("vector", lambda e, cc=cc, icol=icol: e.tensor_tensor_scan(
                hbuf[:, 0:T], cbuf[:, 0:T], ubuf[:, 0:T], icol, ALU.mult, ALU.add),
                deps=cu_toks + [bufs["init_tok"]] + list(pb.tail.get(("mg_h",), ())),
                strict=True)
            scan_toks.append(tscan)
            pb.tail[("mg_scan",)] = (tscan,)
            th = pb.emit("vector", lambda e, l=l, cc=cc: e.tensor_copy(
                hend[:, l * 8 + cc: l * 8 + cc + 1], hbuf[:, T - 1:T]),
                deps=[tscan], strict=True)
            add_toks = []
            for tc2 in range(n_tc):
                ph = tc2 % 2
                tpt = pb.emit("tensor", lambda e, tc2=tc2, ph=ph: e.transpose(
                    ptF[ph][:, 0:128], hbuf[:, tc2 * 128:tc2 * 128 + 128],
                    bufs["identf"][:, 0:128]),
                    deps=[tscan] + list(pb.tail.get(("mg_pt%d" % ph,), ())))
                ta = pb.emit("vector", lambda e, tc2=tc2, cc=cc, ph=ph: e.tensor_tensor(
                    x_mg[:, tc2 * D + cc * 128: tc2 * D + cc * 128 + 128],
                    ptF[ph][:, 0:128],
                    x_mg[:, tc2 * D + cc * 128: tc2 * D + cc * 128 + 128], ALU.add),
                    deps=[tpt], strict=True)
                pb.tail[("mg_pt%d" % ph,)] = (ta,)
                add_toks.append(ta)
            pb.tail[("mg_h",)] = (add_toks[-1],) + (th,)
            x_toks2.extend(add_toks)
        pb.tail[("mg_wfree",)] = (x_toks2[-1],)
        x_toks = x_toks2
    return x_toks


def build_mingru_test(n_tc=NTC):
    nc = new_nc()
    es = ExitStack()
    T = n_tc * 128
    L = 3
    xin = nc.dram_tensor("xin", [n_tc, 128, D], F32, kind="ExternalInput").ap()
    w_d = nc.dram_tensor("w", [L, DC, 128, 2048], F32, kind="ExternalInput").ap()
    gb_d = nc.dram_tensor("gbrows", [1, 2 * L * D], BF16, kind="ExternalInput").ap()
    bc_d = nc.dram_tensor("bcols", [128, L * 16], F32, kind="ExternalInput").ap()
    init_d = nc.dram_tensor("initc", [128, L * 8], F32, kind="ExternalInput").ap()
    ident = nc.dram_tensor("ident", [128, 128], BF16, kind="ExternalInput").ap()
    identf_d = nc.dram_tensor("identf_d", [128, 128], F32, kind="ExternalInput").ap()
    ones_d = nc.dram_tensor("onesc", [1, 128], BF16, kind="ExternalInput").ap()
    xout = nc.dram_tensor("xout", [n_tc, 128, D], F32, kind="ExternalOutput").ap()
    hend_d = nc.dram_tensor("hend", [128, L * 8], F32, kind="ExternalOutput").ap()
    with es:
        pb = PB(nc, es)
        bufs = dict(
            x_mg=sb(pb, "x_mg", [128, n_tc * D], F32),
            tmp_bf=sb(pb, "tmp_bf", [128, D], BF16),
            stats=sb(pb, "stats", [128, 4 * n_tc], F32),
            identb=sb(pb, "identb", [128, 128], BF16),
            identf=sb(pb, "identf", [128, 128], F32),
            onescol=sb(pb, "onescol", [1, 128], BF16),
            gbrows=sb(pb, "gbrows_b", [1, 2 * L * D], BF16),
            bcols=sb(pb, "bcols_b", [128, L * 16], F32),
            nbcols=sb(pb, "nbcols", [128, L * 16], F32),
            init=sb(pb, "init_b", [128, L * 8], F32),
            hend=sb(pb, "hend_b", [128, L * 8], F32),
            g_bc=sb(pb, "g_bc", [128, D], F32),
            b_bc=sb(pb, "b_bc", [128, D], F32),
            lnscr=sb(pb, "lnscr", [128, D], F32),
            ps_g=ps(pb, "ps_g", [128, 512]),
            ps_u=ps(pb, "ps_u", [128, 512]),
            ps_bc=ps(pb, "ps_bc", [128, 512]),
            pt=(ps(pb, "pt0", [128, 128], BF16), ps(pb, "pt1", [128, 128], BF16)),
            ptF=(ps(pb, "ptF0", [128, 128], F32), ps(pb, "ptF1", [128, 128], F32)),
        )
        c1 = pb.dma("sync", bufs["identb"][:, :], ident)
        c2 = pb.dma("sync", bufs["identf"][:, :], identf_d)
        c3 = pb.dma("sync", bufs["onescol"][0:1, :], ones_d)
        c4 = pb.dma("sync", bufs["gbrows"][0:1, :], gb_d)
        c5 = pb.dma("sync", bufs["bcols"][:, :], bc_d)
        c6 = pb.dma("sync", bufs["init"][:, :], init_d)
        tnb = pb.emit("vector", lambda e: e.tensor_scalar(
            bufs["nbcols"][:, :], bufs["bcols"][:, :], -1.0, None, ALU.mult, ALU.bypass),
            deps=[c5])
        bufs["gb_tok"] = c4
        bufs["bc_tok"] = c5
        bufs["nb_tok"] = tnb
        bufs["init_tok"] = c6
        esA = ExitStack()
        with esA:
            saved = pb.es; pb.es = esA
            x = sb(pb, "x", [128, n_tc * D], F32)
            pb.es = saved
            x_toks = []
            for g in range(0, n_tc, 4):
                gn = min(4, n_tc - g)
                x_toks.append(pb.dma(
                    "gpsimd", x[:, g * D:(g + gn) * D].rearrange("p (t d) -> p t d", t=gn),
                    xin[g:g + gn].rearrange("t p d -> p t d")))
            ln_toks = emit_ln_tm_f32(pb, x, bufs["x_mg"], bufs["stats"],
                                     bufs["lnscr"], n_tc=n_tc, tok_x=x_toks + [c6])
        cfg = dict(n_tc=n_tc, layers=L,
                   w=[[w_d[l, c // 2][:, (c % 2) * 1024:(c % 2) * 1024 + 1024]
                       for c in range(DC * 2)] for l in range(L)])
        bufs["x_tok"] = ln_toks
        esB = ExitStack()
        with esB:
            saved = pb.es; pb.es = esB
            bufs["xn_fm"] = sb(pb, "xn_fm", [128, DC * T], BF16)
            bufs["w_bf"] = sb(pb, "w_bf", [128, DC * 2048], BF16)
            bufs["stage"] = sb(pb, "stage", [128, 2048], F32)
            bufs["cbuf"] = sb(pb, "cbuf", [128, T], F32)
            bufs["ubuf"] = sb(pb, "ubuf", [128, T], F32)
            bufs["hbuf"] = sb(pb, "hbuf", [128, T], F32)
            bufs["zscr"] = sb(pb, "zscr", [128, 512], F32)
            bufs["r1scr"] = sb(pb, "r1scr", [128, 512], F32)
            bufs["s1scr"] = sb(pb, "s1scr", [128, 512], F32)
            pb.es = saved
            mg_toks = emit_mingru(pb, bufs, cfg)
        entryC = pb.all_toks()
        esC = ExitStack()
        with esC:
            saved = pb.es; pb.es = esC
            xr = sb(pb, "xr", [128, n_tc * D], F32)
            pb.es = saved
            xr_toks = []
            for g in range(0, n_tc, 4):
                gn = min(4, n_tc - g)
                xr_toks.append(pb.dma(
                    "gpsimd", xr[:, g * D:(g + gn) * D].rearrange("p (t d) -> p t d", t=gn),
                    xin[g:g + gn].rearrange("t p d -> p t d"), deps=entryC))
            add_toks = []
            for tc in range(n_tc):
                ta = pb.emit("vector", lambda e, tc=tc: e.tensor_tensor(
                    xr[:, tc * D:(tc + 1) * D], bufs["x_mg"][:, tc * D:(tc + 1) * D],
                    xr[:, tc * D:(tc + 1) * D], ALU.add),
                    deps=[xr_toks[tc // 4]] + mg_toks[-1:], strict=True)
                add_toks.append(ta)
            st = []
            for g in range(0, n_tc, 4):
                gn = min(4, n_tc - g)
                st.append(pb.dma(
                    "gpsimd", xout[g:g + gn].rearrange("t p d -> p t d"),
                    xr[:, g * D:(g + gn) * D].rearrange("p (t d) -> p t d", t=gn),
                    deps=add_toks))
            st.append(pb.dma("sync", hend_d, bufs["hend"][:, :], deps=mg_toks))
            pb.emit("sync", lambda e: e.nop(), deps=st)
        pb.build()
    return nc



def emit_ln_stream(pb, bufs, src_aps, dst_aps, n_tc, tok0=()):
    scr = [bufs["lnt0"], bufs["lnt1"]]
    cs = bufs["cstats"]
    free = {0: (), 1: ()}
    toks = []
    for tc in range(n_tc):
        half = tc % 2
        s = scr[half]
        tl = pb.dma("gpsimd", s[:, 0:D], src_aps[tc], deps=list(tok0) + list(free[half]))
        t1 = pb.emit("scalar", lambda e, s=s: e.activation(
            bufs["sqscr"][:, 0:D], s[:, 0:D], AF.Square, accum_out=cs[:, 1:2]),
            deps=[tl] + list(pb.tail.get(("lns_sq",), ())), strict=True)
        pb.tail[("lns_sq",)] = (t1,)
        t2 = pb.emit("vector", lambda e, s=s: e.reduce_sum(cs[:, 0:1], s[:, 0:D], axis=AX.X),
                     deps=[tl] + list(pb.tail.get(("lns_cs",), ())), strict=True)
        t3 = pb.emit("vector", lambda e: e.tensor_scalar(cs[:, 2:3], cs[:, 0:1], 1.0 / D, None, ALU.mult, ALU.bypass), deps=[t2], strict=True)
        t4 = pb.emit("vector", lambda e: e.tensor_tensor(cs[:, 3:4], cs[:, 2:3], cs[:, 2:3], ALU.mult), deps=[t3], strict=True)
        t5 = pb.emit("vector", lambda e: e.scalar_tensor_tensor(cs[:, 4:5], cs[:, 1:2], 1.0 / D, cs[:, 3:4], ALU.mult, ALU.subtract), deps=[t4, t1], strict=True)
        t6 = pb.emit("vector", lambda e: e.tensor_scalar(cs[:, 4:5], cs[:, 4:5], EPS_LN, None, ALU.add, ALU.bypass), deps=[t5], strict=True)
        t7 = pb.emit("scalar", lambda e: e.activation(cs[:, 5:6], cs[:, 4:5], AF.Ln, bias=0.0, scale=1.0), deps=[t6], strict=True)
        t8 = pb.emit("scalar", lambda e: e.activation(cs[:, 6:7], cs[:, 5:6], AF.Exp, bias=0.0, scale=-0.5), deps=[t7], strict=True)
        t9 = pb.emit("vector", lambda e, s=s: e.tensor_scalar(
            s[:, 0:D], s[:, 0:D], cs[:, 2:3], cs[:, 6:7], ALU.subtract, ALU.mult),
            deps=[t8], strict=True)
        pb.tail[("lns_cs",)] = (t9,)
        ts = pb.dma("gpsimd", dst_aps[tc], s[:, 0:D], deps=[t9])
        free[half] = (ts,)
        toks.append(ts)
    return toks


def emit_retention_iter_p2(pb, bufs, cfg, ln1):
    n_tc = cfg["n_tc"]
    T = n_tc * 128
    H = 8
    HD = 128
    zi, zo, yd, x1 = cfg["zi"], cfg["zo"], cfg["yd"], cfg["x1"]
    xkv = bufs["xkv_fm"]
    it = cfg["iter"]
    entry0 = tuple(cfg.get("entry", ()))
    wb = bufs["whb"]
    abuf = bufs["abuf"]
    obuf = bufs["obuf"]
    obuf2 = bufs["obuf2"]
    ptB = bufs["ptB"]
    Q, K = bufs["q_fm"], bufs["k_fm"]
    QR, KR = bufs["qrot"], bufs["krot"]
    KH = bufs["khat"]
    V = bufs["v_tm"]
    TS1 = bufs["tscr1"]
    St = bufs["state"]
    Sbf = bufs["state_bf"]
    psA = bufs["ps_s0"]
    psB = bufs["ps_s1"]
    psO1 = bufs["ps_o1"]
    psO2 = bufs["ps_o2"]
    psSU = bufs["ps_su"]
    head_tok = entry0
    yd_toks = []
    for h in range(H):
        wl = {}
        for wi, name in enumerate(("wq", "wk", "wv")):
            wl[name] = emit_weight_load_off(pb, cfg[name][h], bufs["wstage"],
                                            bufs["whb"], 128, wi * 1024,
                                            deps=head_tok, eng_dma="sync")
        pq = {}
        for wi, (name, dst) in enumerate((("wq", Q), ("wk", K))):
            last = None
            for tt in range(T // 512):
                mm = None
                for dc2 in range(DC):
                    dep = [wl[name], ln1] + list(head_tok) + ([pq["last"]] if (dc2 == 0 and pq.get("last")) else [])
                    mm = pb.emit("tensor", lambda e, wi=wi, tt=tt, dc2=dc2, wb=wb: e.matmul(
                        psA[:, 0:512],
                        wb[:, wi * 1024 + dc2 * 128: wi * 1024 + dc2 * 128 + 128],
                        xkv[:, dc2 * T + tt * 512: dc2 * T + tt * 512 + 512],
                        start=(dc2 == 0), stop=(dc2 == DC - 1)), deps=dep)
                last = pb.emit("scalar", lambda e, dst=dst, tt=tt: e.copy(
                    dst[:, tt * 512:tt * 512 + 512], psA[:, 0:512]), deps=[mm], strict=True)
                pq[name] = last
                pq["last"] = last
        rot_toks = []
        rp = list(pb.tail.get(("ret_rot",), ()))
        tab = bufs["tab"]
        TSF = bufs["tscr1"]
        for pi, (src, dst) in enumerate(((Q, QR), (K, KR))):
            tld = [pb.dma("sync", tab[:, 0:T], cfg["tabs_d"][2 * pi],
                          deps=rp + list(head_tok) + [pq["last"]]),
                   pb.dma("sync", tab[:, T:2 * T], cfg["tabs_d"][2 * pi + 1],
                          deps=rp + list(head_tok) + [pq["last"]])]
            tsw1 = pb.dma("gpsimd", TSF[0:64, 0:T], src[64:128, 0:T],
                          deps=[pq["last"]] + rp + list(head_tok))
            tsw2 = pb.dma("gpsimd", TSF[64:128, 0:T], src[0:64, 0:T],
                          deps=[pq["last"]] + rp + list(head_tok))
            base = [pq["wq"], pq["wk"]] + tld + [tsw1, tsw2]
            tq1 = pb.emit("vector", lambda e, src=src, dst=dst, tab=tab: e.tensor_tensor(
                dst[:, 0:T], src[:, 0:T], tab[:, 0:T], ALU.mult),
                deps=base, strict=True)
            tq2 = pb.emit("vector", lambda e, tab=tab, TSF=TSF: e.tensor_tensor(
                TSF[:, 0:T], TSF[:, 0:T], tab[:, T:2 * T], ALU.mult),
                deps=base, strict=True)
            tq3 = pb.emit("vector", lambda e, dst=dst, TSF=TSF: e.tensor_tensor(
                dst[0:64, 0:T], dst[0:64, 0:T], TSF[0:64, 0:T], ALU.subtract),
                deps=[tq1, tq2], strict=True)
            tq4 = pb.emit("vector", lambda e, dst=dst, TSF=TSF: e.tensor_tensor(
                dst[64:128, 0:T], dst[64:128, 0:T], TSF[64:128, 0:T], ALU.add),
                deps=[tq3], strict=True)
            rot_toks.append(tq4)
            rp = [tq4]
        pb.tail[("ret_rot",)] = (rot_toks[-1],)
        pv = None
        kh_toks = []
        for kc in range(n_tc):
            mm = None
            for dc2 in range(DC):
                dep = [wl["wv"], ln1] + list(head_tok) + ([pv] if (dc2 == 0 and pv is not None) else [])
                mm = pb.emit("tensor", lambda e, kc=kc, dc2=dc2, wb=wb: e.matmul(
                    psB[:, 0:128],
                    xkv[:, dc2 * T + kc * 128: dc2 * T + kc * 128 + 128],
                    wb[:, 2 * 1024 + dc2 * 128: 2 * 1024 + dc2 * 128 + 128],
                    start=(dc2 == 0), stop=(dc2 == DC - 1)), deps=dep)
            pv = pb.emit("scalar", lambda e, kc=kc: e.copy(
                V[:, kc * 128:kc * 128 + 128], psB[:, 0:128]), deps=[mm], strict=True)
            tkt = pb.emit("tensor", lambda e, kc=kc, ptB=ptB: e.transpose(
                ptB[kc % 2][:, 0:128], KR[:, kc * 128:kc * 128 + 128],
                bufs["identb"][:, 0:128]),
                deps=rot_toks + list(pb.tail.get(("ret_kh%d" % (kc % 2),), ())))
            tke = pb.emit("vector", lambda e, kc=kc, h=h, ptB=ptB: e.tensor_scalar(
                KH[:, kc * 128:kc * 128 + 128], ptB[kc % 2][:, 0:128],
                bufs["ghcol"][:, h:h + 1], None, ALU.mult, ALU.bypass),
                deps=[tkt, cfg["dk_tok"]], strict=True)
            pb.tail[("ret_kh%d" % (kc % 2),)] = (tke,)
            kh_toks.append(tke)
        tst = pb.emit("vector", lambda e: e.memset(St[:, :], 0.0), deps=list(head_tok))
        st_tok = tst
        sbf_tok = pb.emit("vector", lambda e: e.tensor_copy(Sbf[:, :], St[:, :]),
                          deps=[tst], strict=True)
        prev_A = ()
        prev_o = ()
        for kc in range(n_tc):
            tA = pb.emit("tensor", lambda e, kc=kc: e.matmul(
                psA[:, 0:128], KR[:, kc * 128:kc * 128 + 128],
                QR[:, kc * 128:kc * 128 + 128], start=True, stop=True),
                deps=rot_toks + list(prev_A) + [pq["last"]])
            tAm = pb.emit("vector", lambda e, kc=kc, h=h: e.tensor_tensor(
                abuf[:, 0:128], psA[:, 0:128],
                bufs["dmask"][:, h * 128:(h + 1) * 128], ALU.mult),
                deps=[tA, cfg["dk_tok"]] + list(pb.tail.get(("ret_ab",), ())), strict=True)
            prev_A = (tAm,)
            t1_ = pb.emit("tensor", lambda e, kc=kc: e.matmul(
                psO1[:, 0:128], abuf[:, 0:128], V[:, kc * 128:kc * 128 + 128],
                start=True, stop=True), deps=[tAm, pv] + list(prev_o))
            pb.tail[("ret_ab",)] = (t1_,)
            t2_ = pb.emit("tensor", lambda e, kc=kc: e.matmul(
                psO2[:, 0:128], QR[:, kc * 128:kc * 128 + 128], Sbf[:, 0:128],
                start=True, stop=True), deps=[sbf_tok] + list(prev_o))
            t3_ = pb.emit("tensor", lambda e, kc=kc: e.matmul(
                psSU[:, 0:128], KH[:, kc * 128:kc * 128 + 128],
                V[:, kc * 128:kc * 128 + 128], start=True, stop=True),
                deps=[kh_toks[kc], pv] + list(pb.tail.get(("ret_su",), ())))
            to1 = pb.emit("vector", lambda e, h=h: e.tensor_scalar(
                obuf[:, 0:128], psO2[:, 0:128], bufs["gcol"][:, h:h + 1],
                None, ALU.mult, ALU.bypass),
                deps=[t2_, cfg["dk_tok"]] + list(pb.tail.get(("ret_ob",), ())), strict=True)
            to2 = pb.emit("vector", lambda e: e.tensor_tensor(
                obuf[:, 0:128], obuf[:, 0:128], psO1[:, 0:128], ALU.add),
                deps=[to1, t1_], strict=True)
            prev_o = (to2,)
            ts1_ = pb.emit("vector", lambda e, h=h: e.scalar_tensor_tensor(
                St[:, :], St[:, :], cfg["gC"][h], psSU[:, 0:128], ALU.mult, ALU.add),
                deps=[t3_, st_tok, sbf_tok], strict=True)
            pb.tail[("ret_su",)] = (ts1_,)
            st_tok = ts1_
            sbf_tok = pb.emit("vector", lambda e: e.tensor_copy(Sbf[:, :], St[:, :]),
                              deps=[ts1_, t2_], strict=True)
            g1 = pb.emit("vector", lambda e: e.reduce_sum(
                bufs["cstats"][:, 0:1], obuf[:, 0:128], axis=AX.X),
                deps=[to2] + list(pb.tail.get(("ret_gn",), ())), strict=True)
            g2 = pb.emit("scalar", lambda e: e.activation(
                bufs["sqscr"][:, 0:128], obuf[:, 0:128], AF.Square,
                accum_out=bufs["cstats"][:, 1:2]),
                deps=[to2] + list(pb.tail.get(("ret_gn_sq",), ()))
                     + list(pb.tail.get(("ret_gn",), ())), strict=True)
            pb.tail[("ret_gn_sq",)] = (g2,)
            g3 = pb.emit("vector", lambda e: e.tensor_scalar(
                bufs["cstats"][:, 2:3], bufs["cstats"][:, 0:1], 1.0 / HD, None,
                ALU.mult, ALU.bypass), deps=[g1], strict=True)
            g4 = pb.emit("vector", lambda e: e.tensor_tensor(
                bufs["cstats"][:, 3:4], bufs["cstats"][:, 2:3], bufs["cstats"][:, 2:3],
                ALU.mult), deps=[g3], strict=True)
            g5 = pb.emit("vector", lambda e: e.scalar_tensor_tensor(
                bufs["cstats"][:, 4:5], bufs["cstats"][:, 1:2], 1.0 / HD,
                bufs["cstats"][:, 3:4], ALU.mult, ALU.subtract), deps=[g4, g2], strict=True)
            g6 = pb.emit("vector", lambda e: e.tensor_scalar(
                bufs["cstats"][:, 4:5], bufs["cstats"][:, 4:5], EPS_LN, None,
                ALU.add, ALU.bypass), deps=[g5], strict=True)
            g7 = pb.emit("scalar", lambda e: e.activation(
                bufs["cstats"][:, 5:6], bufs["cstats"][:, 4:5], AF.Ln, bias=0.0, scale=1.0),
                deps=[g6], strict=True)
            g8 = pb.emit("scalar", lambda e: e.activation(
                bufs["cstats"][:, 6:7], bufs["cstats"][:, 5:6], AF.Exp, bias=0.0, scale=-0.5),
                deps=[g7], strict=True)
            g9 = pb.emit("vector", lambda e: e.tensor_scalar(
                obuf[:, 0:128], obuf[:, 0:128], bufs["cstats"][:, 2:3],
                bufs["cstats"][:, 6:7], ALU.subtract, ALU.mult), deps=[g8], strict=True)
            pb.tail[("ret_gn",)] = (g9,)
            g10 = pb.emit("vector", lambda e, h=h: e.tensor_tensor(
                obuf[:, 0:128], obuf[:, 0:128],
                bufs["gnw_bc"][:, h * 128:(h + 1) * 128], ALU.mult),
                deps=[g9, cfg["gn_tok"]], strict=True)
            g11 = pb.emit("vector", lambda e, h=h: e.tensor_tensor(
                obuf2[:, 0:128], obuf[:, 0:128],
                bufs["gnb_bc"][:, h * 128:(h + 1) * 128], ALU.add),
                deps=[g10] + list(pb.tail.get(("ret_st",), ())), strict=True)
            pb.tail[("ret_ob",)] = (g11,)
            tstore = pb.dma("gpsimd", yd[kc][:, h * 128:h * 128 + 128],
                            obuf2[:, 0:128], deps=[g11])
            pb.tail[("ret_st",)] = (tstore,)
            yd_toks.append(tstore)
        head_tok = (yd_toks[-1], sbf_tok) + tuple(rot_toks)
    return ln1, yd_toks


def emit_gb_bcast(pb, bufs, g_off, b_off, dst_g=None, dst_b=None, cols=D):
    dst_g = dst_g if dst_g is not None else bufs["g_bc"]
    dst_b = dst_b if dst_b is not None else bufs["b_bc"]
    psbc = bufs["ps_bc"]
    onesc = bufs["onescol"]
    gbr = bufs["gbrows"]
    prev = list(pb.tail.get(("gb_bc",), ())) + [pb.barrier_tok("vector")]
    toks = []
    for off, dstt in ((g_off, dst_g), (b_off, dst_b)):
        for half in range(cols // 512):
            tm1 = pb.emit("tensor", lambda e, off=off, half=half, psbc=psbc, gbr=gbr, onesc=onesc: e.matmul(
                psbc[:, 0:512], onesc[0:1, 0:128],
                gbr[0:1, off + half * 512: off + half * 512 + 512],
                start=True, stop=True),
                deps=[bufs["gb_tok"]] + prev)
            tcp = pb.emit("scalar", lambda e, dstt=dstt, half=half, psbc=psbc: e.copy(
                dstt[:, half * 512:half * 512 + 512], psbc[:, 0:512]),
                deps=[tm1])
            prev = [tcp]
            toks.append(tcp)
    pb.tail[("gb_bc",)] = (toks[-1],)
    return toks


def emit_retention_p3(pb, bufs, cfg, yd_toks, ln1_tok):
    n_tc = cfg["n_tc"]
    T = n_tc * 128
    xkv = bufs["xkv_fm"]
    wg, wo = bufs["wg_bf"], bufs["wo_bf"]
    psA = bufs["ps_s0"]
    psB = bufs["ps_s1"]
    ytile = bufs["ytile"]
    gtile = bufs["gtile"]
    yfm = bufs["yfm"]
    ztile = bufs["zt"]
    ptB = bufs["ptB"]
    x1_toks = []
    ydep = [yd_toks[-1]]
    for tc in range(n_tc):
        ty = pb.dma("gpsimd", ytile[:, 0:D], cfg["yd"][tc],
                    deps=ydep + list(pb.tail.get(("p3_y",), ())))
        tz = pb.dma("sync", ztile[:, 0:D], cfg["zi"][tc],
                    deps=list(pb.tail.get(("p3_z",), ())))
        tr_toks = []
        for dc2 in range(2):
            mm = None
            for dc in range(DC):
                dep = [cfg["wg_tok"], ln1_tok] + (list(pb.tail.get(("p3_g",), ())) if dc == 0 else [])
                mm = pb.emit("tensor", lambda e, tc=tc, dc=dc, dc2=dc2: e.matmul(
                    psA[:, 0:512], xkv[:, dc * T + tc * 128: dc * T + tc * 128 + 128],
                    wg[:, dc * 1024 + dc2 * 512: dc * 1024 + dc2 * 512 + 512],
                    start=(dc == 0), stop=(dc == DC - 1)), deps=dep)
            tg = pb.emit("scalar", lambda e: e.activation(
                gtile[:, 0:512], psA[:, 0:512], AF.Silu),
                deps=[mm] + list(pb.tail.get(("p3_gt",), ())), strict=True)
            tmul = pb.emit("vector", lambda e, dc2=dc2: e.tensor_tensor(
                ytile[:, dc2 * 512:dc2 * 512 + 512], gtile[:, 0:512],
                ytile[:, dc2 * 512:dc2 * 512 + 512], ALU.mult),
                deps=[tg, ty], strict=True)
            pb.tail[("p3_g",)] = (tmul,)
            pb.tail[("p3_gt",)] = (tmul,)
            tr_toks.append(tmul)
        tps = []
        for dc in range(DC):
            ph = dc % 2
            tt1 = pb.emit("tensor", lambda e, dc=dc, ph=ph, ptB=ptB: e.transpose(
                ptB[ph][:, 0:128], ytile[:, dc * 128:dc * 128 + 128],
                bufs["identb"][:, 0:128]),
                deps=tr_toks + list(pb.tail.get(("p3_pt%d" % ph,), ())))
            tt2 = pb.emit("scalar", lambda e, dc=dc, ph=ph, ptB=ptB: e.copy(
                yfm[:, dc * 128:dc * 128 + 128], ptB[ph][:, 0:128]), deps=[tt1],
                strict=True)
            pb.tail[("p3_pt%d" % ph,)] = (tt2,)
            tps.append(tt2)
        pb.tail[("p3_y",)] = (tps[-1],)
        st_toks = []
        for dc2 in range(2):
            mm = None
            for cf in range(DC):
                dep = [cfg["wo_tok"], tps[-1]] + (list(pb.tail.get(("p3_wo%d" % dc2,), ())) if cf == 0 else [])
                mm = pb.emit("tensor", lambda e, cf=cf, dc2=dc2: e.matmul(
                    psB[:, dc2 * 512:dc2 * 512 + 512], yfm[:, cf * 128:cf * 128 + 128],
                    wo[:, cf * 1024 + dc2 * 512: cf * 1024 + dc2 * 512 + 512],
                    start=(cf == 0), stop=(cf == DC - 1)), deps=dep)
            ta = pb.emit("vector", lambda e, dc2=dc2: e.tensor_tensor(
                ztile[:, dc2 * 512:dc2 * 512 + 512], psB[:, dc2 * 512:dc2 * 512 + 512],
                ztile[:, dc2 * 512:dc2 * 512 + 512], ALU.add),
                deps=[mm, tz], strict=True)
            pb.tail[("p3_wo%d" % dc2,)] = (ta,)
            st_toks.append(ta)
        tst = pb.dma("sync", cfg["x1"][tc], ztile[:, 0:D], deps=st_toks)
        pb.tail[("p3_z",)] = (tst,)
        x1_toks.append(tst)
    return x1_toks


def emit_retention_p4(pb, bufs, cfg, x1_toks):
    n_tc = cfg["n_tc"]
    gb = emit_gb_bcast(pb, bufs, cfg["ln2g_off"], cfg["ln2b_off"])
    w1, w2 = bufs["w1_bf"], bufs["w2_bf"]
    psH = bufs["ps_s1"]
    psO = bufs["ps_o2b"]
    h1c = bufs["h1c"]
    ztile = bufs["zt"]
    cs = bufs["cstats"]
    xn2 = bufs["xn2t"]
    ptp = bufs["pt"]
    b1v = bufs["b1cols_view"]
    b2r = bufs["b2row"]
    zo_toks = []
    for tc in range(n_tc):
        tz = pb.dma("sync", ztile[:, 0:D], cfg["x1"][tc],
                    deps=list(pb.tail.get(("p4_z",), ())) + list(pb.tail.get(("p3_z",), ()))
                         + [x1_toks[tc]])
        t1 = pb.emit("scalar", lambda e: e.activation(
            bufs["sqscr"][:, 0:D], ztile[:, 0:D], AF.Square, accum_out=cs[:, 1:2]),
            deps=[tz] + list(pb.tail.get(("p4_sq",), ())), strict=True)
        pb.tail[("p4_sq",)] = (t1,)
        t2 = pb.emit("vector", lambda e: e.reduce_sum(cs[:, 0:1], ztile[:, 0:D], axis=AX.X),
                     deps=[tz] + list(pb.tail.get(("p4_cs",), ())), strict=True)
        t3 = pb.emit("vector", lambda e: e.tensor_scalar(cs[:, 2:3], cs[:, 0:1], 1.0 / D, None, ALU.mult, ALU.bypass), deps=[t2], strict=True)
        t4 = pb.emit("vector", lambda e: e.tensor_tensor(cs[:, 3:4], cs[:, 2:3], cs[:, 2:3], ALU.mult), deps=[t3], strict=True)
        t5 = pb.emit("vector", lambda e: e.scalar_tensor_tensor(cs[:, 4:5], cs[:, 1:2], 1.0 / D, cs[:, 3:4], ALU.mult, ALU.subtract), deps=[t4, t1], strict=True)
        t6 = pb.emit("vector", lambda e: e.tensor_scalar(cs[:, 4:5], cs[:, 4:5], EPS_LN, None, ALU.add, ALU.bypass), deps=[t5], strict=True)
        t7 = pb.emit("scalar", lambda e: e.activation(cs[:, 5:6], cs[:, 4:5], AF.Ln, bias=0.0, scale=1.0), deps=[t6], strict=True)
        t8 = pb.emit("scalar", lambda e: e.activation(cs[:, 6:7], cs[:, 5:6], AF.Exp, bias=0.0, scale=-0.5), deps=[t7], strict=True)
        t9 = pb.emit("vector", lambda e: e.tensor_scalar(
            bufs["lnscr"][:, 0:D], ztile[:, 0:D], cs[:, 2:3], cs[:, 6:7],
            ALU.subtract, ALU.mult), deps=[t8] + list(pb.tail.get(("p4_ln",), ())), strict=True)
        t10 = pb.emit("vector", lambda e: e.tensor_tensor(
            bufs["lnscr"][:, 0:D], bufs["lnscr"][:, 0:D], bufs["g_bc"][:, 0:D], ALU.mult),
            deps=[t9] + list(gb), strict=True)
        t11 = pb.emit("vector", lambda e: e.tensor_tensor(
            bufs["tmp_bf"][:, 0:D], bufs["lnscr"][:, 0:D], bufs["b_bc"][:, 0:D], ALU.add),
            deps=[t10] + list(pb.tail.get(("p4_tmp",), ())), strict=True)
        pb.tail[("p4_ln",)] = (t11,)
        tps = []
        for dc in range(DC):
            ph = dc % 2
            tt1 = pb.emit("tensor", lambda e, dc=dc, ph=ph, ptp=ptp: e.transpose(
                ptp[ph][:, 0:128], bufs["tmp_bf"][:, dc * 128:dc * 128 + 128],
                bufs["identb"][:, 0:128]),
                deps=[t11] + list(pb.tail.get(("p4_pt%d" % ph,), ())))
            tt2 = pb.emit("scalar", lambda e, dc=dc, ph=ph, ptp=ptp: e.copy(
                xn2[:, dc * 128:dc * 128 + 128], ptp[ph][:, 0:128]), deps=[tt1],
                strict=True)
            pb.tail[("p4_pt%d" % ph,)] = (tt2,)
            tps.append(tt2)
        pb.tail[("p4_tmp",)] = (tps[-1],)
        xn_tok = tps[-1]
        mm_last = None
        for hf in range(32):
            mm = None
            for dc in range(DC):
                dep = [cfg["w1_tok"], xn_tok] + (list(pb.tail.get(("p4_h",), ())) if dc == 0 else [])
                mm = pb.emit("tensor", lambda e, tc=tc, hf=hf, dc=dc: e.matmul(
                    psH[:, 0:128],
                    w1[:, dc * 4096 + hf * 128: dc * 4096 + hf * 128 + 128],
                    xn2[:, dc * 128: dc * 128 + 128],
                    start=(dc == 0), stop=(dc == DC - 1)), deps=dep)
            th = pb.emit("scalar", lambda e, hf=hf: e.activation(
                h1c[:, 0:128], psH[:, 0:128], AF.Silu,
                bias=b1v[:, hf:hf + 1], scale=1.0),
                deps=[mm, cfg["b1_tok"]] + list(pb.tail.get(("p4_h2",), ())), strict=True)
            pb.tail[("p4_h",)] = (th,)
            wmms = []
            for dc2 in range(2):
                wmm = pb.emit("tensor", lambda e, hf=hf, dc2=dc2: e.matmul(
                    psO[:, dc2 * 512:dc2 * 512 + 512],
                    h1c[:, 0:128],
                    w2[:, hf * 1024 + dc2 * 512: hf * 1024 + dc2 * 512 + 512],
                    start=(hf == 0), stop=False),
                    deps=([th, cfg["w2_tok"]] + (list(pb.tail.get(("p4_o",), ())) if hf == 0 else [])))
                wmms.append(wmm)
            pb.tail[("p4_h2",)] = (wmms[-1],)
            mm_last = wmms[-1]
        x0aps = cfg.get("x0")
        if x0aps is not None:
            x0s = bufs["x0scr"]
            tx0 = pb.dma("gpsimd", x0s[:, 0:D], x0aps[tc],
                         deps=list(pb.tail.get(("p4_x0",), ())))
        ev_toks = []
        for dc2 in range(2):
            bmm = pb.emit("tensor", lambda e, dc2=dc2: e.matmul(
                psO[:, dc2 * 512:dc2 * 512 + 512], bufs["onescol"][0:1, 0:128],
                b2r[0:1, dc2 * 512:dc2 * 512 + 512],
                start=False, stop=True), deps=[mm_last, cfg["b2_tok"]])
            ta = pb.emit("vector", lambda e, dc2=dc2, tc=tc: e.tensor_tensor(
                ztile[:, dc2 * 512:dc2 * 512 + 512],
                psO[:, dc2 * 512:dc2 * 512 + 512],
                ztile[:, dc2 * 512:dc2 * 512 + 512], ALU.add),
                deps=[bmm, tz], strict=True)
            if x0aps is not None:
                ta = pb.emit("vector", lambda e, dc2=dc2, x0s=x0s: e.tensor_tensor(
                    ztile[:, dc2 * 512:dc2 * 512 + 512], x0s[:, dc2 * 512:dc2 * 512 + 512],
                    ztile[:, dc2 * 512:dc2 * 512 + 512], ALU.add),
                    deps=[ta, tx0], strict=True)
            ev_toks.append(ta)
        if x0aps is not None:
            pb.tail[("p4_x0",)] = tuple(ev_toks)
        pb.tail[("p4_o",)] = tuple(ev_toks)
        tst = pb.dma("sync", cfg["zo"][tc], ztile[:, 0:D], deps=ev_toks)
        pb.tail[("p4_z",)] = (tst,)
        zo_toks.append(tst)
    return zo_toks


def build_retention_test(n_tc=32, iters=1, outer_ln=False, final_add=False,
                         it_base=0):
    nc = new_nc()
    es = ExitStack()
    T = n_tc * 128
    H = 8
    zin = nc.dram_tensor("zin", [n_tc, 128, D], F32, kind="ExternalInput").ap()
    x0_d = nc.dram_tensor("x0", [n_tc, 128, D], F32, kind="ExternalInput").ap() if final_add else None
    z0_d = nc.dram_tensor("z0_scr", [n_tc, 128, D], F32).ap() if outer_ln else None
    wq_d = nc.dram_tensor("wq", [iters, H, DC, 128, 128], F32, kind="ExternalInput").ap()
    wk_d = nc.dram_tensor("wk", [iters, H, DC, 128, 128], F32, kind="ExternalInput").ap()
    wv_d = nc.dram_tensor("wv", [iters, H, DC, 128, 128], F32, kind="ExternalInput").ap()
    wg_d = nc.dram_tensor("wg", [iters, DC, 128, D], F32, kind="ExternalInput").ap()
    wo_d = nc.dram_tensor("wo", [iters, DC, 128, D], F32, kind="ExternalInput").ap()
    w1_d = nc.dram_tensor("w1", [iters, DC, 128, 4096], F32, kind="ExternalInput").ap()
    w2_d = nc.dram_tensor("w2", [iters, 32, 128, D], F32, kind="ExternalInput").ap()
    gb_d = nc.dram_tensor("gbrows", [1, iters * 6 * D], BF16, kind="ExternalInput").ap()
    b1_d = nc.dram_tensor("b1cols", [128, iters * 32], F32, kind="ExternalInput").ap()
    b2_d = nc.dram_tensor("b2rows", [1, iters * D], BF16, kind="ExternalInput").ap()
    tabs_d = nc.dram_tensor("xpos", [4, 128, T], BF16, kind="ExternalInput").ap()
    dmask_d = nc.dram_tensor("dmask", [128, H * 128], F32, kind="ExternalInput").ap()
    gcols_d = nc.dram_tensor("gcols", [128, 2 * H], F32, kind="ExternalInput").ap()
    ident = nc.dram_tensor("ident", [128, 128], BF16, kind="ExternalInput").ap()
    ones_d = nc.dram_tensor("onesc", [1, 128], BF16, kind="ExternalInput").ap()
    zout = nc.dram_tensor("zout", [n_tc, 128, D], F32, kind="ExternalOutput").ap()
    yd_d = nc.dram_tensor("yd_scr", [n_tc, 128, D], BF16).ap()
    x1_d = nc.dram_tensor("x1_scr", [n_tc, 128, D], F32).ap()
    z1_d = nc.dram_tensor("z1_scr", [n_tc, 128, D], F32).ap()

    with es:
        pb = PB(nc, es)
        bufs = dict(
            stats=sb(pb, "stats", [128, 4 * n_tc], F32),
            tmp_bf=sb(pb, "tmp_bf", [128, D], BF16),
            identb=sb(pb, "identb", [128, 128], BF16),
            onescol=sb(pb, "onescol", [1, 128], BF16),
            gbrows=sb(pb, "gbrows_b", [1, 6 * D], BF16),
            b1cols=sb(pb, "b1cols_b", [128, iters * 32], F32),
            b2rows=sb(pb, "b2rows_b", [1, iters * D], BF16),
            g_bc=sb(pb, "g_bc", [128, D], F32),
            b_bc=sb(pb, "b_bc", [128, D], F32),
            gnw_bc=sb(pb, "gnw_bc", [128, H * 128], F32),
            gnb_bc=sb(pb, "gnb_bc", [128, H * 128], F32),
            lnscr=sb(pb, "lnscr", [128, D], F32),
            dram_scr=sb(pb, "dram_scr", [128, 2 * D], F32),
            zt=sb(pb, "zt", [128, D], F32),
            wstage=sb(pb, "wstage", [128, 1024], F32),
            sqscr=sb(pb, "sqscr", [128, D], F32),
            cstats=sb(pb, "cstats", [128, 8], F32),
            gcol=sb(pb, "gcol", [128, H], F32),
            ghcol=sb(pb, "ghcol", [128, H], F32),
            dmask=sb(pb, "dmask_b", [128, H * 128], F32),
        )
        c1 = pb.dma("sync", bufs["identb"][:, :], ident)
        c2 = pb.dma("sync", bufs["onescol"][0:1, :], ones_d)
        c3 = pb.dma("sync", bufs["gbrows"][0:1, :], gb_d[0:1, 0:6 * D])
        c4 = pb.dma("sync", bufs["b1cols"][:, :], b1_d)
        c5 = pb.dma("sync", bufs["b2rows"][0:1, :], b2_d)
        c6 = pb.dma("sync", bufs["gcol"][:, :], gcols_d[:, 0:H])
        c7 = pb.dma("sync", bufs["ghcol"][:, :], gcols_d[:, H:2 * H])
        c8 = pb.dma("sync", bufs["dmask"][:, :], dmask_d)
        bufs["gb_tok"] = c3
        const_tok = c8
        gammas = 1.0 - np.exp(np.linspace(np.log(1.0 / 32), np.log(1.0 / 512), H))
        entry = [c1, c2, c3, c4, c5, c6, c7, c8]
        if outer_ln:
            es0 = ExitStack()
            with es0:
                sv0 = pb.es; pb.es = es0
                bufs["lnt0"] = sb(pb, "lnt0", [128, D], F32)
                bufs["lnt1"] = sb(pb, "lnt1", [128, D], F32)
                pb.es = sv0
                ln0 = emit_ln_stream(pb, bufs, [zin[t] for t in range(n_tc)],
                                     [z0_d[t] for t in range(n_tc)], n_tc, tok0=entry)
            del bufs["lnt0"], bufs["lnt1"]
            entry = entry + [ln0[-1]]
            zi_aps = [z0_d[t] for t in range(n_tc)]
        else:
            zi_aps = [zin[t] for t in range(n_tc)]
        for it in range(iters):
            if it > 0:
                c3 = pb.dma("sync", bufs["gbrows"][0:1, :],
                            gb_d[0:1, it * 6 * D:(it + 1) * 6 * D], deps=entry)
                bufs["gb_tok"] = c3
            zo_aps = [zout[t] for t in range(n_tc)] if it == iters - 1 else [z1_d[t] for t in range(n_tc)]
            cfg = dict(
                n_tc=n_tc, iter=it,
                zi=zi_aps, zo=zo_aps,
                yd=[yd_d[t] for t in range(n_tc)],
                x1=[x1_d[t] for t in range(n_tc)],
                wq=[[wq_d[it, h, c] for c in range(DC)] for h in range(H)],
                wk=[[wk_d[it, h, c] for c in range(DC)] for h in range(H)],
                wv=[[wv_d[it, h, c] for c in range(DC)] for h in range(H)],
                ln1g_off=0 * D, ln1b_off=1 * D,
                ln2g_off=2 * D, ln2b_off=3 * D,
                gng_off=4 * D, gnb_off=5 * D,
                gC=[float(g) ** 128 for g in gammas],
                dk_tok=const_tok, tab_tok=None,
                b1_tok=c4, b2_tok=c5,
                entry=list(entry),
            )
            es1 = ExitStack()
            with es1:
                sv = pb.es; pb.es = es1
                bufs["pt"] = (ps(pb, "ptP1a%d" % it, [128, 128], BF16),
                              ps(pb, "ptP1b%d" % it, [128, 128], BF16))
                bufs["ps_bc"] = ps(pb, "psbc1%d" % it, [128, 512])
                pb.es = sv
                gn_toks = []
                prevgn = list(pb.tail.get(("gb_bc",), ())) + [pb.barrier_tok("vector")]
                for h in range(H):
                    for which, dstt in ((cfg["gng_off"], bufs["gnw_bc"]),
                                        (cfg["gnb_off"], bufs["gnb_bc"])):
                        psbcl = bufs["ps_bc"]
                        tm1 = pb.emit("tensor", lambda e, which=which, h=h, psbcl=psbcl: e.matmul(
                            psbcl[:, 0:128], bufs["onescol"][0:1, 0:128],
                            bufs["gbrows"][0:1, which + h * 128: which + h * 128 + 128],
                            start=True, stop=True), deps=[bufs["gb_tok"]] + prevgn)
                        tcp = pb.emit("scalar", lambda e, dstt=dstt, h=h, psbcl=psbcl: e.copy(
                            dstt[:, h * 128:(h + 1) * 128], psbcl[:, 0:128]),
                            deps=[tm1])
                        prevgn = [tcp]
                        gn_toks.append(tcp)
                pb.tail[("gb_bc",)] = (gn_toks[-1],)
                cfg["gn_tok"] = gn_toks[-1]
                gb = emit_gb_bcast(pb, bufs, cfg["ln1g_off"], cfg["ln1b_off"])
                esKV = ExitStack()
                svv = pb.es; pb.es = esKV
                xkv = sb(pb, "xkv%d" % it, [128, DC * T], BF16)
                pb.es = svv
                bufs["xkv_fm"] = xkv
                ln1 = emit_ln_fm(pb, None, xkv, bufs["stats"], bufs["tmp_bf"][:, 0:D],
                                 bufs["tmp_bf"], bufs["identb"], bufs["pt"], n_tc=n_tc,
                                 tok_x=list(gb) + list(entry),
                                 affine_wb=(bufs["g_bc"], bufs["b_bc"], bufs["lnscr"]),
                                 src_dram=zi_aps, dram_scr=bufs["dram_scr"], dst_stride=T)
            es2 = ExitStack()
            with es2:
                sv = pb.es; pb.es = es2
                for nm, shp, dt_ in (("q_fm", [128, T], BF16), ("k_fm", [128, T], BF16),
                                     ("qrot", [128, T], BF16), ("krot", [128, T], BF16),
                                     ("khat", [128, T], BF16), ("v_tm", [128, T], BF16),
                                     ("tscr1", [128, T], BF16),
                                     ("whb", [128, 3 * 1024], BF16),
                                     ("abuf", [128, 128], BF16),
                                     ("obuf", [128, 128], F32),
                                     ("obuf2", [128, 128], BF16),
                                     ("state", [128, 128], F32),
                                     ("state_bf", [128, 128], BF16),
                                     ("tab", [128, 2 * T], BF16)):
                    bufs[nm] = sb(pb, nm + "_%d" % it, shp, dt_)
                bufs["ps_s0"] = ps(pb, "psA%d" % it, [128, 512])
                bufs["ps_s1"] = ps(pb, "psB%d" % it, [128, 512])
                bufs["ps_o1"] = ps(pb, "psO1%d" % it, [128, 128])
                bufs["ps_o2"] = ps(pb, "psO2%d" % it, [128, 128])
                bufs["ps_su"] = ps(pb, "psSU%d" % it, [128, 128])
                bufs["ptB"] = (ps(pb, "ptB0%d" % it, [128, 128], BF16),
                               ps(pb, "ptB1%d" % it, [128, 128], BF16))
                pb.es = sv
                cfg["tabs_d"] = [tabs_d[j] for j in range(4)]
                _, yd_toks = None, None
                ln1_, yd_toks = emit_retention_iter_p2(pb, bufs, cfg, ln1)
            es3 = ExitStack()
            with es3:
                sv = pb.es; pb.es = es3
                bufs["wg_bf"] = sb(pb, "wg_bf%d" % it, [128, DC * D], BF16)
                bufs["wo_bf"] = sb(pb, "wo_bf%d" % it, [128, DC * D], BF16)
                bufs["ytile"] = sb(pb, "ytile%d" % it, [128, D], BF16)
                bufs["gtile"] = sb(pb, "gtile%d" % it, [128, 512], BF16)
                bufs["yfm"] = sb(pb, "yfm%d" % it, [128, D], BF16)
                bufs["ps_s0"] = ps(pb, "psA3%d" % it, [128, 512])
                bufs["ps_s1"] = ps(pb, "psB3%d" % it, [128, 1024])
                bufs["ptB"] = (ps(pb, "ptB30%d" % it, [128, 128], BF16),
                               ps(pb, "ptB31%d" % it, [128, 128], BF16))
                pb.es = sv
                _, cfg["wg_tok"] = emit_weight_load(pb, [wg_d[it, c] for c in range(DC)],
                                                    bufs["wstage"], bufs["wg_bf"], DC, 1024,
                                                    deps=[yd_toks[-1]], eng_dma="sync")
                _, cfg["wo_tok"] = emit_weight_load(pb, [wo_d[it, c] for c in range(DC)],
                                                    bufs["wstage"], bufs["wo_bf"], DC, 1024,
                                                    deps=[], eng_dma="sync")
                x1_toks = emit_retention_p3(pb, bufs, cfg, yd_toks, ln1)
            esKV.close()
            es4 = ExitStack()
            with es4:
                sv = pb.es; pb.es = es4
                if final_add and it == iters - 1:
                    bufs["x0scr"] = sb(pb, "x0scr_%d" % it, [128, D], F32)
                    cfg["x0"] = [x0_d[t] for t in range(n_tc)]
                bufs["xn2t"] = sb(pb, "xn2t_%d" % it, [128, D], BF16)
                bufs["w1_bf"] = sb(pb, "w1_bf%d" % it, [128, DC * 4096], BF16)
                bufs["w2_bf"] = sb(pb, "w2_bf%d" % it, [128, 32 * 1024], BF16)
                bufs["h1c"] = sb(pb, "h1c%d" % it, [128, 128], BF16)
                bufs["ps_s1"] = ps(pb, "psH4%d" % it, [128, 512])
                bufs["ps_bc"] = ps(pb, "psbc4%d" % it, [128, 512])
                bufs["pt"] = (ps(pb, "ptP4a%d" % it, [128, 128], BF16),
                              ps(pb, "ptP4b%d" % it, [128, 128], BF16))
                bufs["ps_o2b"] = ps(pb, "psO4%d" % it, [128, 1024])
                pb.es = sv
                cfg["b1_tok"] = c4
                bufs["b1cols_it"] = bufs["b1cols"]
                _, cfg["w1_tok"] = emit_weight_load(
                    pb, [w1_d[it, c // 4][:, (c % 4) * 1024:(c % 4) * 1024 + 1024]
                         for c in range(DC * 4)],
                    bufs["wstage"], bufs["w1_bf"], DC * 4, 1024,
                    deps=[x1_toks[-1]], eng_dma="gpsimd")
                _, cfg["w2_tok"] = emit_weight_load(
                    pb, [w2_d[it, c] for c in range(32)],
                    bufs["wstage"], bufs["w2_bf"], 32, 1024,
                    deps=[], eng_dma="gpsimd")
                cfg["b1_base"] = it * 32
                bufs["b1cols_view"] = bufs["b1cols"][:, it * 32:(it + 1) * 32]
                bufs["b2row"] = bufs["b2rows"][0:1, it * D:(it + 1) * D]
                zo_toks = emit_retention_p4(pb, bufs, cfg, x1_toks)
            zi_aps = zo_aps
            entry = pb.all_toks()
        pb.emit("sync", lambda e: e.nop(), deps=zo_toks)
        pb.build()
    return nc



_PROG_CACHE = {}

def _prog(name, fn):
    if name not in _PROG_CACHE:
        _PROG_CACHE[name] = fn()
    return _PROG_CACHE[name]


def _halves(x):
    out = []
    for c in range(8):
        b, h = c // 2, c % 2
        out.append(np.ascontiguousarray(x[b, h * 2048:(h + 1) * 2048].reshape(16, 128, 1024)))
    return out


def _unhalves(res, key):
    x = np.empty((4, 4096, 1024), np.float32)
    for c in range(8):
        b, h = c // 2, c % 2
        x[b, h * 2048:(h + 1) * 2048] = np.asarray(res[c][key]).reshape(2048, 1024)
    return x


def kernel(**z):
    import math
    ONES = np.ones((1, 128), BF)
    B = 4
    x = np.asarray(z["L0_attn_a_X"], np.float32)

    H8, HD = 8, 128
    perm = np.concatenate([np.arange(0, HD, 2), np.arange(1, HD, 2)])
    WQ, WK, WV = z["L0_attn_a_WQ"], z["L0_attn_a_WK"], z["L0_attn_a_WV"]
    T4 = 4096
    sv = (np.arange(0, HD, 2) + 0.4 * HD) / (1.4 * HD)
    power = (np.arange(T4, dtype=np.float64) / 512)[:, None]
    scale = sv[None, :] ** power
    inv_freq = 1.0 / 10000.0 ** (np.arange(64, dtype=np.float64) / 64)
    ang = np.arange(T4, dtype=np.float64)[:, None] * inv_freq[None, :]
    sin_, cos_ = np.sin(ang), np.cos(ang)
    tabs = np.stack([np.ascontiguousarray(
        np.concatenate([a.T, a.T], 0).astype(BF))
        for a in (cos_ * scale, sin_ * scale, cos_ / scale, sin_ / scale)])
    gammas = 1.0 - np.exp(np.linspace(np.log(1 / 32), np.log(1 / 512), H8))
    pp = np.arange(128)
    dmask = np.concatenate([np.where(pp[None, :] >= pp[:, None],
                                     g ** (pp[None, :] - pp[:, None]).astype(np.float64), 0.0).astype(np.float32)
                            for g in gammas], 1)
    gcols = np.concatenate([np.stack([g ** (pp + 1) for g in gammas], 1),
                            np.stack([g ** (127 - pp) for g in gammas], 1)], 1).astype(np.float32)
    gbrows = np.concatenate([np.concatenate(
        [z["L0_attn_a_ln1_w"][i], z["L0_attn_a_ln1_b"][i],
         z["L0_attn_a_ln2_w"][i], z["L0_attn_a_ln2_b"][i],
         z["L0_attn_a_gn_w"][i], z["L0_attn_a_gn_b"][i]]) for i in range(2)]
        ).astype(BF)[None, :]
    imr = dict(
        wq=np.ascontiguousarray(WQ[:, :, :, perm].reshape(2, H8, 8, 128, 128)),
        wk=np.ascontiguousarray(WK[:, :, :, perm].reshape(2, H8, 8, 128, 128)),
        wv=np.ascontiguousarray(WV.reshape(2, H8, 8, 128, 128)),
        wg=np.ascontiguousarray(z["L0_attn_a_WG"].reshape(2, 8, 128, D)),
        wo=np.ascontiguousarray(z["L0_attn_a_WO"].reshape(2, 8, 128, D)),
        w1=np.ascontiguousarray(z["L0_attn_a_ffn_w1"].reshape(2, 8, 128, 4096)),
        w2=np.ascontiguousarray(z["L0_attn_a_ffn_w2"].reshape(2, 32, 128, D)),
        gbrows=gbrows,
        b1cols=np.ascontiguousarray(np.concatenate(
            [z["L0_attn_a_ffn_b1"][i].reshape(32, 128).T for i in range(2)], 1)),
        b2rows=z["L0_attn_a_ffn_b2"].ravel().astype(BF)[None, :],
        xpos=tabs, dmask=dmask, gcols=gcols,
        ident=IDENT_BF, onesc=ONES)
    RET = _prog("RET", lambda: build_retention_test(n_tc=32, iters=2,
                                                    outer_ln=True, final_add=True))
    in_maps = []
    for c in range(8):
        b = c // 2
        im = dict(imr)
        im["zin"] = np.ascontiguousarray(x[b].reshape(32, 128, 1024))
        im["x0"] = im["zin"]
        in_maps.append(im)
    res = run_bass_kernel_spmd(RET, in_maps, core_ids=list(range(8)))
    xcur = np.stack([np.asarray(res.results[2 * b]["zout"]).reshape(4096, 1024)
                     for b in range(B)])

    ML0 = _prog("ML0", lambda: build_moe_l0_test(n_tc=16))
    im_m = dict(
        gwT=np.ascontiguousarray(z["L0_ffn_gate_w"].T.reshape(8, 128, 8)
                                 .transpose(1, 0, 2).reshape(128, 64)),
        gu=np.ascontiguousarray(z["L0_ffn_gate_up_proj"].reshape(8, 8, 128, 2048)),
        down=np.ascontiguousarray(z["L0_ffn_down_proj"].reshape(8, 8, 128, 1024)),
        ident=IDENT_BF)

    def run_moe_l0(xc):
        hs = _halves(xc)
        ims = [dict(im_m, xin=hs[c]) for c in range(8)]
        r = run_bass_kernel_spmd(ML0, ims, core_ids=list(range(8)))
        return _unhalves(r.results, "xout")

    ppp, fff = np.arange(128)[:, None], np.arange(512)[None, :]
    cmask = np.concatenate([np.where(ppp + idx * 128 > fff, -1e30, 0).astype(np.float32)
                            for idx in range(4)], 1)

    def attn_im_common(style):
        im = dict(ident=IDENT_BF, onesc=ONES, cmask=cmask)
        if style == "tanh":
            Wq, Wk, Wv, Wo = z["L1_attn_a_Wq"], z["L1_attn_a_Wk"], z["L1_attn_a_Wv"], z["L1_attn_a_Wo"]
            bq, bk, bv, bo = z["L1_attn_a_bq"], z["L1_attn_a_bk"], z["L1_attn_a_bv"], z["L1_attn_a_bo"]
            bqk = np.zeros((64, 32), np.float32)
            for h in range(16):
                bqk[:, 2 * h] = bq[h * 64:(h + 1) * 64]
                bqk[:, 2 * h + 1] = bk[h * 64:(h + 1) * 64]
            im["bqk"] = bqk
            im["bvo"] = np.concatenate([bv, bo]).astype(BF)[None, :]
        else:
            Wq = np.ascontiguousarray(z["L0_attn_b_Wq"].T)
            Wk = np.ascontiguousarray(z["L0_attn_b_Wk"].T)
            Wv = np.ascontiguousarray(z["L0_attn_b_Wv"].T)
            Wo = np.ascontiguousarray(z["L0_attn_b_Wp"].T)
            for n2 in "qkv":
                im["bT" + n2] = np.ascontiguousarray(z["L0_attn_b_B" + n2].T.reshape(8, 128, 8))
                im["aT" + n2] = np.ascontiguousarray(z["L0_attn_b_A" + n2].T)
        im["wq"] = np.ascontiguousarray(Wq.reshape(8, 128, 16, 64).transpose(2, 0, 1, 3))
        im["wk"] = np.ascontiguousarray(Wk.reshape(8, 128, 16, 64).transpose(2, 0, 1, 3))
        im["wv"] = np.ascontiguousarray(Wv.reshape(8, 128, 16, 64).transpose(2, 0, 1, 3))
        im["wo"] = np.ascontiguousarray(Wo.reshape(8, 128, 1024))
        return im

    AT = _prog("AT", lambda: build_attn_test(nq_tc=16, style="tanh"))
    AA = _prog("AA", lambda: build_attn_test(nq_tc=16, style="alibi"))

    def run_attn(prog, xc, style):
        imc = attn_im_common(style)
        zeros_h = np.zeros((16, 128, 1024), np.float32)
        ims = []
        for c in range(8):
            b, h = c // 2, c % 2
            im = dict(imc)
            im["xin"] = np.ascontiguousarray(xc[b, h * 2048:(h + 1) * 2048].reshape(16, 128, 1024))
            im["xprev"] = (np.ascontiguousarray(xc[b, 0:2048].reshape(16, 128, 1024))
                           if h == 1 else zeros_h)
            km = np.zeros((128, 32), np.float32)
            if h == 0:
                km[:, 0:16] = -1e30
            im["kmask"] = km
            base = h * 2048 - 2048
            im["kpos"] = (base + np.arange(32)[None, :] * 128
                          + np.arange(128)[:, None]).astype(np.float32)
            ims.append(im)
        r = run_bass_kernel_spmd(prog, ims, core_ids=list(range(8)))
        return _unhalves(r.results, "xout")

    ML1 = _prog("ML1", lambda: build_moe_l1_test(n_tc=16))
    im_f = dict(
        gwT=np.ascontiguousarray(z["L1_ffn_router_w"].reshape(8, 128, 8)
                                 .transpose(1, 0, 2).reshape(128, 64)),
        ebias=np.ascontiguousarray(np.broadcast_to(z["L1_ffn_expert_bias"], (128, 8))),
        gu=np.ascontiguousarray(np.concatenate([z["L1_ffn_w1"], z["L1_ffn_w3"]], axis=2)
                                .reshape(8, 8, 128, 2048)),
        down=np.ascontiguousarray(z["L1_ffn_w2"].reshape(8, 8, 128, 1024)),
        sgu=np.ascontiguousarray(np.concatenate([z["L1_ffn_sw1"], z["L1_ffn_sw3"]], axis=1)
                                 .reshape(8, 128, 2048)),
        sdown=np.ascontiguousarray(z["L1_ffn_sw2"].reshape(8, 128, 1024)),
        ident=IDENT_BF)

    def run_moe_l1(xc):
        hs = _halves(xc)
        ims = [dict(im_f, xin=hs[c]) for c in range(8)]
        r = run_bass_kernel_spmd(ML1, ims, core_ids=list(range(8)))
        return _unhalves(r.results, "xout")

    MG = _prog("MG", lambda: build_mingru_test(n_tc=16))
    im_g = dict(
        w=np.ascontiguousarray(z["L1_attn_b_W"].reshape(3, 8, 128, 2048)),
        gbrows=np.concatenate([z["L1_attn_b_ln_gamma"].ravel(),
                               z["L1_attn_b_ln_beta"].ravel()]).astype(BF)[None, :],
        bcols=np.ascontiguousarray(z["L1_attn_b_b"].reshape(3, 16, 128)
                                   .transpose(2, 0, 1).reshape(128, 48)),
        ident=IDENT_BF, identf_d=np.eye(128, dtype=np.float32), onesc=ONES)

    def run_mingru(xc):
        ims = []
        for c in range(8):
            b = c // 2
            im = dict(im_g, xin=np.ascontiguousarray(xc[b, 0:2048].reshape(16, 128, 1024)),
                      initc=np.full((128, 24), 0.5, np.float32))
            ims.append(im)
        r1 = run_bass_kernel_spmd(MG, ims, core_ids=list(range(8)))
        ims2 = []
        for c in range(8):
            b = c // 2
            im = dict(im_g, xin=np.ascontiguousarray(xc[b, 2048:4096].reshape(16, 128, 1024)),
                      initc=np.asarray(r1.results[2 * b]["hend"]))
            ims2.append(im)
        r2 = run_bass_kernel_spmd(MG, ims2, core_ids=list(range(8)))
        out = np.empty((4, 4096, 1024), np.float32)
        for b in range(B):
            out[b, 0:2048] = np.asarray(r1.results[2 * b]["xout"]).reshape(2048, 1024)
            out[b, 2048:4096] = np.asarray(r2.results[2 * b]["xout"]).reshape(2048, 1024)
        return out

    xcur = run_moe_l0(xcur)
    xcur = run_attn(AA, xcur, "alibi")
    xcur = run_moe_l0(xcur)
    xcur = run_attn(AT, xcur, "tanh")
    xcur = run_moe_l1(xcur)
    xcur = run_mingru(xcur)
    xcur = run_moe_l1(xcur)
    return xcur.astype(np.float32)



CCK = mybir.CollectiveComputeKind
PAIRS = [[0, 1], [2, 3], [4, 5], [6, 7]]


def emit_allgather(pb, ins_ap, outs_ap, deps=()):
    return pb.emit("gpsimd", lambda e: e.collective_compute(
        "AllGather", ALU.bypass, PAIRS, [ins_ap], [outs_ap]),
        deps=deps, coll=True)


def emit_allreduce(pb, ins_ap, outs_ap, deps=()):
    return pb.emit("gpsimd", lambda e: e.collective_compute(
        "AllReduce", ALU.add, PAIRS, [ins_ap], [outs_ap]),
        deps=deps, coll=True)


def build_fused(nq=16, upto=8):
    nc = new_nc()
    es = ExitStack()
    T = nq * 128
    S_tc = 2 * nq
    S = S_tc * 128
    H8 = 8
    H16, dh = 16, 64

    def din(name, shape, dt_=F32):
        return nc.dram_tensor(name, list(shape), dt_, kind="ExternalInput").ap()

    def dscr(name, shape, dt_=F32):
        return nc.dram_tensor(name, list(shape), dt_).ap()

    xin = din("xin", [S_tc, 128, D])
    ident = din("ident", [128, 128], BF16)
    identf = din("identf_d", [128, 128])
    ones_d = din("onesc", [1, 128], BF16)
    cmask_d = din("cmask", [128, 4 * 512], BF16)
    hflags = din("hflags", [128, 4])
    r = {k: din("r_" + k, s, d) for k, s, d in (
        ("wq", [2, H8, DC, 128, 128], F32), ("wk", [2, H8, DC, 128, 128], F32),
        ("wv", [2, H8, DC, 128, 128], F32),
        ("wg", [2, DC, 128, D], F32), ("wo", [2, DC, 128, D], F32),
        ("w1", [2, DC, 128, 4096], F32), ("w2", [2, 32, 128, D], F32),
        ("gbrows", [1, 2 * 6 * D], BF16), ("b1cols", [128, 2 * 32], F32),
        ("b2rows", [1, 2 * D], BF16), ("xpos", [4, 128, S], BF16),
        ("dmask", [128, H8 * 128], F32), ("gcols", [128, 2 * H8], F32))}
    m0 = {k: din("m0_" + k, s, d) for k, s, d in (
        ("gwT", [128, DC * 8], F32), ("gu", [8, 2 * DC, 128, 1024], F32),
        ("down", [8, DC, 128, 1024], F32))}
    aa = {k: din("aa_" + k, s, d) for k, s, d in (
        ("wq", [H16, DC, 128, dh], F32), ("wk", [H16, DC, 128, dh], F32),
        ("wv", [H16, DC, 128, dh], F32), ("wo", [DC, 128, D], F32),
        ("bTq", [DC, 128, 8], F32), ("bTk", [DC, 128, 8], F32), ("bTv", [DC, 128, 8], F32),
        ("aTq", [8, D], F32), ("aTk", [8, D], F32), ("aTv", [8, D], F32),
        ("kmask", [128, S_tc], F32), ("kpos", [128, S_tc], F32),
        ("coef", [128, 2 * S_tc], F32))}
    at = {k: din("at_" + k, s, d) for k, s, d in (
        ("wq", [H16, DC, 128, dh], F32), ("wk", [H16, DC, 128, dh], F32),
        ("wv", [H16, DC, 128, dh], F32), ("wo", [DC, 128, D], F32),
        ("bqk", [64, 2 * H16], F32), ("bvo", [1, 2 * D], BF16),
        ("kmask", [128, S_tc], F32))}
    m1 = {k: din("m1_" + k, s, d) for k, s, d in (
        ("gwT", [128, DC * 8], F32), ("gu", [8, 2 * DC, 128, 1024], F32),
        ("down", [8, DC, 128, 1024], F32), ("sgu", [2 * DC, 128, 1024], F32),
        ("sdown", [DC, 128, 1024], F32), ("ebias", [128, 8], F32))}
    g = {k: din("g_" + k, s, d) for k, s, d in (
        ("w", [3, DC, 128, 2048], F32), ("gbrows", [1, 6 * D], BF16),
        ("bcols", [128, 48], F32))}
    r_z0 = dscr("r_z0", [S_tc, 128, D])
    r_z1 = dscr("r_z1", [S_tc, 128, D])
    r_zf = dscr("r_zf", [S_tc, 128, D])
    r_yd = dscr("r_yd", [S_tc, 128, D], BF16)
    r_x1 = dscr("r_x1", [S_tc, 128, D])
    ag_own = dscr("ag_own", [T, D])
    ag_full = dscr("ag_full", [S, D])
    kvsrc = dscr("kvsrc", [S_tc, 128, D])
    p_sp = dscr("p_sp", [DC, 128, T])
    xsp = dscr("xsp", [nq, 128, D])
    gcin = dscr("gcin", [3, 128, 8])
    gcout = dscr("gcout", [3, 128, 8])
    xout = nc.dram_tensor("xout", [S_tc, 128, D], F32, kind="ExternalOutput").ap()

    with es:
        pb = PB(nc, es)
        bP = dict(
            stats=sb(pb, "stats", [128, 4 * S_tc], F32),
            tmp_bf=sb(pb, "tmp_bf", [128, D], BF16),
            identb=sb(pb, "identb", [128, 128], BF16),
            identfb=sb(pb, "identfb", [128, 128], F32),
            onescol=sb(pb, "onescol", [1, 128], BF16),
            zt=sb(pb, "zt", [128, D], F32),
            cstats=sb(pb, "cstats", [128, 8], F32),
            wstage=sb(pb, "wstage", [128, 1024], F32),
            rcol=sb(pb, "rcol", [128, 8], F32),
            fcols=sb(pb, "fcols", [128, 4], F32),
        )
        cst = [pb.dma("sync", bP["identb"][:, :], ident),
               pb.dma("sync", bP["identfb"][:, :], identf),
               pb.dma("sync", bP["onescol"][0:1, :], ones_d),
               pb.dma("sync", bP["fcols"][:, :], hflags)]
        c_tok = cst[-1]
        gammas = 1.0 - np.exp(np.linspace(np.log(1.0 / 32), np.log(1.0 / 512), H8))
        ret_toks = None
        pb.tail[("fused_entry",)] = tuple(cst)
        esR = ExitStack()
        with esR:
            svR = pb.es; pb.es = esR
            bR = dict(bP)
            bR["gbrows"] = sb(pb, "r_gbrows_b", [1, 6 * D], BF16)
            bR["b1cols"] = sb(pb, "r_b1cols_b", [128, 2 * 32], F32)
            bR["b2rows"] = sb(pb, "r_b2rows_b", [1, 2 * D], BF16)
            bR["gnw_bc"] = sb(pb, "r_gnw_bc", [128, H8 * 128], F32)
            bR["gnb_bc"] = sb(pb, "r_gnb_bc", [128, H8 * 128], F32)
            bR["gcol"] = sb(pb, "r_gcol", [128, H8], F32)
            bR["ghcol"] = sb(pb, "r_ghcol", [128, H8], F32)
            bR["dmask"] = sb(pb, "r_dmask_b", [128, H8 * 128], F32)
            bR["g_bc"] = sb(pb, "r_g_bc", [128, D], F32)
            bR["b_bc"] = sb(pb, "r_b_bc", [128, D], F32)
            bR["dram_scr"] = sb(pb, "r_dram_scr", [128, 2 * D], F32)
            bR["sqscr"] = sb(pb, "r_sqscr", [128, D], F32)
            bR["lnscr"] = sb(pb, "r_lnscr", [128, D], F32)
            pb.es = svR
            rc = [pb.dma("sync", bR["gbrows"][0:1, :], r["gbrows"][0:1, 0:6 * D]),
                  pb.dma("sync", bR["b1cols"][:, :], r["b1cols"]),
                  pb.dma("sync", bR["b2rows"][0:1, :], r["b2rows"]),
                  pb.dma("sync", bR["gcol"][:, :], r["gcols"][:, 0:H8]),
                  pb.dma("sync", bR["ghcol"][:, :], r["gcols"][:, H8:2 * H8]),
                  pb.dma("sync", bR["dmask"][:, :], r["dmask"])]
            bR["gb_tok"] = rc[0]
            entry = cst + rc
            es0 = ExitStack()
            with es0:
                sv0 = pb.es; pb.es = es0
                bR["lnt0"] = sb(pb, "lnt0", [128, D], F32)
                bR["lnt1"] = sb(pb, "lnt1", [128, D], F32)
                pb.es = sv0
                ln0 = emit_ln_stream(pb, bR, [xin[t] for t in range(S_tc)],
                                     [r_z0[t] for t in range(S_tc)], S_tc, tok0=entry)
            del bR["lnt0"], bR["lnt1"]
            entry = entry + [ln0[-1]]
            zi_aps = [r_z0[t] for t in range(S_tc)]
            for it in range(2):
                if it > 0:
                    c3b = pb.dma("sync", bR["gbrows"][0:1, :],
                                 r["gbrows"][0:1, 6 * D:12 * D], deps=entry)
                    bR["gb_tok"] = c3b
                zo_aps = [r_zf[t] for t in range(S_tc)] if it == 1 else [r_z1[t] for t in range(S_tc)]
                cfg = dict(
                    n_tc=S_tc, iter=it, zi=zi_aps, zo=zo_aps,
                    yd=[r_yd[t] for t in range(S_tc)],
                    x1=[r_x1[t] for t in range(S_tc)],
                    wq=[[r["wq"][it, h, c] for c in range(DC)] for h in range(H8)],
                    wk=[[r["wk"][it, h, c] for c in range(DC)] for h in range(H8)],
                    wv=[[r["wv"][it, h, c] for c in range(DC)] for h in range(H8)],
                    ln1g_off=0, ln1b_off=D, ln2g_off=2 * D, ln2b_off=3 * D,
                    gng_off=4 * D, gnb_off=5 * D,
                    gC=[float(gm) ** 128 for gm in gammas],
                    dk_tok=rc[-1], b1_tok=rc[1], b2_tok=rc[2],
                    entry=list(entry),
                    tabs_d=[r["xpos"][j] for j in range(4)],
                )
                es1 = ExitStack()
                with es1:
                    sv = pb.es; pb.es = es1
                    bR["pt"] = (ps(pb, "ptP1a%d" % it, [128, 128], BF16),
                                ps(pb, "ptP1b%d" % it, [128, 128], BF16))
                    bR["ps_bc"] = ps(pb, "psbc1%d" % it, [128, 512])
                    pb.es = sv
                    gn_toks = []
                    prevgn = list(pb.tail.get(("gb_bc",), ())) + [pb.barrier_tok("vector")]
                    for h in range(H8):
                        for which, dstt in ((cfg["gng_off"], bR["gnw_bc"]),
                                            (cfg["gnb_off"], bR["gnb_bc"])):
                            psbcl = bR["ps_bc"]
                            gbr_l = bR["gbrows"]
                            tm1 = pb.emit("tensor", lambda e, which=which, h=h, psbcl=psbcl, gbr_l=gbr_l: e.matmul(
                                psbcl[:, 0:128], bP["onescol"][0:1, 0:128],
                                gbr_l[0:1, which + h * 128: which + h * 128 + 128],
                                start=True, stop=True), deps=[bR["gb_tok"]] + prevgn)
                            tcp = pb.emit("scalar", lambda e, dstt=dstt, h=h, psbcl=psbcl: e.copy(
                                dstt[:, h * 128:(h + 1) * 128], psbcl[:, 0:128]),
                                deps=[tm1])
                            prevgn = [tcp]
                            gn_toks.append(tcp)
                    pb.tail[("gb_bc",)] = (gn_toks[-1],)
                    cfg["gn_tok"] = gn_toks[-1]
                    gb = emit_gb_bcast(pb, bR, cfg["ln1g_off"], cfg["ln1b_off"])
                    esKV = ExitStack()
                    svv = pb.es; pb.es = esKV
                    xkv = sb(pb, "r_xkv%d" % it, [128, DC * S], BF16)
                    pb.es = svv
                    bR["xkv_fm"] = xkv
                    ln1 = emit_ln_fm(pb, None, xkv, bR["stats"], bR["tmp_bf"][:, 0:D],
                                     bR["tmp_bf"], bR["identb"], bR["pt"], n_tc=S_tc,
                                     tok_x=list(gb) + list(entry),
                                     affine_wb=(bR["g_bc"], bR["b_bc"], bR["lnscr"]),
                                     src_dram=zi_aps, dram_scr=bR["dram_scr"], dst_stride=S)
                es2 = ExitStack()
                with es2:
                    sv = pb.es; pb.es = es2
                    for nm, shp, dt_ in (("q_fm", [128, S], BF16), ("k_fm", [128, S], BF16),
                                         ("qrot", [128, S], BF16), ("krot", [128, S], BF16),
                                         ("khat", [128, S], BF16), ("v_tm", [128, S], BF16),
                                         ("tscr1", [128, S], BF16),
                                         ("whb", [128, 3 * 1024], BF16),
                                         ("abuf", [128, 128], BF16),
                                         ("obuf", [128, 128], F32),
                                         ("obuf2", [128, 128], BF16),
                                         ("state", [128, 128], F32),
                                         ("state_bf", [128, 128], BF16),
                                         ("tab", [128, 2 * S], BF16)):
                        bR[nm] = sb(pb, "r_" + nm + "_%d" % it, shp, dt_)
                    bR["ps_s0"] = ps(pb, "psA%d" % it, [128, 512])
                    bR["ps_s1"] = ps(pb, "psB%d" % it, [128, 512])
                    bR["ps_o1"] = ps(pb, "psO1%d" % it, [128, 128])
                    bR["ps_o2"] = ps(pb, "psO2%d" % it, [128, 128])
                    bR["ps_su"] = ps(pb, "psSU%d" % it, [128, 128])
                    bR["ptB"] = (ps(pb, "ptB0%d" % it, [128, 128], BF16),
                                 ps(pb, "ptB1%d" % it, [128, 128], BF16))
                    pb.es = sv
                    _, yd_toks = emit_retention_iter_p2(pb, bR, cfg, ln1)
                es3 = ExitStack()
                with es3:
                    sv = pb.es; pb.es = es3
                    bR["wg_bf"] = sb(pb, "r_wg_bf%d" % it, [128, DC * D], BF16)
                    bR["wo_bf"] = sb(pb, "r_wo_bf%d" % it, [128, DC * D], BF16)
                    bR["ytile"] = sb(pb, "r_ytile%d" % it, [128, D], BF16)
                    bR["gtile"] = sb(pb, "r_gtile%d" % it, [128, 512], BF16)
                    bR["yfm"] = sb(pb, "r_yfm%d" % it, [128, D], BF16)
                    bR["ps_s0"] = ps(pb, "psA3%d" % it, [128, 512])
                    bR["ps_s1"] = ps(pb, "psB3%d" % it, [128, 1024])
                    bR["ptB"] = (ps(pb, "ptB30%d" % it, [128, 128], BF16),
                                 ps(pb, "ptB31%d" % it, [128, 128], BF16))
                    pb.es = sv
                    _, cfg["wg_tok"] = emit_weight_load(pb, [r["wg"][it, c] for c in range(DC)],
                                                        bR["wstage"], bR["wg_bf"], DC, 1024,
                                                        deps=[yd_toks[-1]], eng_dma="sync")
                    _, cfg["wo_tok"] = emit_weight_load(pb, [r["wo"][it, c] for c in range(DC)],
                                                        bR["wstage"], bR["wo_bf"], DC, 1024,
                                                        deps=[], eng_dma="sync")
                    x1_toks = emit_retention_p3(pb, bR, cfg, yd_toks, ln1)
                esKV.close()
                es4 = ExitStack()
                with es4:
                    sv = pb.es; pb.es = es4
                    if it == 1:
                        bR["x0scr"] = sb(pb, "r_x0scr", [128, D], F32)
                        cfg["x0"] = [xin[t] for t in range(S_tc)]
                    bR["xn2t"] = sb(pb, "r_xn2t_%d" % it, [128, D], BF16)
                    bR["w1_bf"] = sb(pb, "r_w1_bf%d" % it, [128, DC * 4096], BF16)
                    bR["w2_bf"] = sb(pb, "r_w2_bf%d" % it, [128, 32 * 1024], BF16)
                    bR["h1c"] = sb(pb, "r_h1c%d" % it, [128, 128], BF16)
                    bR["ps_s1"] = ps(pb, "psH4%d" % it, [128, 512])
                    bR["ps_bc"] = ps(pb, "psbc4%d" % it, [128, 512])
                    bR["pt"] = (ps(pb, "ptP4a%d" % it, [128, 128], BF16),
                                ps(pb, "ptP4b%d" % it, [128, 128], BF16))
                    bR["ps_o2b"] = ps(pb, "psO4%d" % it, [128, 1024])
                    pb.es = sv
                    _, cfg["w1_tok"] = emit_weight_load(
                        pb, [r["w1"][it, c // 4][:, (c % 4) * 1024:(c % 4) * 1024 + 1024]
                             for c in range(DC * 4)],
                        bR["wstage"], bR["w1_bf"], DC * 4, 1024,
                        deps=[x1_toks[-1]], eng_dma="gpsimd")
                    _, cfg["w2_tok"] = emit_weight_load(
                        pb, [r["w2"][it, c] for c in range(32)],
                        bR["wstage"], bR["w2_bf"], 32, 1024,
                        deps=[], eng_dma="gpsimd")
                    bR["b1cols_view"] = bR["b1cols"][:, it * 32:(it + 1) * 32]
                    bR["b2row"] = bR["b2rows"][0:1, it * D:(it + 1) * D]
                    zo_toks = emit_retention_p4(pb, bR, cfg, x1_toks)
                zi_aps = zo_aps
                entry = pb.all_toks()
            ret_toks = zo_toks
        ctxd = dict(T=T, S=S, nq=nq, S_tc=S_tc, upto=upto,
                    xin=xin, r_zf=r_zf, r_z1=r_z1, r_x1=r_x1,
                    xout=xout, m0=m0, aa=aa, at=at, m1=m1, g=g,
                    cmask_d=cmask_d, ret_toks=ret_toks)
        _fused_rest(pb, bP, ctxd)
        pb.build()
    return nc


def _fused_rest(pb, bP, C):
    nq = C["nq"]; T = C["T"]; S = C["S"]; S_tc = C["S_tc"]
    c_tok = pb.tail[("fused_entry",)][-1]
    xa = [C["r_zf"][t] for t in range(S_tc)]
    xb = [C["r_z1"][t] for t in range(S_tc)]
    xg_aps = None

    def fused_moe_s(which, xi, xo, in_toks):
        toks = list(in_toks)
        TGT = 16 if S_tc % 16 == 0 else 8
        for tg in range(S_tc // TGT):
            esM = ExitStack()
            with esM:
                sv = pb.es; pb.es = esM
                b = dict(bP)
                b["x"] = sb(pb, which + "x%d" % tg, [128, TGT * D], F32)
                b["xn_fm"] = sb(pb, which + "xn%d" % tg, [128, DC * TGT * 128], BF16)
                b["gwT"] = sb(pb, which + "gwT%d" % tg, [128, DC * 8], BF16)
                st1 = sb(pb, which + "st1%d" % tg, [128, 1024], F32)
                st2 = sb(pb, which + "st2%d" % tg, [128, 1024], F32)
                b["stages"] = [st1, st2]
                b["stages2"] = b["stages"]
                gu0 = sb(pb, which + "gu0%d" % tg, [128, DC * 2048], BF16)
                gu1 = sb(pb, which + "gu1%d" % tg, [128, DC * 2048], BF16)
                b["gu_bf0"] = gu0
                b["gu_bf1"] = gu1
                b["dn_bf0"] = sb(pb, which + "dn%d" % tg, [128, DC * 1024], BF16)
                b["hbuf0"] = sb(pb, which + "h0%d" % tg, [128, 4096], BF16)
                b["hbuf1"] = b["hbuf0"]
                b["hf_major"] = True
                b["sg"] = sb(pb, which + "sg%d" % tg, [128, 512], F32)
                b["routed_tm"] = sb(pb, which + "ro%d" % tg, [128, TGT * 8], F32)
                b["rscr"] = sb(pb, which + "rs%d" % tg, [128, 64], F32)
                b["bias_bc"] = sb(pb, which + "eb%d" % tg, [128, 8], F32)
                b["ps_r"] = ps(pb, which + "psr%d" % tg, [128, 8])
                b["ps_g"] = ps(pb, which + "psg%d" % tg, [128, 512])
                b["ps_u"] = ps(pb, which + "psu%d" % tg, [128, 512])
                b["ps_d"] = ps(pb, which + "psd%d" % tg, [128, 1024])
                b["pt"] = (ps(pb, which + "pt0%d" % tg, [128, 128], BF16),
                           ps(pb, which + "pt1%d" % tg, [128, 128], BF16))
                pb.es = sv
                entry = pb.all_toks()
                l1 = which.startswith("f")
                md = C["m1"] if l1 else C["m0"]
                x_toks = []
                prevd = ()
                for i in range(TGT):
                    t = pb.dma("gpsimd", b["x"][:, i * D:(i + 1) * D], xi[tg * TGT + i],
                               deps=toks[-1:] + list(prevd) + list(entry))
                    prevd = (t,)
                    x_toks.append(t)
                tgw = pb.dma("sync", b["rscr"][:, 0:64], md["gwT"], deps=entry)
                tgwb = pb.emit("vector", lambda e, b=b: e.tensor_copy(
                    b["gwT"][:, :], b["rscr"][:, 0:64]), deps=[tgw], strict=True)
                extra = [tgwb]
                if l1:
                    tbias = pb.dma("sync", b["bias_bc"][:, :], md["ebias"], deps=entry)
                    b["bias_tok"] = tbias
                    extra.append(tbias)
                fm_last = emit_ln_fm(pb, b["x"], b["xn_fm"], b["stats"],
                                     b["tmp_bf"][:, 0:D], b["tmp_bf"], b["identb"],
                                     b["pt"], n_tc=TGT,
                                     tok_x=x_toks + extra + list(entry))
                gu_d, dn_d = md["gu"], md["down"]
                wap = {
                    "gu": [[gu_d[e_i, cc] for cc in range(DC * 2)] for e_i in range(8)],
                    "down": [[dn_d[e_i, cc] for cc in range(DC)] for e_i in range(8)],
                }
                if l1:
                    wap["sgu"] = [md["sgu"][cc] for cc in range(DC * 2)]
                    wap["sdown"] = [md["sdown"][cc] for cc in range(DC)]
                mtoks = emit_moe_l0(pb, b, wap, n_tc=TGT, fm_toks=[fm_last],
                                    renorm=not l1, shared=l1, l1=l1)
                st = []
                prevd = ()
                for i in range(TGT):
                    t = pb.dma("gpsimd", xo[tg * TGT + i], b["x"][:, i * D:(i + 1) * D],
                               deps=mtoks + list(prevd))
                    prevd = (t,)
                    st.append(t)
                toks = st
        return toks

    def fused_attn_s(which, xi, xo, in_toks):
        style = "alibi" if which == "aa" else "tanh"
        ad = C[which]
        tanh = style == "tanh"
        lora = style == "alibi"
        KVT = S
        esA = ExitStack()
        with esA:
            sv = pb.es; pb.es = esA
            b = dict(bP)
            kposb = sb(pb, which + "kpos", [128, S_tc], F32)
            kmaskb = sb(pb, which + "kmask", [128, S_tc], F32)
            cmaskb = sb(pb, which + "cmask", [128, 4 * 512], BF16)
            bqk = sb(pb, which + "bqk", [64, 2 * 16], F32)
            bvo = sb(pb, which + "bvo", [1, D], BF16)
            b.update(kpos=kposb, kmask=kmaskb, cmask=cmaskb)
            pb.es = sv
            esKV2 = ExitStack()
            svk = pb.es; pb.es = esKV2
            xkv_fm = sb(pb, which + "xkv", [128, DC * KVT], BF16)
            pb.es = svk
            b["xkv_fm"] = xkv_fm
            entry = pb.all_toks()
            ctoks = [pb.dma("sync", kmaskb[:, :],
                            ad["kmask"] if lora else C["at"]["kmask"], deps=entry),
                     pb.dma("sync", cmaskb[:, :], C["cmask_d"], deps=entry)]
            if lora:
                ctoks.append(pb.dma("sync", kposb[:, :], ad["kpos"], deps=entry))
            if tanh:
                ctoks += [pb.dma("sync", bqk[0:64, :], ad["bqk"], deps=entry),
                          pb.dma("sync", bvo[0:1, :], ad["bvo"][0:1, 0:D], deps=entry)]
                b["bqk"] = bqk
                b["bqk_tok"] = ctoks[-2]
                b["bv_rows"] = bvo[0:1, 0:D]
            esPT = ExitStack()
            svp = pb.es; pb.es = esPT
            b["pt"] = (ps(pb, which + "ptA0", [128, 128], BF16),
                       ps(pb, which + "ptA1", [128, 128], BF16))
            b["dram_scr"] = sb(pb, which + "dscr", [128, 2 * D], F32)
            pb.es = svp
            ln1 = emit_ln_fm(pb, None, xkv_fm, b["stats"], b["tmp_bf"][:, 0:D],
                             b["tmp_bf"], b["identb"], b["pt"], n_tc=S_tc,
                             tok_x=list(in_toks) + list(entry),
                             dst_off=0, dst_stride=KVT,
                             src_dram=xi, dram_scr=b["dram_scr"], dma_eng="gpsimd")
            esPT.close()
            esB = ExitStack()
            svx = pb.es; pb.es = esA
            ctx_tm = sb(pb, which + "ctx", [128, S_tc * 1024], BF16, side="right")
            pb.es = svx
            with esB:
                svb = pb.es; pb.es = esB
                qh_ = sb(pb, which + "qh0", [64, KVT], BF16)
                kh_ = sb(pb, which + "kh0", [64, KVT], BF16)
                vh_ = sb(pb, which + "vh0", [128, S_tc * 65], BF16)
                wh_ = sb(pb, which + "wh0", [128, 3 * 512], BF16)
                bias_ = sb(pb, which + "bias0", [128, S_tc], F32)
                b2 = dict(
                    qh0=qh_, qh1=qh_, kh0=kh_, kh1=kh_, vh0=vh_, vh1=vh_,
                    wh0=wh_, wh1=wh_,
                    pb0=sb(pb, which + "pb0", [128, 512], BF16),
                    pb1=sb(pb, which + "pb1", [128, 512], BF16),
                    pb2=sb(pb, which + "pb2", [128, 512], BF16),
                    bias0=bias_, bias1=bias_,
                )
                if lora:
                    for n2 in ("q", "k", "v"):
                        b2["t1_" + n2] = sb(pb, which + "t1" + n2, [8, KVT], BF16)
                        b2["bT_" + n2] = sb(pb, which + "bT" + n2, [128, 64], BF16)
                        b2["aT_" + n2] = sb(pb, which + "aT" + n2, [8, D], BF16)
                b2["ps_s0"] = ps(pb, which + "pss0", [128, 512])
                b2["ps_s1"] = ps(pb, which + "pss1", [128, 512])
                b2["ps_s2"] = ps(pb, which + "pss2", [128, 512])
                b2["ps_o"] = [ps(pb, which + "pso%d" % i, [128, 65]) for i in range(4)]
                b2["ps_q"] = ps(pb, which + "psq", [128, 512])
                pb.es = svb
                b.update(b2)
                b["ctx_tm"] = ctx_tm
                b["xkv_tok"] = ln1
                b["const_tok"] = ctoks[-1]
                b["entry_tok"] = pb.all_toks()
                lw = []
                if lora:
                    for n2 in ("q", "k", "v"):
                        tb1 = emit_weight_load_off(pb, [ad["bT" + n2][cc] for cc in range(DC)],
                                                   b["wstage"], b["bT_" + n2], 8, 0,
                                                   deps=list(entry), eng_dma="sync")
                        ta1 = pb.dma("sync", b["wstage"][0:8, 0:D], ad["aT" + n2], deps=[tb1])
                        ta2 = pb.emit("scalar", lambda e, n2=n2, b=b: e.copy(
                            b["aT_" + n2][0:8, 0:D], b["wstage"][0:8, 0:D]), deps=[ta1])
                        pb.tail[("stage", b["wstage"].name)] = ta2
                        lw.append(ta2)
                b["lw_tok"] = tuple(lw)
                slopes = ([2.0 ** (-0.5 - 0.5 * i) for i in range(16)] if lora else None)
                cfg = dict(nq_tc=S_tc, heads=16, dh=64, tanh=tanh, alibi=lora, lora=lora,
                           prefix=False, single_hbuf=True, slopes=slopes,
                           wq=[[ad["wq"][h, cc] for cc in range(DC)] for h in range(16)],
                           wk=[[ad["wk"][h, cc] for cc in range(DC)] for h in range(16)],
                           wv=[[ad["wv"][h, cc] for cc in range(DC)] for h in range(16)])
                attn_toks = emit_attn(pb, b, cfg)
            esKV2.close()
            esC = ExitStack()
            with esC:
                svc = pb.es; pb.es = esC
                b["pt"] = (ps(pb, which + "ptC0", [128, 128], BF16),
                           ps(pb, which + "ptC1", [128, 128], BF16))
                b["ps_q"] = ps(pb, which + "psqC", [128, 512])
                ctx_fm = sb(pb, which + "ctxfm", [128, DC * KVT], BF16)
                wo_bf = sb(pb, which + "wo", [128, DC * 1024], BF16)
                projscr = sb(pb, which + "pscr", [128, 1024], F32)
                boC = sb(pb, which + "boC", [1, D], BF16)
                pb.es = svc
                entryC = pb.all_toks()
                b["projscr"] = projscr
                tbo = (pb.dma("sync", boC[0:1, :], ad["bvo"][0:1, D:2 * D],
                              deps=list(entryC)) if tanh else None)
                _, wo_last = emit_weight_load(pb, [ad["wo"][cc] for cc in range(DC)],
                                              b["wstage"], wo_bf, DC, 1024,
                                              deps=list(entryC), eng_dma="sync")
                fm_toks = emit_tm_to_fm(pb, ctx_tm, ctx_fm, b["identb"], b["pt"],
                                        n_tc=S_tc, deps_per_tile=None, src_cols=1024,
                                        dst_cols=KVT,
                                        extra_deps=[attn_toks[-1]] + list(entryC))
                pr = emit_proj_residual(
                    pb, b, ctx_fm, wo_bf, S_tc, bP["zt"],
                    act=(AF.Tanh if tanh else None),
                    bias_row=(boC[0:1, 0:D] if tanh else None),
                    src_tok=[fm_toks[-1]], w_tok=[wo_last],
                    x_tok=None, x_dram=xi, xo_dram=xo)
        return pr[-1:]

    def fused_mingru_s(xi, xo, in_toks):
        L = 3
        mg = [C["r_x1"][t] for t in range(S_tc)]
        esG = ExitStack()
        with esG:
            sv = pb.es; pb.es = esG
            b = dict(bP)
            b["lnt0"] = sb(pb, "g_lnt0", [128, D], F32)
            b["lnt1"] = sb(pb, "g_lnt1", [128, D], F32)
            b["gbrows"] = sb(pb, "g_gbrows_b", [1, 6 * D], BF16)
            b["g_bc"] = sb(pb, "g_g_bc", [128, D], F32)
            b["b_bc"] = sb(pb, "g_b_bc", [128, D], F32)
            b["dram_scr"] = sb(pb, "g_dram_scr", [128, 2 * D], F32)
            b["sqscr"] = sb(pb, "g_sqscr", [128, D], F32)
            b["lnscr"] = sb(pb, "g_lnscr", [128, D], F32)
            b["bcols"] = sb(pb, "g_bcols_b", [128, 48], F32)
            b["nbcols"] = sb(pb, "g_nbcols", [128, 48], F32)
            b["xn_fm"] = sb(pb, "g_xn_fm", [128, DC * S // 2], BF16)
            b["h_all"] = sb(pb, "g_h_all", [128, DC * S // 2], BF16)
            b["hmid"] = sb(pb, "g_hmid", [128, 8], F32)
            b["w_bf"] = sb(pb, "g_w_bf", [128, 2 * DC * 128], BF16)
            b["stage"] = sb(pb, "g_stage", [128, 512], F32)
            b["cbuf"] = sb(pb, "g_cbuf", [128, S // 2], F32)
            b["ubuf"] = sb(pb, "g_ubuf", [128, S // 2], F32)
            b["zscr"] = sb(pb, "g_zscr", [128, 512], F32)
            b["r1scr"] = sb(pb, "g_r1", [128, 512], F32)
            b["s1scr"] = sb(pb, "g_s1", [128, 512], F32)
            b["ps_g"] = ps(pb, "g_psg", [128, 512])
            b["ps_u"] = ps(pb, "g_psu", [128, 512])
            b["ps_bc"] = ps(pb, "g_psbc", [128, 512])
            b["pt"] = (ps(pb, "g_pt0", [128, 128], BF16), ps(pb, "g_pt1", [128, 128], BF16))
            pb.es = sv
            entry = pb.all_toks()
            gc1 = pb.dma("sync", b["gbrows"][0:1, :], C["g"]["gbrows"], deps=entry)
            gc2 = pb.dma("sync", b["bcols"][:, :], C["g"]["bcols"], deps=entry)
            tnb = pb.emit("vector", lambda e, b=b: e.tensor_scalar(
                b["nbcols"][:, :], b["bcols"][:, :], -1.0, None, ALU.mult, ALU.bypass),
                deps=[gc2], strict=True)
            b["gb_tok"] = gc1
            ln0 = emit_ln_stream(pb, b, xi, mg, S_tc, tok0=list(in_toks) + list(entry))
            cur = [ln0[-1]]
            HS_tc = S_tc // 2
            HS = S // 2
            for l in range(L):
                gb = emit_gb_bcast(pb, b, (0 + l) * D, (L + l) * D)
                for th in range(2):
                    tbase = th * HS_tc
                    lnt = emit_ln_fm(pb, None, b["xn_fm"], b["stats"], b["sqscr"][:, 0:D],
                                     b["tmp_bf"], b["identb"], b["pt"], n_tc=HS_tc,
                                     tok_x=cur + list(gb), dst_stride=HS,
                                     affine_wb=(b["g_bc"], b["b_bc"], b["lnscr"]),
                                     src_dram=[mg[tbase + t] for t in range(HS_tc)],
                                     dram_scr=b["dram_scr"], dma_eng="gpsimd")
                    scan_toks = []
                    wl_prev = ()
                    for cc in range(DC):
                        wch = [C["g"]["w"][l, dc][:, cc * 128:cc * 128 + 128] for dc in range(DC)]
                        wch += [C["g"]["w"][l, dc][:, 1024 + cc * 128:1024 + cc * 128 + 128]
                                for dc in range(DC)]
                        wl_tok = emit_weight_load_off(pb, wch, b["stage"], b["w_bf"], 128, 0,
                                                      deps=list(wl_prev) or cur, eng_dma="sync")
                        cu_toks = []
                        mmg = mmh = None
                        for tt in range(HS // 512):
                            for dc in range(DC):
                                dep = [wl_tok, lnt] + (list(pb.tail.get(("mgs_pg",), ())) if dc == 0 else [])
                                mmg = pb.emit("tensor", lambda e, cc=cc, tt=tt, dc=dc, b=b: e.matmul(
                                    b["ps_g"][:, 0:512],
                                    b["w_bf"][:, dc * 128: dc * 128 + 128],
                                    b["xn_fm"][:, dc * HS + tt * 512: dc * HS + tt * 512 + 512],
                                    start=(dc == 0), stop=(dc == DC - 1)), deps=dep)
                            for dc in range(DC):
                                dep = [wl_tok, lnt] + (list(pb.tail.get(("mgs_ph",), ())) if dc == 0 else [])
                                mmh = pb.emit("tensor", lambda e, cc=cc, tt=tt, dc=dc, b=b: e.matmul(
                                    b["ps_u"][:, 0:512],
                                    b["w_bf"][:, DC * 128 + dc * 128: DC * 128 + dc * 128 + 128],
                                    b["xn_fm"][:, dc * HS + tt * 512: dc * HS + tt * 512 + 512],
                                    start=(dc == 0), stop=(dc == DC - 1)), deps=dep)
                            bcol = b["bcols"][:, l * 16 + cc: l * 16 + cc + 1]
                            nbcol = b["nbcols"][:, l * 16 + cc: l * 16 + cc + 1]
                            bcol2 = b["bcols"][:, l * 16 + 8 + cc: l * 16 + 8 + cc + 1]
                            tc1 = pb.emit("scalar", lambda e, tt=tt, nbcol=nbcol, b=b: e.activation(
                                b["cbuf"][:, tt * 512:tt * 512 + 512], b["ps_g"][:, 0:512],
                                AF.Sigmoid, bias=nbcol, scale=-1.0),
                                deps=[mmg, tnb] + list(pb.tail.get(("mgs_cu",), ()))
                                     + list(pb.tail.get(("mgs_scan",), ())), strict=True)
                            tz = pb.emit("scalar", lambda e, tt=tt, bcol=bcol, b=b: e.activation(
                                b["zscr"][:, 0:512], b["ps_g"][:, 0:512], AF.Sigmoid,
                                bias=bcol, scale=1.0), deps=[tc1, gc2], strict=True)
                            pb.tail[("mgs_pg",)] = (tz,)
                            tr1 = pb.emit("scalar", lambda e, bcol2=bcol2, b=b: e.activation(
                                b["r1scr"][:, 0:512], b["ps_u"][:, 0:512], AF.Relu,
                                bias=bcol2, scale=1.0), deps=[mmh, gc2, tz], strict=True)
                            ts1 = pb.emit("scalar", lambda e, bcol2=bcol2, b=b: e.activation(
                                b["s1scr"][:, 0:512], b["ps_u"][:, 0:512], AF.Sigmoid,
                                bias=bcol2, scale=1.0), deps=[tr1], strict=True)
                            pb.tail[("mgs_ph",)] = (ts1,)
                            tm1_ = pb.emit("vector", lambda e, b=b: e.tensor_scalar_min(
                                b["s1scr"][:, 0:512], b["s1scr"][:, 0:512], 0.5),
                                deps=[ts1], strict=True)
                            tg_ = pb.emit("vector", lambda e, b=b: e.tensor_tensor(
                                b["r1scr"][:, 0:512], b["r1scr"][:, 0:512], b["s1scr"][:, 0:512],
                                ALU.add), deps=[tm1_, tr1], strict=True)
                            tu = pb.emit("vector", lambda e, tt=tt, b=b: e.tensor_tensor(
                                b["ubuf"][:, tt * 512:tt * 512 + 512], b["zscr"][:, 0:512],
                                b["r1scr"][:, 0:512], ALU.mult),
                                deps=[tg_, tz] + list(pb.tail.get(("mgs_scan",), ())), strict=True)
                            pb.tail[("mgs_cu",)] = (tu,)
                            cu_toks += [tc1, tu]
                        init = 0.5 if th == 0 else b["hmid"][:, cc:cc + 1]
                        hdep = list(pb.tail.get(("mgs_h",), ())) +                                (list(pb.tail.get(("mgs_hm",), ())) if th == 1 else [])
                        tscan = pb.emit("vector", lambda e, cc=cc, b=b, init=init: e.tensor_tensor_scan(
                            b["h_all"][:, cc * HS:(cc + 1) * HS], b["cbuf"][:, 0:HS],
                            b["ubuf"][:, 0:HS], init, ALU.mult, ALU.add),
                            deps=cu_toks + hdep, strict=True)
                        pb.tail[("mgs_scan",)] = (tscan,)
                        if th == 0:
                            thm = pb.emit("vector", lambda e, cc=cc, b=b: e.tensor_copy(
                                b["hmid"][:, cc:cc + 1], b["h_all"][:, cc * HS + HS - 1: cc * HS + HS]),
                                deps=[tscan], strict=True)
                            pb.tail[("mgs_hm",)] = pb.tail.get(("mgs_hm",), ()) + (thm,)
                            scan_toks.append(thm)
                        scan_toks.append(tscan)
                        wl_prev = (mmg, mmh)
                    if th == 1:
                        pb.tail[("mgs_hm",)] = ()
                    nxt = []
                    rm_prev = ()
                    for tc in range(HS_tc):
                        tcg = tbase + tc
                        tl = pb.dma("gpsimd", bP["zt"][:, 0:D], mg[tcg],
                                    deps=scan_toks + list(rm_prev))
                        tprev = tl
                        for cc in range(DC):
                            phh = cc % 2
                            tpt = pb.emit("tensor", lambda e, tc=tc, cc=cc, phh=phh, b=b: e.transpose(
                                b["pt"][phh][:, 0:128],
                                b["h_all"][:, cc * HS + tc * 128: cc * HS + tc * 128 + 128],
                                b["identb"][:, 0:128]),
                                deps=[scan_toks[-1]] + list(pb.tail.get(("mgs_pt%d" % phh,), ())))
                            ta = pb.emit("vector", lambda e, tc=tc, cc=cc, phh=phh, b=b: e.tensor_tensor(
                                bP["zt"][:, cc * 128:(cc + 1) * 128], b["pt"][phh][:, 0:128],
                                bP["zt"][:, cc * 128:(cc + 1) * 128], ALU.add),
                                deps=[tpt, tl], strict=True)
                            pb.tail[("mgs_pt%d" % phh,)] = (ta,)
                            tprev = ta
                        ts = pb.dma("gpsimd", mg[tcg], bP["zt"][:, 0:D], deps=[tprev])
                        rm_prev = (ts,)
                        nxt.append(ts)
                    cur = nxt[-1:]
            fin = []
            fprev = ()
            for tc in range(S_tc):
                tl1 = pb.dma("gpsimd", bP["zt"][:, 0:D], mg[tc], deps=cur + list(fprev))
                tl2 = pb.dma("sync", b["lnscr"][:, 0:D], xi[tc], deps=cur + list(fprev))
                ta = pb.emit("vector", lambda e, tc=tc, b=b: e.tensor_tensor(
                    bP["zt"][:, 0:D], b["lnscr"][:, 0:D], bP["zt"][:, 0:D], ALU.add),
                    deps=[tl1, tl2], strict=True)
                ts = pb.dma("gpsimd", xo[tc], bP["zt"][:, 0:D], deps=[ta])
                fprev = (ts,)
                fin.append(ts)
        return fin[-1:]

    upto = C.get("upto", 8)
    t = list(C["ret_toks"])
    cur = xa
    steps = [("m0a", fused_moe_s), ("aa", fused_attn_s), ("m0b", fused_moe_s),
             ("at", fused_attn_s), ("f1a", fused_moe_s), ("mg", None),
             ("f1b", fused_moe_s)]
    done = 1
    for name, fn in steps:
        if done >= upto:
            break
        nxt = xb if cur is xa else xa
        if name == "mg":
            t = fused_mingru_s(cur, nxt, t)
        else:
            t = fn(name, cur, nxt, t)
        cur = nxt
        done += 1
    st = []
    prevd = ()
    for tc in range(S_tc):
        tl = pb.dma("gpsimd", bP["zt"][:, 0:D], cur[tc], deps=list(t) + list(prevd))
        ts = pb.dma("sync", C["xout"][tc], bP["zt"][:, 0:D], deps=[tl])
        prevd = (ts, tl)
        st.append(ts)
    pb.emit("sync", lambda e: e.nop(), deps=st)


def _fused_host_inputs(z, x_full, nq=16):
    import math
    S = 2 * nq * 128
    S_tc = 2 * nq
    H8 = 8
    perm = np.concatenate([np.arange(0, 128, 2), np.arange(1, 128, 2)])
    sv = (np.arange(0, 128, 2) + 0.4 * 128) / (1.4 * 128)
    power = (np.arange(S, dtype=np.float64) / 512)[:, None]
    scale = sv[None, :] ** power
    inv_freq = 1.0 / 10000.0 ** (np.arange(64, dtype=np.float64) / 64)
    ang = np.arange(S, dtype=np.float64)[:, None] * inv_freq[None, :]
    sin_, cos_ = np.sin(ang), np.cos(ang)
    tabs = np.stack([np.ascontiguousarray(np.concatenate([a.T, a.T], 0).astype(BF))
                     for a in (cos_ * scale, sin_ * scale, cos_ / scale, sin_ / scale)])
    gammas = 1.0 - np.exp(np.linspace(np.log(1 / 32), np.log(1 / 512), H8))
    pp = np.arange(128)
    dmask = np.concatenate([np.where(pp[None, :] >= pp[:, None],
                                     g ** (pp[None, :] - pp[:, None]).astype(np.float64), 0.0).astype(np.float32)
                            for g in gammas], 1)
    gcols = np.concatenate([np.stack([g ** (pp + 1) for g in gammas], 1),
                            np.stack([g ** (127 - pp) for g in gammas], 1)], 1).astype(np.float32)
    base = dict(
        ident=IDENT_BF, identf_d=np.eye(128, dtype=np.float32),
        onesc=np.ones((1, 128), BF),
        r_wq=np.ascontiguousarray(z["L0_attn_a_WQ"][:, :, :, perm].reshape(2, H8, 8, 128, 128)),
        r_wk=np.ascontiguousarray(z["L0_attn_a_WK"][:, :, :, perm].reshape(2, H8, 8, 128, 128)),
        r_wv=np.ascontiguousarray(z["L0_attn_a_WV"].reshape(2, H8, 8, 128, 128)),
        r_wg=np.ascontiguousarray(z["L0_attn_a_WG"].reshape(2, 8, 128, D)),
        r_wo=np.ascontiguousarray(z["L0_attn_a_WO"].reshape(2, 8, 128, D)),
        r_w1=np.ascontiguousarray(z["L0_attn_a_ffn_w1"].reshape(2, 8, 128, 4096)),
        r_w2=np.ascontiguousarray(z["L0_attn_a_ffn_w2"].reshape(2, 32, 128, D)),
        r_gbrows=np.concatenate([np.concatenate(
            [z["L0_attn_a_ln1_w"][i], z["L0_attn_a_ln1_b"][i],
             z["L0_attn_a_ln2_w"][i], z["L0_attn_a_ln2_b"][i],
             z["L0_attn_a_gn_w"][i], z["L0_attn_a_gn_b"][i]]) for i in range(2)]
            ).astype(BF)[None, :],
        r_b1cols=np.ascontiguousarray(np.concatenate(
            [z["L0_attn_a_ffn_b1"][i].reshape(32, 128).T for i in range(2)], 1)),
        r_b2rows=z["L0_attn_a_ffn_b2"].ravel().astype(BF)[None, :],
        r_xpos=tabs, r_dmask=dmask, r_gcols=gcols,
        m0_gwT=np.ascontiguousarray(z["L0_ffn_gate_w"].T.reshape(8, 128, 8)
                                    .transpose(1, 0, 2).reshape(128, 64)).astype(np.float32),
        m0_gu=np.ascontiguousarray(z["L0_ffn_gate_up_proj"].reshape(8, 8, 128, 2, 8, 128)
                                   .transpose(0, 3, 4, 2, 1, 5).reshape(8, 16, 128, 1024)),
        m0_down=np.ascontiguousarray(z["L0_ffn_down_proj"].reshape(8, 8, 128, 1024)),
        m1_gwT=np.ascontiguousarray(z["L1_ffn_router_w"].reshape(8, 128, 8)
                                    .transpose(1, 0, 2).reshape(128, 64)).astype(np.float32),
        m1_gu=np.ascontiguousarray(np.concatenate([z["L1_ffn_w1"], z["L1_ffn_w3"]], axis=2)
                                   .reshape(8, 8, 128, 2, 8, 128)
                                   .transpose(0, 3, 4, 2, 1, 5).reshape(8, 16, 128, 1024)),
        m1_down=np.ascontiguousarray(z["L1_ffn_w2"].reshape(8, 8, 128, 1024)),
        m1_sgu=np.ascontiguousarray(np.concatenate([z["L1_ffn_sw1"], z["L1_ffn_sw3"]], axis=1)
                                    .reshape(8, 128, 2, 8, 128)
                                    .transpose(2, 3, 1, 0, 4).reshape(16, 128, 1024)),
        m1_sdown=np.ascontiguousarray(z["L1_ffn_sw2"].reshape(8, 128, 1024)),
        m1_ebias=np.ascontiguousarray(np.broadcast_to(z["L1_ffn_expert_bias"], (128, 8))).astype(np.float32),
        g_w=np.ascontiguousarray(z["L1_attn_b_W"].reshape(3, 8, 128, 2048)),
        g_gbrows=np.concatenate([z["L1_attn_b_ln_gamma"].ravel(),
                                 z["L1_attn_b_ln_beta"].ravel()]).astype(BF)[None, :],
        g_bcols=np.ascontiguousarray(z["L1_attn_b_b"].reshape(3, 16, 128)
                                     .transpose(2, 0, 1).reshape(128, 48)),
    )
    ppp, fff = np.arange(128)[:, None], np.arange(512)[None, :]
    base["cmask"] = np.concatenate(
        [np.where(ppp + idx * 128 > fff, -1e30, 0) for idx in range(4)], 1).astype(BF)
    for pre, Wq, Wk, Wv, Wo in (
        ("aa_", np.ascontiguousarray(z["L0_attn_b_Wq"].T), np.ascontiguousarray(z["L0_attn_b_Wk"].T),
         np.ascontiguousarray(z["L0_attn_b_Wv"].T), np.ascontiguousarray(z["L0_attn_b_Wp"].T)),
        ("at_", z["L1_attn_a_Wq"], z["L1_attn_a_Wk"], z["L1_attn_a_Wv"], z["L1_attn_a_Wo"])):
        base[pre + "wq"] = np.ascontiguousarray(Wq.reshape(8, 128, 16, 64).transpose(2, 0, 1, 3))
        base[pre + "wk"] = np.ascontiguousarray(Wk.reshape(8, 128, 16, 64).transpose(2, 0, 1, 3))
        base[pre + "wv"] = np.ascontiguousarray(Wv.reshape(8, 128, 16, 64).transpose(2, 0, 1, 3))
        base[pre + "wo"] = np.ascontiguousarray(Wo.reshape(8, 128, 1024))
    for n2 in "qkv":
        base["aa_bT" + n2] = np.ascontiguousarray(z["L0_attn_b_B" + n2].T.reshape(8, 128, 8))
        base["aa_aT" + n2] = np.ascontiguousarray(z["L0_attn_b_A" + n2].T)
    bqk = np.zeros((64, 32), np.float32)
    for h in range(16):
        bqk[:, 2 * h] = z["L1_attn_a_bq"][h * 64:(h + 1) * 64]
        bqk[:, 2 * h + 1] = z["L1_attn_a_bk"][h * 64:(h + 1) * 64]
    base["at_bqk"] = bqk
    base["at_bvo"] = np.concatenate([z["L1_attn_a_bv"], z["L1_attn_a_bo"]]).astype(BF)[None, :]
    base["aa_coef"] = np.zeros((128, 2 * S_tc), np.float32)
    base["aa_kmask"] = np.zeros((128, S_tc), np.float32)
    base["at_kmask"] = np.zeros((128, S_tc), np.float32)
    base["aa_kpos"] = (np.arange(S_tc)[None, :] * 128
                       + np.arange(128)[:, None]).astype(np.float32)
    in_maps = []
    for c in range(8):
        b, h = c // 2, c % 2
        im = dict(base)
        im["xin"] = np.ascontiguousarray(x_full[b].reshape(S_tc, 128, 1024))
        hf = np.zeros((128, 4), np.float32)
        hf[:, 0] = 1.0 - h
        hf[:, 1] = h
        im["hflags"] = hf
        in_maps.append(im)
    return in_maps


_FUSED_CACHE = {}


def kernel_fused(**z):
    nq = 16
    x_full = np.asarray(z["L0_attn_a_X"], np.float32)
    if nq not in _FUSED_CACHE:
        _FUSED_CACHE[nq] = build_fused(nq=nq)
    nc = _FUSED_CACHE[nq]
    in_maps = _fused_host_inputs(z, x_full, nq=nq)
    res = run_bass_kernel_spmd(nc, in_maps, core_ids=list(range(8)))
    out = np.stack([np.asarray(res.results[2 * b]["xout"]).reshape(2 * nq * 128, 1024)
                    for b in range(4)])
    return out


kernel_multi = kernel
kernel = kernel_fused


# version 08
# speedup vs baseline: 1.4000x; 1.0168x over previous
import numpy as np
import ml_dtypes
from contextlib import ExitStack
import concourse.bass as bass
import concourse.mybir as mybir
from concourse.bass_utils import run_bass_kernel_spmd

F32 = mybir.dt.float32
BF16 = mybir.dt.bfloat16
AF = mybir.ActivationFunctionType
ALU = mybir.AluOpType
AX = mybir.AxisListType
BF = ml_dtypes.bfloat16

D = 1024
DC = 8
TCORE = 2048
NTC = TCORE // 128
EPS_LN = 1e-5


ENGINES = ("tensor", "vector", "scalar", "gpsimd", "sync")

class PB:

    def __init__(self, nc, es):
        self.nc = nc
        self.es = es
        self.sems = {}
        self.cnt = {}
        for s in ENGINES + tuple("d" + e for e in ENGINES) + ("coll",):
            self.sems[s] = es.enter_context(nc.semaphore("S_" + s))
            self.cnt[s] = 0
        self.q = {e: [] for e in ENGINES}
        self.hwm = {}
        self.tail = {}

    def _waits(self, eng, deps, strict):
        need = {}
        for (s, v) in deps:
            if v > need.get(s, 0):
                need[s] = v
        ws = []
        for s, v in need.items():
            k = (eng, s)
            if self.hwm.get(k, 0) >= v:
                continue
            self.hwm[k] = v
            ws.append((s, v))
        return ws

    def emit(self, eng, fn, deps=(), strict=False, dma=False, coll=False):
        ws = self._waits(eng, deps, strict)
        if coll:
            sem = "coll"
        elif dma:
            sem = "d" + eng
        else:
            sem = eng
        self.cnt[sem] += 16 if (dma or coll) else 1
        tok = (sem, self.cnt[sem])
        self.q[eng].append((fn, ws, sem, 16 if (dma or coll) else 1))
        return tok

    def barrier_tok(self, eng):
        return (eng, self.cnt[eng])

    def all_toks(self):
        return [(s, c) for s, c in self.cnt.items() if c > 0]

    def dma(self, eng, out, in_, deps=(), strict=False):
        deps = list(deps) + [("d" + eng, self.cnt["d" + eng])]
        return self.emit(eng, lambda e, o=out, i=in_: e.dma_start(out=o, in_=i),
                         deps=deps, strict=strict, dma=True)

    def build(self):
        block = self.es.enter_context(self.nc.Block())
        for eng in ENGINES:
            items = self.q[eng]
            if not items:
                continue
            def runner(e, items=items):
                for (fn, ws, sem, amt) in items:
                    for (s, v) in ws:
                        e.wait_ge(self.sems[s], v)
                    inst = fn(e)
                    inst.then_inc(self.sems[sem], amt)
            getattr(block, eng)(runner)
        return block


def new_nc():
    return bass.Bass("TRN2", target_bir_lowering=False)



class Buf:
    __slots__ = ("t",)
    def __init__(self, t):
        self.t = t
    def __getitem__(self, idx):
        return self.t[idx]


def sb(pb, name, shape, dtype, side=None):
    return pb.es.enter_context(pb.nc.sbuf_tensor(name, list(shape), dtype, side=side))

def ps(pb, name, shape, dtype=F32):
    shape = list(shape)
    min_cols = 512 if dtype == F32 else 1024
    shape[1] = max(shape[1], min_cols)
    return pb.es.enter_context(pb.nc.psum_tensor(name, shape, dtype))



def emit_ln_tm(pb, x, xn_bf, stats, sq_scr, n_tc=NTC, tok_x=(), affine_wb=None,
               gamma_bc=None, beta_bc=None, out_extra=None):
    n = n_tc
    toks = []
    t_red = []
    for tc in range(n):
        xs = x[:, tc * D:(tc + 1) * D]
        t1 = pb.emit("scalar", lambda e, xs=xs, tc=tc: e.activation(
            sq_scr[:, 0:D], xs, AF.Square, accum_out=stats[:, n + tc:n + tc + 1]),
            deps=tok_x)
        t2 = pb.emit("vector", lambda e, xs=xs, tc=tc: e.reduce_sum(
            stats[:, tc:tc + 1], xs, axis=AX.X), deps=tok_x)
        t_red += [t1, t2]
    tmu = pb.emit("vector", lambda e: e.tensor_scalar(
        stats[:, 2 * n:3 * n], stats[:, 0:n], 1.0 / D, None, ALU.mult, ALU.bypass),
        deps=t_red, strict=True)
    tm2 = pb.emit("vector", lambda e: e.tensor_tensor(
        stats[:, 0:n], stats[:, 2 * n:3 * n], stats[:, 2 * n:3 * n], ALU.mult),
        deps=[tmu], strict=True)
    tvar = pb.emit("vector", lambda e: e.scalar_tensor_tensor(
        stats[:, n:2 * n], stats[:, n:2 * n], 1.0 / D, stats[:, 0:n],
        ALU.mult, ALU.subtract), deps=[tm2], strict=True)
    tve = pb.emit("vector", lambda e: e.tensor_scalar(
        stats[:, n:2 * n], stats[:, n:2 * n], EPS_LN, None, ALU.add, ALU.bypass),
        deps=[tvar], strict=True)
    tln = pb.emit("scalar", lambda e: e.activation(
        stats[:, n:2 * n], stats[:, n:2 * n], AF.Ln, bias=0.0, scale=1.0),
        deps=[tve], strict=True)
    trstd = pb.emit("scalar", lambda e: e.activation(
        stats[:, 3 * n:4 * n], stats[:, n:2 * n], AF.Exp, bias=0.0, scale=-0.5),
        deps=[tln], strict=True)
    for tc in range(n):
        xs = x[:, tc * D:(tc + 1) * D]
        os = xn_bf[:, tc * D:(tc + 1) * D]
        if affine_wb is None:
            tt = pb.emit("vector", lambda e, xs=xs, os=os, tc=tc: e.tensor_scalar(
                os, xs, stats[:, 2 * n + tc:2 * n + tc + 1],
                stats[:, 3 * n + tc:3 * n + tc + 1], ALU.subtract, ALU.mult),
                deps=[trstd], strict=True)
        else:
            g_bc, b_bc, scr = affine_wb
            t0 = pb.emit("vector", lambda e, xs=xs, tc=tc, scr=scr: e.tensor_scalar(
                scr[:, 0:D], xs, stats[:, 2 * n + tc:2 * n + tc + 1],
                stats[:, 3 * n + tc:3 * n + tc + 1], ALU.subtract, ALU.mult),
                deps=[trstd] + ldep + list(prev_aff), strict=True)
            t0b = pb.emit("vector", lambda e, scr=scr: e.tensor_tensor(
                scr[:, 0:D], scr[:, 0:D], g_bc[:, 0:D], ALU.mult),
                deps=[t0], strict=True)
            tt = pb.emit("vector", lambda e, os=os, scr=scr: e.tensor_tensor(
                os, scr[:, 0:D], b_bc[:, 0:D], ALU.add), deps=[t0b], strict=True)
        toks.append(tt)
    return toks, toks[-1]


def emit_tm_to_fm(pb, src_bf, dst_fm, ident_bf, pt, n_tc=NTC, deps_per_tile=None,
                  src_cols=D, dst_cols=None, extra_deps=()):
    if dst_cols is None:
        dst_cols = n_tc * 128
    ndc = src_cols // 128
    toks = [None] * ndc
    prev_evac = {0: (), 1: ()}
    k = 0
    for tc in range(n_tc):
        dep = list(deps_per_tile[tc]) if deps_per_tile else []
        dep += list(extra_deps)
        for dc in range(ndc):
            half = k % 2
            k += 1
            po = pt[half][:, 0:128]
            ts = pb.emit("tensor", lambda e, tc=tc, dc=dc, po=po: e.transpose(
                po, src_bf[:, tc * src_cols + dc * 128: tc * src_cols + dc * 128 + 128],
                ident_bf[:, 0:128]), deps=dep + list(prev_evac[half]))
            tv = pb.emit("scalar", lambda e, tc=tc, dc=dc, po=po: e.copy(
                dst_fm[:, dc * dst_cols + tc * 128: dc * dst_cols + tc * 128 + 128],
                po), deps=[ts])
            prev_evac[half] = (tv,)
            toks[dc] = tv
    return toks


def emit_weight_load(pb, dram_ap, stage, wbf, nchunks, cols, deps=(), eng_dma="sync",
                     cast_eng="scalar"):
    toks = []
    key = ("stage", stage.name if hasattr(stage, "name") else id(stage))
    prev_cast = pb.tail.get(key)
    for c in range(nchunks):
        sl = stage[:, 0:cols]
        dep = list(deps)
        if prev_cast is not None:
            dep.append(prev_cast)
        td = pb.dma(eng_dma, sl, dram_ap[c], deps=dep)
        prev_cast = pb.emit(cast_eng, lambda e, sl=sl, c=c: e.copy(
            wbf[:, c * cols:(c + 1) * cols], sl), deps=[td])
        toks.append(prev_cast)
    pb.tail[key] = prev_cast
    return toks, toks[-1]



def emit_routing_l0(pb, psum_tok, psum_r, routed_tm, rscr, tc, renorm=True):
    lg = rscr[:, 0:8]
    ex = rscr[:, 8:16]
    rw = rscr[:, 16:24]
    rw2 = rscr[:, 24:32]
    m = rscr[:, 32:33]
    nm = rscr[:, 33:34]
    s = rscr[:, 34:35]
    rs = rscr[:, 35:36]
    m1 = rscr[:, 36:37]
    m2 = rscr[:, 37:38]
    a = rscr[:, 38:39]
    d1 = rscr[:, 39:40]
    s2 = rscr[:, 40:41]
    d2 = rscr[:, 41:42]
    den = rscr[:, 42:43]
    coef = rscr[:, 43:44]
    mask1 = rscr[:, 44:52]
    mask2 = rscr[:, 52:60]
    tl = pb.emit("vector", lambda e: e.tensor_copy(lg, psum_r[:, 0:8]), deps=[psum_tok])
    t1 = pb.emit("vector", lambda e: e.reduce_max(m, lg, axis=AX.X), deps=[tl], strict=True)
    t2 = pb.emit("vector", lambda e: e.tensor_scalar(nm, m, -1.0, None, ALU.mult, ALU.bypass), deps=[t1], strict=True)
    t3 = pb.emit("scalar", lambda e: e.activation(ex, lg, AF.Exp, bias=nm[:, 0:1], scale=1.0), deps=[t2], strict=True)
    t4 = pb.emit("vector", lambda e: e.reduce_sum(s, ex, axis=AX.X), deps=[t3], strict=True)
    t5 = pb.emit("vector", lambda e: e.reciprocal(rs, s), deps=[t4], strict=True)
    t6 = pb.emit("vector", lambda e: e.tensor_scalar(rw, ex, rs[:, 0:1], None, ALU.mult, ALU.bypass), deps=[t5], strict=True)
    t7 = pb.emit("vector", lambda e: e.reduce_max(m1, rw, axis=AX.X), deps=[t6], strict=True)
    t8 = pb.emit("vector", lambda e: e.tensor_scalar(mask1, rw, m1[:, 0:1], None, ALU.is_ge, ALU.bypass), deps=[t7], strict=True)
    t9 = pb.emit("vector", lambda e: e.scalar_tensor_tensor(rw2, mask1, -1.0, rw, ALU.mult, ALU.add), deps=[t8], strict=True)
    t10 = pb.emit("vector", lambda e: e.reduce_max(m2, rw2, axis=AX.X), deps=[t9], strict=True)
    t11 = pb.emit("vector", lambda e: e.tensor_scalar(mask2, rw2, m2[:, 0:1], None, ALU.is_ge, ALU.bypass), deps=[t10], strict=True)
    if renorm:
        t12 = pb.emit("vector", lambda e: e.tensor_tensor(a, m1, m2, ALU.add), deps=[t11], strict=True)
        t13 = pb.emit("vector", lambda e: e.tensor_scalar(d1, a, 1e-9, None, ALU.add, ALU.bypass), deps=[t12], strict=True)
        rd1 = rscr[:, 60:61]
        t13b = pb.emit("vector", lambda e: e.reciprocal(rd1, d1), deps=[t13], strict=True)
        t14 = pb.emit("vector", lambda e: e.tensor_tensor(s2, a, rd1, ALU.mult), deps=[t13b], strict=True)
        t15 = pb.emit("vector", lambda e: e.tensor_scalar(d2, s2, 1e-9, None, ALU.add, ALU.bypass), deps=[t14], strict=True)
        t16 = pb.emit("vector", lambda e: e.tensor_tensor(den, d1, d2, ALU.mult), deps=[t15], strict=True)
        t17 = pb.emit("vector", lambda e: e.reciprocal(coef, den), deps=[t16], strict=True)
    else:
        t17 = pb.emit("vector", lambda e: e.memset(coef, 1.0), deps=[t11], strict=True)
    ta = pb.emit("vector", lambda e: e.tensor_scalar(mask1, mask1, m1[:, 0:1], coef[:, 0:1], ALU.mult, ALU.mult), deps=[t17], strict=True)
    tb = pb.emit("vector", lambda e: e.tensor_scalar(mask2, mask2, m2[:, 0:1], coef[:, 0:1], ALU.mult, ALU.mult), deps=[ta], strict=True)
    td = pb.emit("vector", lambda e: e.tensor_tensor(
        routed_tm[:, tc * 8:(tc + 1) * 8], mask1, mask2, ALU.add), deps=[tb], strict=True)
    return td




def emit_moe_l0(pb, bufs, wap, n_tc=NTC, fm_toks=None, renorm=True,
                shared=False, l1=False):
    x = bufs["x"]; xn = bufs["xn_fm"]
    TT_ = n_tc * 128
    fm_dep = ([fm_toks[-1]] if fm_toks else [])
    pr = bufs["ps_r"]
    prev_route = None
    for tc in range(n_tc):
        mm_toks = []
        for dc in range(DC):
            dep = list(fm_dep)
            if prev_route is not None and dc == 0:
                dep.append(prev_route)
            tmm = pb.emit("tensor", lambda e, tc=tc, dc=dc: e.matmul(
                pr[:, 0:8], xn[:, dc * TT_ + tc * 128: dc * TT_ + tc * 128 + 128],
                bufs["gwT"][:, dc * 8:(dc + 1) * 8],
                start=(dc == 0), stop=(dc == DC - 1)), deps=dep)
            mm_toks.append(tmm)
        if l1:
            prev_route = emit_routing_l1(pb, mm_toks[-1], pr, bufs["routed_tm"],
                                         bufs["rscr"], tc, bufs["bias_bc"],
                                         bias_tok=bufs.get("bias_tok", ("sync", 0)))
        else:
            prev_route = emit_routing_l0(pb, mm_toks[-1], pr, bufs["routed_tm"],
                                         bufs["rscr"], tc, renorm=renorm)
    route_done = prev_route

    n_exp = 8
    gu_bufs = [bufs["gu_bf0"], bufs["gu_bf1"]]
    dn_bufs = [bufs["dn_bf0"], bufs.get("dn_bf1", bufs["dn_bf0"])]
    h_bufs = [bufs["hbuf0"], bufs["hbuf1"]]
    w_free = [(), ()]
    h_free = {0: (), 1: ()}
    x_tok = {}
    psg = bufs["ps_g"]; psu = bufs["ps_u"]; psd = bufs["ps_d"]
    prev_g_use = ()
    prev_u_use = ()
    prev_sg_use = ()
    prev_d_use = {0: (), 1: ()}
    out_toks = []
    hf_major = bufs.get("hf_major", False)
    for e_i in range(n_exp + (1 if shared else 0)):
        is_shared = (e_i == n_exp)
        half = e_i % 2
        gub = gu_bufs[half]; dnb = dn_bufs[half]
        gu_src = wap["sgu"] if is_shared else wap["gu"][e_i]
        dn_src = wap["sdown"] if is_shared else wap["down"][e_i]
        if hf_major:
            gu_toks = emit_weight_load_mq(pb, gu_src, bufs["stages"], gub, 1024,
                                          deps=list(w_free[0]) + list(w_free[1]),
                                          order=[hf + half8 for hf in range(8)
                                                 for half8 in (0, 8)])
            dn_toks = emit_weight_load_mq(pb, dn_src, bufs["stages2"], dnb, 1024,
                                          deps=list(w_free[0]) + list(w_free[1]))
            tgu = gu_toks[-1]
            tdn = dn_toks[-1]
        else:
            _, tgu = emit_weight_load(pb, gu_src, bufs["stage"], gub, len(gu_src), 1024,
                                      deps=list(w_free[0]) + list(w_free[1]), eng_dma="sync")
            _, tdn = emit_weight_load(pb, dn_src, bufs["stage2"], dnb, len(dn_src), 1024,
                                      deps=list(w_free[0]) + list(w_free[1]), eng_dma="gpsimd")
            gu_toks = dn_toks = None
        last_use = []
        n_hb = 1 if h_bufs[1] is h_bufs[0] else 2
        for tt in range(n_tc // 4):
            hh = tt % n_hb
            hb = h_bufs[hh]
            h_ready = []
            for hf in range(8):
                gmm = umm = None
                if hf_major:
                    gdep = [gu_toks[hf]]
                    udep = [gu_toks[8 + hf]]
                    goff = lambda dc, hf=hf: hf * 1024 + dc * 128
                    uoff = lambda dc, hf=hf: (8 + hf) * 1024 + dc * 128
                else:
                    gdep = [tgu]
                    udep = [tgu]
                    goff = lambda dc, hf=hf: dc * 2048 + hf * 128
                    uoff = lambda dc, hf=hf: dc * 2048 + 1024 + hf * 128
                for dc in range(DC):
                    dep = gdep + (list(prev_g_use) if dc == 0 else []) + fm_dep
                    gmm = pb.emit("tensor", lambda e, gub=gub, tt=tt, hf=hf, dc=dc, goff=goff: e.matmul(
                        psg[:, 0:512],
                        gub[:, goff(dc): goff(dc) + 128],
                        xn[:, dc * TT_ + tt * 512: dc * TT_ + tt * 512 + 512],
                        start=(dc == 0), stop=(dc == DC - 1)), deps=dep)
                for dc in range(DC):
                    dep = udep + (list(prev_u_use) if dc == 0 else []) + fm_dep
                    umm = pb.emit("tensor", lambda e, gub=gub, tt=tt, hf=hf, dc=dc, uoff=uoff: e.matmul(
                        psu[:, 0:512],
                        gub[:, uoff(dc): uoff(dc) + 128],
                        xn[:, dc * TT_ + tt * 512: dc * TT_ + tt * 512 + 512],
                        start=(dc == 0), stop=(dc == DC - 1)), deps=dep)
                tsg = pb.emit("scalar", lambda e: e.activation(
                    bufs["sg"][:, 0:512], psg[:, 0:512], AF.Silu),
                    deps=[gmm] + list(prev_sg_use))
                th = pb.emit("vector", lambda e, hb=hb, hf=hf: e.tensor_tensor(
                    hb[:, hf * 512:(hf + 1) * 512], bufs["sg"][:, 0:512],
                    psu[:, 0:512], ALU.mult),
                    deps=[tsg, umm] + list(h_free[hh]), strict=True)
                prev_g_use = (tsg,)
                prev_u_use = (th,)
                prev_sg_use = (th,)
                h_ready.append(th)
            d_toks = []
            for tc2 in range(4):
                tcg = tt * 4 + tc2
                for dc2 in range(2):
                    pd = psd[:, dc2 * 512:(dc2 + 1) * 512]
                    dmm = None
                    for hf in range(8):
                        dep = [h_ready[hf], dn_toks[hf] if hf_major else tdn]
                        if hf == 0:
                            dep = dep + list(prev_d_use[dc2])
                        dmm = pb.emit("tensor", lambda e, hb=hb, dnb=dnb, tc2=tc2, dc2=dc2, hf=hf, pd=pd: e.matmul(
                            pd,
                            hb[:, hf * 512 + tc2 * 128: hf * 512 + tc2 * 128 + 128],
                            dnb[:, hf * 1024 + dc2 * 512: hf * 1024 + dc2 * 512 + 512],
                            start=(hf == 0), stop=(hf == 7)), deps=dep)
                    xs = x[:, tcg * D + dc2 * 512: tcg * D + dc2 * 512 + 512]
                    xdep = [dmm, x_tok.get((tcg, dc2), ("vector", 0))]
                    if is_shared:
                        tacc = pb.emit("vector", lambda e, xs=xs, pd=pd: e.tensor_tensor(
                            xs, pd, xs, ALU.add), deps=xdep, strict=True)
                    else:
                        rcol = bufs["routed_tm"][:, tcg * 8 + e_i: tcg * 8 + e_i + 1]
                        tacc = pb.emit("vector", lambda e, xs=xs, pd=pd, rcol=rcol: e.scalar_tensor_tensor(
                            xs, pd, rcol, xs, ALU.mult, ALU.add),
                            deps=xdep + [route_done], strict=True)
                    x_tok[(tcg, dc2)] = tacc
                    prev_d_use[dc2] = (tacc,)
                    d_toks.append(tacc)
                    last_use.append(tacc)
            h_free[hh] = (d_toks[-1],)
        w_free[half] = (last_use[-1],)
    return [x_tok[k] for k in x_tok]


def emit_ln_fm(pb, x, xn_fm, stats, sq_scr, tmp_bf, ident_bf, pt, n_tc=NTC,
               tok_x=(), affine_wb=None, n_dc=DC, dst_off=0, dst_stride=None,
               src_dram=None, dram_scr=None, dma_eng="gpsimd"):
    n = n_tc
    t_red = []
    prev_sq = None
    scr_free = {0: (), 1: ()}
    def tile_src(tc):
        half = tc % 2
        sl = dram_scr[:, half * D:(half + 1) * D]
        dep = list(tok_x) + list(scr_free[half])
        t = pb.dma(dma_eng, sl, src_dram[tc], deps=dep)
        return sl, [t], half
    for tc in range(n):
        if src_dram is None:
            xs = x[:, tc * D:(tc + 1) * D]
            dep0 = list(tok_x)
            half = None
        else:
            xs, dep0, half = tile_src(tc)
        dep1 = dep0 + ([prev_sq] if prev_sq is not None else [])
        t1 = pb.emit("scalar", lambda e, xs=xs, tc=tc: e.activation(
            sq_scr[:, 0:D], xs, AF.Square, accum_out=stats[:, n + tc:n + tc + 1]),
            deps=dep1, strict=True)
        prev_sq = t1
        t2 = pb.emit("vector", lambda e, xs=xs, tc=tc: e.reduce_sum(
            stats[:, tc:tc + 1], xs, axis=AX.X), deps=dep0)
        t_red += [t1, t2]
        if src_dram is not None:
            scr_free[half] = (t1, t2)
    tmu = pb.emit("vector", lambda e: e.tensor_scalar(
        stats[:, 2 * n:3 * n], stats[:, 0:n], 1.0 / D, None, ALU.mult, ALU.bypass),
        deps=t_red, strict=True)
    tm2 = pb.emit("vector", lambda e: e.tensor_tensor(
        stats[:, 0:n], stats[:, 2 * n:3 * n], stats[:, 2 * n:3 * n], ALU.mult),
        deps=[tmu], strict=True)
    tvar = pb.emit("vector", lambda e: e.scalar_tensor_tensor(
        stats[:, n:2 * n], stats[:, n:2 * n], 1.0 / D, stats[:, 0:n],
        ALU.mult, ALU.subtract), deps=[tm2], strict=True)
    tve = pb.emit("vector", lambda e: e.tensor_scalar(
        stats[:, n:2 * n], stats[:, n:2 * n], EPS_LN, None, ALU.add, ALU.bypass),
        deps=[tvar], strict=True)
    tln = pb.emit("scalar", lambda e: e.activation(
        stats[:, n:2 * n], stats[:, n:2 * n], AF.Ln, bias=0.0, scale=1.0),
        deps=[tve], strict=True)
    trstd = pb.emit("scalar", lambda e: e.activation(
        stats[:, 3 * n:4 * n], stats[:, n:2 * n], AF.Exp, bias=0.0, scale=-0.5),
        deps=[tln], strict=True)
    TT_ = dst_stride if dst_stride is not None else n * 128
    nh = tmp_bf.shape[1] // D if hasattr(tmp_bf, "shape") else 1
    nh = max(1, nh)
    prev_tr = {i: () for i in range(nh)}
    prev_pt = {0: (), 1: ()}
    prev_aff = ()
    last = None
    k = 0
    for tc in range(n):
        half = tc % nh
        if src_dram is None:
            xs = x[:, tc * D:(tc + 1) * D]
            ldep = []
            shalf = None
        else:
            xs, ldep, shalf = tile_src(tc)
        os = tmp_bf[:, half * D:(half + 1) * D]
        if affine_wb is None:
            tt = pb.emit("vector", lambda e, xs=xs, os=os, tc=tc: e.tensor_scalar(
                os, xs, stats[:, 2 * n + tc:2 * n + tc + 1],
                stats[:, 3 * n + tc:3 * n + tc + 1], ALU.subtract, ALU.mult),
                deps=[trstd] + list(prev_tr[half]) + ldep, strict=True)
            if shalf is not None:
                scr_free[shalf] = (tt,)
        else:
            g_bc, b_bc, scr = affine_wb
            t0 = pb.emit("vector", lambda e, xs=xs, tc=tc, scr=scr: e.tensor_scalar(
                scr[:, 0:D], xs, stats[:, 2 * n + tc:2 * n + tc + 1],
                stats[:, 3 * n + tc:3 * n + tc + 1], ALU.subtract, ALU.mult),
                deps=[trstd] + ldep + list(prev_aff), strict=True)
            t0b = pb.emit("vector", lambda e, scr=scr: e.tensor_tensor(
                scr[:, 0:D], scr[:, 0:D], g_bc[:, 0:D], ALU.mult),
                deps=[t0], strict=True)
            tt = pb.emit("vector", lambda e, os=os, scr=scr: e.tensor_tensor(
                os, scr[:, 0:D], b_bc[:, 0:D], ALU.add),
                deps=[t0b] + list(prev_tr[half]), strict=True)
            prev_aff = (tt,)
            if shalf is not None:
                scr_free[shalf] = (tt,)
        trs = []
        for dc in range(n_dc):
            ph = k % 2
            k += 1
            po = pt[ph][:, 0:128]
            ts = pb.emit("tensor", lambda e, os=os, dc=dc, po=po: e.transpose(
                po, os[:, dc * 128:dc * 128 + 128], ident_bf[:, 0:128]),
                deps=[tt] + list(prev_pt[ph]))
            tv = pb.emit("scalar", lambda e, tc=tc, dc=dc, po=po: e.copy(
                xn_fm[:, dc * TT_ + dst_off + tc * 128: dc * TT_ + dst_off + tc * 128 + 128], po),
                deps=[ts])
            prev_pt[ph] = (tv,)
            trs.append(tv)
            last = tv
        prev_tr[half] = (trs[-1],)
    return last



def chunk_w(w):
    w = np.ascontiguousarray(np.asarray(w, np.float32))
    return w.reshape(w.shape[0] // 128, 128, w.shape[1])

IDENT_BF = np.eye(128, dtype=BF)


def build_moe_l0_test(n_tc=NTC):
    nc = new_nc()
    es = ExitStack()
    T = n_tc * 128
    xin = nc.dram_tensor("xin", [n_tc, 128, D], F32, kind="ExternalInput").ap()
    gwT = nc.dram_tensor("gwT", [128, DC * 8], F32, kind="ExternalInput").ap()
    gu = nc.dram_tensor("gu", [8, DC, 128, 2048], F32, kind="ExternalInput").ap()
    down = nc.dram_tensor("down", [8, DC, 128, 1024], F32, kind="ExternalInput").ap()
    ident = nc.dram_tensor("ident", [128, 128], BF16, kind="ExternalInput").ap()
    xout = nc.dram_tensor("xout", [n_tc, 128, D], F32, kind="ExternalOutput").ap()
    with es:
        pb = PB(nc, es)
        bufs = dict(
            x=sb(pb, "x", [128, n_tc * D], F32),
            xn_fm=sb(pb, "xn_fm", [128, DC * T], BF16),
            tmp_bf=sb(pb, "tmp_bf", [128, D], BF16),
            stats=sb(pb, "stats", [128, 4 * n_tc], F32),
            identb=sb(pb, "identb", [128, 128], BF16),
            gwT=sb(pb, "gwTb", [128, DC * 8], BF16),
            stage=sb(pb, "stage", [128, 1024], F32),
            stage2=sb(pb, "stage2", [128, 1024], F32),
            gu_bf0=sb(pb, "gu_bf0", [128, DC * 2048], BF16),
            gu_bf1=sb(pb, "gu_bf1", [128, DC * 2048], BF16),
            dn_bf0=sb(pb, "dn_bf0", [128, DC * 1024], BF16),
            hbuf0=sb(pb, "hbuf0", [128, 4096], BF16),
            hbuf1=sb(pb, "hbuf1", [128, 4096], BF16),
            sg=sb(pb, "sg", [128, 512], F32),
            routed_tm=sb(pb, "routed_tm", [128, n_tc * 8], F32),
            rscr=sb(pb, "rscr", [128, 64], F32),
            ps_r=ps(pb, "ps_r", [128, 8]),
            ps_g=ps(pb, "ps_g", [128, 512]),
            ps_u=ps(pb, "ps_u", [128, 512]),
            ps_d=ps(pb, "ps_d", [128, 1024]),
            pt=(ps(pb, "pt0", [128, 128], BF16), ps(pb, "pt1", [128, 128], BF16)),
        )
        tid = pb.dma("sync", bufs["identb"][:, :], ident)
        x_toks = []
        for g in range(0, n_tc, 4):
            gn = min(4, n_tc - g)
            x_toks.append(pb.dma(
                "sync",
                bufs["x"][:, g * D:(g + gn) * D].rearrange("p (t d) -> p t d", t=gn),
                xin[g:g + gn].rearrange("t p d -> p t d")))
        tgw = pb.dma("sync", bufs["rscr"][:, 0:64], gwT)
        tgwb = pb.emit("vector", lambda e: e.tensor_copy(bufs["gwT"][:, :], bufs["rscr"][:, 0:64]), deps=[tgw])
        fm_last = emit_ln_fm(pb, bufs["x"], bufs["xn_fm"], bufs["stats"],
                             bufs["tmp_bf"][:, 0:D],
                             bufs["tmp_bf"], bufs["identb"], bufs["pt"], n_tc=n_tc,
                             tok_x=x_toks + [tid, tgwb])
        class W:
            pass
        wap = {
            "gu": [[gu[e_i, c // 2][:, (c % 2) * 1024:(c % 2) * 1024 + 1024] for c in range(DC * 2)] for e_i in range(8)],
            "down": [[down[e_i, c] for c in range(DC)] for e_i in range(8)],
        }
        moe_toks = emit_moe_l0(pb, bufs, wap, n_tc=n_tc, fm_toks=[fm_last], renorm=True)
        st_toks = []
        for g in range(0, n_tc, 4):
            gn = min(4, n_tc - g)
            st_toks.append(pb.dma(
                "gpsimd", xout[g:g + gn].rearrange("t p d -> p t d"),
                bufs["x"][:, g * D:(g + gn) * D].rearrange("p (t d) -> p t d", t=gn),
                deps=moe_toks))
        pb.emit("sync", lambda e: e.nop(), deps=st_toks)
        pb.build()
    return nc



def emit_routing_l1(pb, psum_tok, psum_r, routed_tm, rscr, tc, bias_bc, bias_tok=("sync", 0)):
    lg = rscr[:, 0:8]
    ex = rscr[:, 8:16]
    rw = rscr[:, 16:24]
    sp = rscr[:, 24:32]
    m = rscr[:, 32:33]
    nm = rscr[:, 33:34]
    s = rscr[:, 34:35]
    rs = rscr[:, 35:36]
    m1 = rscr[:, 36:37]
    m2 = rscr[:, 37:38]
    mask1 = rscr[:, 44:52]
    mask2 = rscr[:, 52:60]
    sp2 = rscr[:, 24:32]
    tl = pb.emit("vector", lambda e: e.tensor_copy(lg, psum_r[:, 0:8]), deps=[psum_tok])
    t1 = pb.emit("vector", lambda e: e.reduce_max(m, lg, axis=AX.X), deps=[tl], strict=True)
    t2 = pb.emit("vector", lambda e: e.tensor_scalar(nm, m, -1.0, None, ALU.mult, ALU.bypass), deps=[t1], strict=True)
    t3 = pb.emit("scalar", lambda e: e.activation(ex, lg, AF.Exp, bias=nm[:, 0:1], scale=1.0), deps=[t2], strict=True)
    t4 = pb.emit("vector", lambda e: e.reduce_sum(s, ex, axis=AX.X), deps=[t3], strict=True)
    t5 = pb.emit("vector", lambda e: e.reciprocal(rs, s), deps=[t4], strict=True)
    t6 = pb.emit("vector", lambda e: e.tensor_scalar(rw, ex, rs[:, 0:1], None, ALU.mult, ALU.bypass), deps=[t5], strict=True)
    t7 = pb.emit("vector", lambda e: e.tensor_tensor(sp, rw, bias_bc[:, 0:8], ALU.add), deps=[t6, bias_tok], strict=True)
    t8 = pb.emit("vector", lambda e: e.reduce_max(m1, sp, axis=AX.X), deps=[t7], strict=True)
    t9 = pb.emit("vector", lambda e: e.tensor_scalar(mask1, sp, m1[:, 0:1], None, ALU.is_ge, ALU.bypass), deps=[t8], strict=True)
    t10 = pb.emit("vector", lambda e: e.scalar_tensor_tensor(sp2, mask1, -1e30, sp, ALU.mult, ALU.add), deps=[t9], strict=True)
    t11 = pb.emit("vector", lambda e: e.reduce_max(m2, sp2, axis=AX.X), deps=[t10], strict=True)
    t12 = pb.emit("vector", lambda e: e.tensor_scalar(mask2, sp2, m2[:, 0:1], None, ALU.is_ge, ALU.bypass), deps=[t11], strict=True)
    t13 = pb.emit("vector", lambda e: e.tensor_tensor(mask1, mask1, mask2, ALU.add), deps=[t12], strict=True)
    td = pb.emit("vector", lambda e: e.tensor_tensor(
        routed_tm[:, tc * 8:(tc + 1) * 8], rw, mask1, ALU.mult), deps=[t13], strict=True)
    return td



def emit_weight_load_mq(pb, chunks, stages, wbf, cols, deps=(), queues=("sync", "gpsimd"),
                        cast_engs=("scalar", "vector"), order=None, dst_list=None):
    nq_ = len(queues)
    skey = [("stage", s.name if hasattr(s, "name") else id(s)) for s in stages]
    prev_cast = [pb.tail.get(k) for k in skey]
    toks = [None] * len(chunks)
    for oi, c in enumerate(order if order is not None else range(len(chunks))):
        ap = chunks[c]
        qi = oi % nq_
        sl = stages[qi][:, 0:cols]
        dep = list(deps)
        if prev_cast[qi] is not None:
            dep.append(prev_cast[qi])
        td = pb.dma(queues[qi], sl, ap, deps=dep)
        ce = cast_engs[oi % len(cast_engs)]
        dst = dst_list[c] if dst_list is not None else wbf[:, c * cols:(c + 1) * cols]
        if ce == "scalar":
            prev_cast[qi] = pb.emit(ce, lambda e, sl=sl, dst=dst: e.copy(
                dst, sl), deps=[td], strict=True)
        else:
            prev_cast[qi] = pb.emit(ce, lambda e, sl=sl, dst=dst: e.tensor_copy(
                dst, sl), deps=[td], strict=True)
        toks[c] = prev_cast[qi]
    for qi in range(nq_):
        pb.tail[skey[qi]] = prev_cast[qi]
    return toks


def emit_weight_load_off(pb, chunks, stage, wbf, cols, col_off, deps=(), eng_dma="sync",
                         cast_eng="scalar"):
    key = ("stage", stage.name if hasattr(stage, "name") else id(stage))
    prev_cast = pb.tail.get(key)
    for c, ap in enumerate(chunks):
        sl = stage[:, 0:cols]
        dep = list(deps)
        if prev_cast is not None:
            dep.append(prev_cast)
        td = pb.dma(eng_dma, sl, ap, deps=dep)
        prev_cast = pb.emit(cast_eng, lambda e, sl=sl, c=c: e.copy(
            wbf[:, col_off + c * cols: col_off + (c + 1) * cols], sl), deps=[td])
    pb.tail[key] = prev_cast
    return prev_cast


def emit_proj_residual(pb, bufs, src_fm, w_bf, n_tc, out_x, act=None, bias_row=None,
                       src_tok=(), w_tok=(), x_tok=None, scale_cols=None,
                       x_dram=None, xo_dram=None):
    T = n_tc * 128
    pd = bufs["ps_q"]
    scr = bufs.get("projscr")
    prev_use = pb.tail.get(("ps_proj",))
    scr_use = {}
    toks = []
    for tc in range(n_tc):
        if x_dram is not None:
            txl = pb.dma("gpsimd", out_x[:, 0:D], x_dram[tc],
                         deps=list(pb.tail.get(("prj_x",), ())))
        for dc2 in range(2):
            mm = None
            for hf in range(DC):
                dep = list(src_tok) + list(w_tok) + ([prev_use] if (hf == 0 and prev_use is not None) else [])
                if hf == 0 and prev_use is not None:
                    dep = dep + [prev_use]
                mm = pb.emit("tensor", lambda e, tc=tc, dc2=dc2, hf=hf: e.matmul(
                    pd[:, 0:512],
                    src_fm[:, hf * T + tc * 128: hf * T + tc * 128 + 128],
                    w_bf[:, hf * 1024 + dc2 * 512: hf * 1024 + dc2 * 512 + 512],
                    start=(hf == 0), stop=(bias_row is None and hf == DC - 1)),
                    deps=dep)
            if bias_row is not None:
                mm = pb.emit("tensor", lambda e, dc2=dc2: e.matmul(
                    pd[:, 0:512], bufs["onescol"][0:1, 0:128],
                    bias_row[0:1, dc2 * 512:dc2 * 512 + 512],
                    start=False, stop=True), deps=[])
            if x_dram is not None:
                xs = out_x[:, dc2 * 512: dc2 * 512 + 512]
            else:
                xs = out_x[:, tc * D + dc2 * 512: tc * D + dc2 * 512 + 512]
            xdep = [mm]
            if x_dram is not None:
                xdep.append(txl)
            if x_tok is not None:
                xdep.append(x_tok.get((tc, dc2), ("vector", 0)))
            if act is not None:
                ta = pb.emit("scalar", lambda e, dc2=dc2: e.activation(
                    scr[:, dc2 * 512:dc2 * 512 + 512], pd[:, 0:512], act),
                    deps=[mm] + scr_use.get(dc2, []))
                tacc = pb.emit("vector", lambda e, xs=xs, dc2=dc2: e.tensor_tensor(
                    xs, scr[:, dc2 * 512:dc2 * 512 + 512], xs, ALU.add),
                    deps=[ta] + xdep[1:], strict=True)
                scr_use[dc2] = [tacc]
                prev_use = ta
            else:
                tacc = pb.emit("vector", lambda e, xs=xs: e.tensor_tensor(
                    xs, pd[:, 0:512], xs, ALU.add), deps=xdep, strict=True)
                prev_use = tacc
            if x_tok is not None:
                x_tok[(tc, dc2)] = tacc
            toks.append(tacc)
        if x_dram is not None:
            tst = pb.dma("gpsimd", xo_dram[tc], out_x[:, 0:D], deps=toks[-2:])
            pb.tail[("prj_x",)] = (tst,)
            toks.append(tst)
    pb.tail[("ps_proj",)] = prev_use
    return toks


def emit_attn(pb, bufs, cfg):
    nq = cfg["nq_tc"]
    prefix = cfg.get("prefix", True)
    nkv = 2 * nq if prefix else nq
    T = nq * 128; KVT = nkv * 128
    QOFF = KVT // 2 if prefix else 0
    H = cfg["heads"]; dh = cfg["dh"]
    xkv = bufs["xkv_fm"]
    tanh = cfg.get("tanh", False)
    alibi = cfg.get("alibi", False)
    lora = cfg.get("lora", False)
    AFp = AF.Tanh if tanh else AF.Copy
    ps_q = bufs["ps_q"]
    ps_s = [bufs["ps_s0"], bufs["ps_s1"]] + ([bufs["ps_s2"]] if "ps_s2" in bufs else [])
    ps_o = bufs["ps_o"]
    pbuf = [bufs["pb0"], bufs["pb1"]] + ([bufs["pb2"]] if "pb2" in bufs else [])
    NSL = len(ps_s)
    PV_DELAY = NSL - 1
    ctx = bufs["ctx_tm"]
    HDall = H * dh
    xkv_tok = bufs["xkv_tok"]
    const_tok = bufs["const_tok"]

    psq_use = None
    t1_tok = ()
    if lora:
        t1 = {n: bufs["t1_" + n] for n in ("q", "k", "v")}
        last = None
        for name in ("q", "k", "v"):
            bT = bufs["bT_" + name]
            for tt in range(KVT // 512):
                mm = None
                for dc in range(DC):
                    dep = ([xkv_tok, const_tok] + list(bufs.get("lw_tok", ())) +
                           ([psq_use] if (psq_use and dc == 0) else []))
                    mm = pb.emit("tensor", lambda e, name=name, tt=tt, dc=dc, bT=bT: e.matmul(
                        ps_q[0:8, 0:512], bT[:, dc * 8:(dc + 1) * 8],
                        xkv[:, dc * KVT + tt * 512: dc * KVT + tt * 512 + 512],
                        start=(dc == 0), stop=(dc == DC - 1)), deps=dep)
                last = pb.emit("scalar", lambda e, name=name, tt=tt: e.activation(
                    t1[name][0:8, tt * 512:tt * 512 + 512], ps_q[0:8, 0:512],
                    AF.Copy, bias=0.0, scale=2.0), deps=[mm])
                psq_use = last
        t1_tok = (last,)

    qh = [bufs["qh0"], bufs["qh1"]]
    kh = [bufs["kh0"], bufs["kh1"]]
    vh = [bufs["vh0"], bufs["vh1"]]
    wstage = bufs["wstage"]
    whb = [bufs["wh0"], bufs["wh1"]]
    biasc = [bufs["bias0"], bufs["bias1"]]
    e0 = tuple(bufs.get("entry_tok", ()))
    head_free = [e0, e0]
    pexp = [None] * NSL
    ppv = [()] * NSL
    pso_use = ()
    x_toks = []
    single = cfg.get("single_hbuf", False)
    for h in range(H):
        hf = 0 if single else h % 2
        Q, K, V, W = qh[hf], kh[hf], vh[hf], whb[hf]
        wt = {}
        for wi, name in enumerate(("wq", "wk", "wv")):
            wt[name] = emit_weight_load_off(
                pb, cfg[name][h], wstage, W, 64, wi * 512,
                deps=head_free[hf], eng_dma="sync")
        if alibi:
            tb = pb.emit("vector", lambda e, h=h, hf=hf: e.scalar_tensor_tensor(
                biasc[hf][:, 0:nkv], bufs["kpos"][:, 0:nkv], -cfg["slopes"][h],
                bufs["kmask"][:, 0:nkv], ALU.mult, ALU.add),
                deps=[const_tok] + list(head_free[hf]))
        else:
            tb = pb.emit("vector", lambda e, hf=hf: e.tensor_copy(
                biasc[hf][:, 0:nkv], bufs["kmask"][:, 0:nkv]),
                deps=[const_tok] + list(head_free[hf]))
        q_toks = []
        k_toks = []
        v_toks = []
        for tt in range(T // 512):
            mm = None
            for dc in range(DC):
                dep = ([wt["wq"], xkv_tok] + list(head_free[hf]) +
                       ([psq_use] if (psq_use and dc == 0) else []))
                mm = pb.emit("tensor", lambda e, W=W, tt=tt, dc=dc: e.matmul(
                    ps_q[0:64, 0:512], W[:, dc * 64: dc * 64 + 64],
                    xkv[:, dc * KVT + QOFF + tt * 512: dc * KVT + QOFF + tt * 512 + 512],
                    start=(dc == 0), stop=(not lora and dc == DC - 1)), deps=dep)
            if lora:
                mm = pb.emit("tensor", lambda e, h=h, tt=tt: e.matmul(
                    ps_q[0:64, 0:512], bufs["aT_q"][0:8, h * 64:(h + 1) * 64],
                    bufs["t1_q"][0:8, QOFF + tt * 512: QOFF + tt * 512 + 512],
                    start=False, stop=True), deps=list(t1_tok) + list(bufs.get("lw_tok", ())))
            tq = pb.emit("scalar", lambda e, Q=Q, tt=tt, h=h: e.activation(
                Q[0:64, tt * 512:tt * 512 + 512], ps_q[0:64, 0:512], AFp,
                bias=(bufs["bqk"][0:64, 2 * h:2 * h + 1] if tanh else 0.0),
                scale=1.0), deps=[mm, bufs.get("bqk_tok", ("scalar", 0))], strict=True)
            psq_use = tq
            q_toks.append(tq)
        for tt in range(KVT // 512):
            mm = None
            for dc in range(DC):
                dep = [wt["wk"], xkv_tok] + list(head_free[hf]) + ([psq_use] if dc == 0 else [])
                mm = pb.emit("tensor", lambda e, W=W, tt=tt, dc=dc: e.matmul(
                    ps_q[0:64, 0:512], W[:, 512 + dc * 64: 512 + dc * 64 + 64],
                    xkv[:, dc * KVT + tt * 512: dc * KVT + tt * 512 + 512],
                    start=(dc == 0), stop=(not lora and dc == DC - 1)), deps=dep)
            if lora:
                mm = pb.emit("tensor", lambda e, h=h, tt=tt: e.matmul(
                    ps_q[0:64, 0:512], bufs["aT_k"][0:8, h * 64:(h + 1) * 64],
                    bufs["t1_k"][0:8, tt * 512: tt * 512 + 512],
                    start=False, stop=True), deps=list(t1_tok))
            tk = pb.emit("scalar", lambda e, K=K, tt=tt, h=h: e.activation(
                K[0:64, tt * 512:tt * 512 + 512], ps_q[0:64, 0:512], AFp,
                bias=(bufs["bqk"][0:64, 2 * h + 1:2 * h + 2] if tanh else 0.0),
                scale=1.0), deps=[mm, bufs.get("bqk_tok", ("scalar", 0))], strict=True)
            psq_use = tk
            k_toks.append(tk)
        tmset = pb.emit("vector", lambda e, V=V: e.memset(V[:, :], 1.0),
                        deps=list(head_free[hf]))
        for kc in range(nkv):
            mm = None
            for dc in range(DC):
                dep = [wt["wv"], xkv_tok, tmset] + ([psq_use] if dc == 0 else [])
                mm = pb.emit("tensor", lambda e, W=W, kc=kc, dc=dc: e.matmul(
                    ps_q[0:128, 0:64],
                    xkv[:, dc * KVT + kc * 128: dc * KVT + kc * 128 + 128],
                    W[:, 1024 + dc * 64: 1024 + dc * 64 + 64],
                    start=(dc == 0), stop=(not lora and not tanh and dc == DC - 1)), deps=dep)
            if lora:
                mm = pb.emit("tensor", lambda e, h=h, kc=kc: e.matmul(
                    ps_q[0:128, 0:64], bufs["t1_v"][0:8, kc * 128: kc * 128 + 128],
                    bufs["aT_v"][0:8, h * 64:(h + 1) * 64],
                    start=False, stop=(not tanh)), deps=list(t1_tok))
            if tanh:
                mm = pb.emit("tensor", lambda e, h=h: e.matmul(
                    ps_q[0:128, 0:64], bufs["onescol"][0:1, 0:128],
                    bufs["bv_rows"][0:1, h * 64:(h + 1) * 64],
                    start=False, stop=True), deps=[const_tok])
            elif not lora:
                pass
            tv = pb.emit("scalar", lambda e, V=V, kc=kc: e.activation(
                V[:, kc * 65: kc * 65 + 64], ps_q[0:128, 0:64], AFp), deps=[mm, tmset],
                strict=True)
            psq_use = tv
            v_toks.append(tv)
        qkv_tok = q_toks[-1]
        kv_done = [k_toks[-1], v_toks[-1], qkv_tok, tb]
        last_s = ()
        last_pv = ()
        qt0, qt1 = cfg.get("qt_range", (0, T // 512))
        ctx_base = cfg.get("ctx_base", 0)
        for qt in range(qt0, qt1):
            nkc = (nq if prefix else 0) + 4 * qt + 4
            pend = []
            pend_pv = []

            def emit_pv(kc, te, nkc=nkc):
                sl = kc % NSL
                for qs in range(4):
                    dep = [te]
                    if kc == 0:
                        dep += list(pso_use)
                    tpv = pb.emit("tensor", lambda e, V=V, sl=sl, qs=qs, kc=kc, nkc=nkc: e.matmul(
                        ps_o[qs][:, 0:65],
                        pbuf[sl][:, qs * 128:(qs + 1) * 128],
                        V[:, kc * 65: kc * 65 + 65],
                        start=(kc == 0), stop=(kc == nkc - 1)), deps=dep)
                    pend.append(tpv)
                ppv[sl] = (pend[-1],)

            for kc in range(nkc):
                sl = kc % NSL
                dep = [pexp[sl]] if pexp[sl] else []
                ts = pb.emit("tensor", lambda e, K=K, Q=Q, kc=kc, qt=qt, sl=sl: e.matmul(
                    ps_s[sl][:, 0:512], K[0:64, kc * 128: kc * 128 + 128],
                    Q[0:64, qt * 512: qt * 512 + 512],
                    start=True, stop=True), deps=dep + kv_done)
                band = kc - ((nq if prefix else 0) + 4 * qt)
                tm_ = ts
                if band >= 0:
                    tm_ = pb.emit("vector", lambda e, sl=sl, band=band: e.tensor_tensor(
                        ps_s[sl][:, 0:512], ps_s[sl][:, 0:512],
                        bufs["cmask"][:, band * 512:(band + 1) * 512], ALU.add),
                        deps=[ts], strict=True)
                te = pb.emit("scalar", lambda e, sl=sl, kc=kc, hf=hf: e.activation(
                    pbuf[sl][:, 0:512], ps_s[sl][:, 0:512], AF.Exp,
                    bias=biasc[hf][:, kc:kc + 1], scale=cfg.get("sscale", 0.125)),
                    deps=[tm_, tb] + list(ppv[sl]), strict=True)
                pexp[sl] = te
                pend_pv.append((kc, te))
                if len(pend_pv) > PV_DELAY:
                    kc_, te_ = pend_pv.pop(0)
                    emit_pv(kc_, te_)
            for kc_, te_ in pend_pv:
                emit_pv(kc_, te_)
            ev = []
            for qs in range(4):
                tcg = qt * 4 + qs
                tr = pb.emit("vector", lambda e, qs=qs: e.reciprocal(
                    bufs["rcol"][:, qs:qs + 1], ps_o[qs][:, 64:65]),
                    deps=[pend[-1]], strict=True)
                tn = pb.emit("vector", lambda e, qs=qs, tcg=tcg, h=h: e.tensor_scalar(
                    ctx[:, (tcg - ctx_base) * HDall + h * dh: (tcg - ctx_base) * HDall + (h + 1) * dh],
                    ps_o[qs][:, 0:64],
                    bufs["rcol"][:, qs:qs + 1], None, ALU.mult, ALU.bypass),
                    deps=[tr], strict=True)
                ev.append(tn)
            pso_use = tuple(ev)
            last_pv = pend[-1]
            x_toks.extend(ev)
        head_free[hf] = (last_pv,) + tuple(pso_use) + tuple(t for t in pexp if t)
    return x_toks


def build_attn_test(nq_tc=NTC, style="tanh"):
    nc = new_nc()
    es = ExitStack()
    T = nq_tc * 128
    KVT = 2 * T
    nkv = 2 * nq_tc
    H, dh = 16, 64
    tanh = (style == "tanh")
    lora = (style == "alibi")
    xin = nc.dram_tensor("xin", [nq_tc, 128, D], F32, kind="ExternalInput").ap()
    xprev = nc.dram_tensor("xprev", [nq_tc, 128, D], F32, kind="ExternalInput").ap()
    wq_d = nc.dram_tensor("wq", [H, DC, 128, dh], F32, kind="ExternalInput").ap()
    wk_d = nc.dram_tensor("wk", [H, DC, 128, dh], F32, kind="ExternalInput").ap()
    wv_d = nc.dram_tensor("wv", [H, DC, 128, dh], F32, kind="ExternalInput").ap()
    wo_d = nc.dram_tensor("wo", [DC, 128, D], F32, kind="ExternalInput").ap()
    ident = nc.dram_tensor("ident", [128, 128], BF16, kind="ExternalInput").ap()
    cmask_d = nc.dram_tensor("cmask", [128, 4 * 512], F32, kind="ExternalInput").ap()
    kmask_d = nc.dram_tensor("kmask", [128, nkv], F32, kind="ExternalInput").ap()
    kpos_d = nc.dram_tensor("kpos", [128, nkv], F32, kind="ExternalInput").ap()
    ones_d = nc.dram_tensor("onesc", [1, 128], BF16, kind="ExternalInput").ap()
    if tanh:
        bqk_d = nc.dram_tensor("bqk", [64, 2 * H], F32, kind="ExternalInput").ap()
        bvo_d = nc.dram_tensor("bvo", [1, 2 * D], BF16, kind="ExternalInput").ap()
    if lora:
        bT_d = {n2: nc.dram_tensor("bT" + n2, [DC, 128, 8], F32, kind="ExternalInput").ap()
                for n2 in ("q", "k", "v")}
        aT_d = {n2: nc.dram_tensor("aT" + n2, [8, D], F32, kind="ExternalInput").ap()
                for n2 in ("q", "k", "v")}
    xout = nc.dram_tensor("xout", [nq_tc, 128, D], F32, kind="ExternalOutput").ap()
    slopes = [2.0 ** (-0.5 - 0.5 * i) for i in range(H)] if lora else None

    with es:
        pb = PB(nc, es)
        stats = sb(pb, "stats", [128, 4 * nq_tc], F32)
        tmp_bf = sb(pb, "tmp_bf", [128, D], BF16)
        identb = sb(pb, "identb", [128, 128], BF16)
        onescol = sb(pb, "onescol", [1, 128], BF16)
        wstage = sb(pb, "wstage", [128, 1024], F32)
        rcol = sb(pb, "rcol", [128, 8], F32)
        kposb = sb(pb, "kposb", [128, nkv], F32)
        kmaskb = sb(pb, "kmaskb", [128, nkv], F32)
        cmaskb = sb(pb, "cmaskb", [128, 4 * 512], F32)
        ps_q = ps(pb, "ps_q", [128, 512])
        esKV = ExitStack()
        c_toks = [pb.dma("sync", identb[:, :], ident),
                  pb.dma("sync", onescol[0:1, :], ones_d),
                  pb.dma("sync", kposb[:, :], kpos_d),
                  pb.dma("sync", kmaskb[:, :], kmask_d),
                  pb.dma("sync", cmaskb[:, :], cmask_d)]
        bufs = dict(ps_q=ps_q,
                    onescol=onescol, wstage=wstage, rcol=rcol,
                    kpos=kposb, kmask=kmaskb, cmask=cmaskb)
        if tanh:
            bqk = sb(pb, "bqkb", [64, 2 * H], F32)
            bvo = sb(pb, "bvob", [1, 2 * D], BF16)
            c_toks += [pb.dma("sync", bqk[0:64, :], bqk_d),
                       pb.dma("sync", bvo[0:1, :], bvo_d)]
            bufs["bqk"] = bqk
            bufs["bqk_tok"] = c_toks[-2]
            bufs["bv_rows"] = bvo[0:1, 0:D]
        saved0 = pb.es
        pb.es = esKV
        xkv_fm = sb(pb, "xkv_fm", [128, DC * KVT], BF16)
        pb.es = saved0
        bufs["xkv_fm"] = xkv_fm
        esA = ExitStack()
        with esA:
            pbA_saved = pb.es
            pb.es = esA
            x = sb(pb, "x", [128, nq_tc * D], F32)
            dram_scr = sb(pb, "dram_scr", [128, 2 * D], F32)
            pt = (ps(pb, "ptA0", [128, 128], BF16), ps(pb, "ptA1", [128, 128], BF16))
            pb.es = pbA_saved
            x_toks = []
            for g in range(0, nq_tc, 4):
                gn = min(4, nq_tc - g)
                x_toks.append(pb.dma(
                    "gpsimd", x[:, g * D:(g + gn) * D].rearrange("p (t d) -> p t d", t=gn),
                    xin[g:g + gn].rearrange("t p d -> p t d")))
            ln1 = emit_ln_fm(pb, x, xkv_fm, stats, tmp_bf[:, 0:D], tmp_bf, identb, pt,
                             n_tc=nq_tc, tok_x=x_toks + [c_toks[0]],
                             dst_off=KVT // 2, dst_stride=KVT)
            ln2 = emit_ln_fm(pb, None, xkv_fm, stats, tmp_bf[:, 0:D], tmp_bf, identb, pt,
                             n_tc=nq_tc, tok_x=[ln1],
                             dst_off=0, dst_stride=KVT,
                             src_dram=[xprev[t] for t in range(nq_tc)],
                             dram_scr=dram_scr, dma_eng="gpsimd")
        entryB = pb.all_toks()
        esB = ExitStack()
        ctx_tm = sb(pb, "ctx_tm", [128, nq_tc * H * dh], BF16, side="right")
        with esB:
            saved = pb.es
            pb.es = esB
            ps_s0 = ps(pb, "ps_s0", [128, 512])
            ps_s1 = ps(pb, "ps_s1", [128, 512])
            ps_o = [ps(pb, "ps_o%d" % i, [128, 65]) for i in range(4)]
            bufs["ps_s0"] = ps_s0; bufs["ps_s1"] = ps_s1; bufs["ps_o"] = ps_o
            b2 = dict(
                qh0=sb(pb, "qh0", [64, T], BF16), qh1=sb(pb, "qh1", [64, T], BF16),
                kh0=sb(pb, "kh0", [64, KVT], BF16), kh1=sb(pb, "kh1", [64, KVT], BF16),
                vh0=sb(pb, "vh0", [128, nkv * 65], BF16), vh1=sb(pb, "vh1", [128, nkv * 65], BF16),
                wh0=sb(pb, "wh0", [128, 3 * 512], BF16), wh1=sb(pb, "wh1", [128, 3 * 512], BF16),
                pb0=sb(pb, "pb0", [128, 512], BF16), pb1=sb(pb, "pb1", [128, 512], BF16),
                bias0=sb(pb, "bias0", [128, nkv], F32), bias1=sb(pb, "bias1", [128, nkv], F32),
            )
            if lora:
                for n2 in ("q", "k", "v"):
                    b2["t1_" + n2] = sb(pb, "t1b" + n2, [8, KVT], BF16)
                    b2["bT_" + n2] = sb(pb, "bTb" + n2, [128, 64], BF16)
                    b2["aT_" + n2] = sb(pb, "aTb" + n2, [8, D], BF16)
            pb.es = saved
            bufs.update(b2)
            bufs["ctx_tm"] = ctx_tm
            bufs["xkv_tok"] = ln2
            bufs["const_tok"] = c_toks[-1]
            lw_toks = list(entryB)
            if lora:
                for n2 in ("q", "k", "v"):
                    tb1 = emit_weight_load_off(pb, [bT_d[n2][c] for c in range(DC)],
                                               wstage, bufs["bT_" + n2], 8, 0,
                                               deps=entryB, eng_dma="sync")
                    ta1 = pb.dma("sync", wstage[0:8, 0:D], aT_d[n2], deps=[tb1])
                    ta2 = pb.emit("scalar", lambda e, n2=n2: e.copy(
                        bufs["aT_" + n2][0:8, 0:D], wstage[0:8, 0:D]), deps=[ta1])
                    pb.tail[("stage", wstage.name)] = ta2
                    lw_toks.append(ta2)
            cfg = dict(nq_tc=nq_tc, heads=H, dh=dh, tanh=tanh, alibi=lora, lora=lora,
                       slopes=slopes,
                       wq=[[wq_d[h, c] for c in range(DC)] for h in range(H)],
                       wk=[[wk_d[h, c] for c in range(DC)] for h in range(H)],
                       wv=[[wv_d[h, c] for c in range(DC)] for h in range(H)])
            bufs["const_tok"] = c_toks[-1]
            bufs["lw_tok"] = tuple(lw_toks[len(entryB):]) if lora else ()
            attn_toks = emit_attn(pb, bufs, cfg)
        esKV.close()
        entryC = pb.all_toks()
        esC = ExitStack()
        with esC:
            saved = pb.es
            pb.es = esC
            pt = (ps(pb, "ptC0", [128, 128], BF16), ps(pb, "ptC1", [128, 128], BF16))
            ctx_fm = sb(pb, "ctx_fm", [128, DC * T], BF16)
            wo_bf = sb(pb, "wo_bf", [128, DC * 1024], BF16)
            projscr = sb(pb, "projscr", [128, 1024], F32)
            xr = sb(pb, "xr", [128, nq_tc * D], F32)
            pb.es = saved
            bufs["projscr"] = projscr
            xr_toks = []
            for g in range(0, nq_tc, 4):
                gn = min(4, nq_tc - g)
                xr_toks.append(pb.dma(
                    "gpsimd", xr[:, g * D:(g + gn) * D].rearrange("p (t d) -> p t d", t=gn),
                    xin[g:g + gn].rearrange("t p d -> p t d"), deps=entryC))
            wo_toks, wo_last = emit_weight_load(pb, [wo_d[c] for c in range(DC)],
                                                wstage, wo_bf, DC, 1024,
                                                deps=entryC, eng_dma="sync")
            fm_toks = emit_tm_to_fm(pb, ctx_tm, ctx_fm, identb, pt, n_tc=nq_tc,
                                    deps_per_tile=None, src_cols=H * dh,
                                    dst_cols=T, extra_deps=[attn_toks[-1]] + list(entryC))
            x_tok = {}
            for tc in range(nq_tc):
                for dc2 in range(2):
                    x_tok[(tc, dc2)] = xr_toks[tc // 4]
            pr_toks = emit_proj_residual(
                pb, bufs, ctx_fm, wo_bf, nq_tc, xr,
                act=(AF.Tanh if tanh else None),
                bias_row=(bvo[0:1, D:2 * D] if tanh else None),
                src_tok=[fm_toks[-1]], w_tok=[wo_last],
                x_tok=x_tok)
            st_toks = []
            for g in range(0, nq_tc, 4):
                gn = min(4, nq_tc - g)
                st_toks.append(pb.dma(
                    "gpsimd", xout[g:g + gn].rearrange("t p d -> p t d"),
                    xr[:, g * D:(g + gn) * D].rearrange("p (t d) -> p t d", t=gn),
                    deps=pr_toks))
            pb.emit("sync", lambda e: e.nop(), deps=st_toks)
        pb.build()
    return nc


def build_moe_l1_test(n_tc=NTC):
    nc = new_nc()
    es = ExitStack()
    T = n_tc * 128
    xin = nc.dram_tensor("xin", [n_tc, 128, D], F32, kind="ExternalInput").ap()
    gwT = nc.dram_tensor("gwT", [128, DC * 8], F32, kind="ExternalInput").ap()
    bias_d = nc.dram_tensor("ebias", [128, 8], F32, kind="ExternalInput").ap()
    gu = nc.dram_tensor("gu", [8, DC, 128, 2048], F32, kind="ExternalInput").ap()
    down = nc.dram_tensor("down", [8, DC, 128, 1024], F32, kind="ExternalInput").ap()
    sgu = nc.dram_tensor("sgu", [DC, 128, 2048], F32, kind="ExternalInput").ap()
    sdown = nc.dram_tensor("sdown", [DC, 128, 1024], F32, kind="ExternalInput").ap()
    ident = nc.dram_tensor("ident", [128, 128], BF16, kind="ExternalInput").ap()
    xout = nc.dram_tensor("xout", [n_tc, 128, D], F32, kind="ExternalOutput").ap()
    with es:
        pb = PB(nc, es)
        bufs = dict(
            x=sb(pb, "x", [128, n_tc * D], F32),
            xn_fm=sb(pb, "xn_fm", [128, DC * T], BF16),
            tmp_bf=sb(pb, "tmp_bf", [128, D], BF16),
            stats=sb(pb, "stats", [128, 4 * n_tc], F32),
            identb=sb(pb, "identb", [128, 128], BF16),
            gwT=sb(pb, "gwTb", [128, DC * 8], BF16),
            bias_bc=sb(pb, "bias_bc", [128, 8], F32),
            stage=sb(pb, "stage", [128, 1024], F32),
            stage2=sb(pb, "stage2", [128, 1024], F32),
            gu_bf0=sb(pb, "gu_bf0", [128, DC * 2048], BF16),
            gu_bf1=sb(pb, "gu_bf1", [128, DC * 2048], BF16),
            dn_bf0=sb(pb, "dn_bf0", [128, DC * 1024], BF16),
            hbuf0=sb(pb, "hbuf0", [128, 4096], BF16),
            hbuf1=sb(pb, "hbuf1", [128, 4096], BF16),
            sg=sb(pb, "sg", [128, 512], F32),
            routed_tm=sb(pb, "routed_tm", [128, n_tc * 8], F32),
            rscr=sb(pb, "rscr", [128, 64], F32),
            ps_r=ps(pb, "ps_r", [128, 8]),
            ps_g=ps(pb, "ps_g", [128, 512]),
            ps_u=ps(pb, "ps_u", [128, 512]),
            ps_d=ps(pb, "ps_d", [128, 1024]),
            pt=(ps(pb, "pt0", [128, 128], BF16), ps(pb, "pt1", [128, 128], BF16)),
        )
        tid = pb.dma("sync", bufs["identb"][:, :], ident)
        tbias = pb.dma("sync", bufs["bias_bc"][:, :], bias_d)
        bufs["bias_tok"] = tbias
        x_toks = []
        for g in range(0, n_tc, 4):
            gn = min(4, n_tc - g)
            x_toks.append(pb.dma(
                "sync",
                bufs["x"][:, g * D:(g + gn) * D].rearrange("p (t d) -> p t d", t=gn),
                xin[g:g + gn].rearrange("t p d -> p t d")))
        tgw = pb.dma("sync", bufs["rscr"][:, 0:64], gwT)
        tgwb = pb.emit("vector", lambda e: e.tensor_copy(bufs["gwT"][:, :], bufs["rscr"][:, 0:64]), deps=[tgw])
        fm_last = emit_ln_fm(pb, bufs["x"], bufs["xn_fm"], bufs["stats"],
                             bufs["tmp_bf"][:, 0:D],
                             bufs["tmp_bf"], bufs["identb"], bufs["pt"], n_tc=n_tc,
                             tok_x=x_toks + [tid, tgwb, tbias])
        wap = {
            "gu": [[gu[e_i, c // 2][:, (c % 2) * 1024:(c % 2) * 1024 + 1024] for c in range(DC * 2)] for e_i in range(8)],
            "down": [[down[e_i, c] for c in range(DC)] for e_i in range(8)],
            "sgu": [sgu[c // 2][:, (c % 2) * 1024:(c % 2) * 1024 + 1024] for c in range(DC * 2)],
            "sdown": [sdown[c] for c in range(DC)],
        }
        moe_toks = emit_moe_l0(pb, bufs, wap, n_tc=n_tc, fm_toks=[fm_last],
                               renorm=False, shared=True, l1=True)
        st_toks = []
        for g in range(0, n_tc, 4):
            gn = min(4, n_tc - g)
            st_toks.append(pb.dma(
                "gpsimd", xout[g:g + gn].rearrange("t p d -> p t d"),
                bufs["x"][:, g * D:(g + gn) * D].rearrange("p (t d) -> p t d", t=gn),
                deps=moe_toks))
        pb.emit("sync", lambda e: e.nop(), deps=st_toks)
        pb.build()
    return nc



def emit_ln_tm_f32(pb, x, out_tm, stats, sq_scr, n_tc=NTC, tok_x=()):
    n = n_tc
    t_red = []
    prev_sq = None
    for tc in range(n):
        xs = x[:, tc * D:(tc + 1) * D]
        dep1 = list(tok_x) + ([prev_sq] if prev_sq is not None else [])
        t1 = pb.emit("scalar", lambda e, xs=xs, tc=tc: e.activation(
            sq_scr[:, 0:D], xs, AF.Square, accum_out=stats[:, n + tc:n + tc + 1]),
            deps=dep1, strict=True)
        prev_sq = t1
        t2 = pb.emit("vector", lambda e, xs=xs, tc=tc: e.reduce_sum(
            stats[:, tc:tc + 1], xs, axis=AX.X), deps=tok_x)
        t_red += [t1, t2]
    tmu = pb.emit("vector", lambda e: e.tensor_scalar(
        stats[:, 2 * n:3 * n], stats[:, 0:n], 1.0 / D, None, ALU.mult, ALU.bypass),
        deps=t_red, strict=True)
    tm2 = pb.emit("vector", lambda e: e.tensor_tensor(
        stats[:, 0:n], stats[:, 2 * n:3 * n], stats[:, 2 * n:3 * n], ALU.mult),
        deps=[tmu], strict=True)
    tvar = pb.emit("vector", lambda e: e.scalar_tensor_tensor(
        stats[:, n:2 * n], stats[:, n:2 * n], 1.0 / D, stats[:, 0:n],
        ALU.mult, ALU.subtract), deps=[tm2], strict=True)
    tve = pb.emit("vector", lambda e: e.tensor_scalar(
        stats[:, n:2 * n], stats[:, n:2 * n], EPS_LN, None, ALU.add, ALU.bypass),
        deps=[tvar], strict=True)
    tln = pb.emit("scalar", lambda e: e.activation(
        stats[:, n:2 * n], stats[:, n:2 * n], AF.Ln, bias=0.0, scale=1.0),
        deps=[tve], strict=True)
    trstd = pb.emit("scalar", lambda e: e.activation(
        stats[:, 3 * n:4 * n], stats[:, n:2 * n], AF.Exp, bias=0.0, scale=-0.5),
        deps=[tln], strict=True)
    toks = []
    for tc in range(n):
        xs = x[:, tc * D:(tc + 1) * D]
        os = out_tm[:, tc * D:(tc + 1) * D]
        tt = pb.emit("vector", lambda e, xs=xs, os=os, tc=tc: e.tensor_scalar(
            os, xs, stats[:, 2 * n + tc:2 * n + tc + 1],
            stats[:, 3 * n + tc:3 * n + tc + 1], ALU.subtract, ALU.mult),
            deps=[trstd], strict=True)
        toks.append(tt)
    return toks


def emit_mingru(pb, bufs, cfg):
    n_tc = cfg["n_tc"]
    T = n_tc * 128
    L = cfg["layers"]
    x_mg = bufs["x_mg"]
    xn_fm = bufs["xn_fm"]
    W = bufs["w_bf"]
    psg, psu = bufs["ps_g"], bufs["ps_u"]
    ptF = bufs["ptF"]
    cbuf, ubuf, hbuf = bufs["cbuf"], bufs["ubuf"], bufs["hbuf"]
    zscr = bufs["zscr"]
    r1 = bufs["r1scr"]
    s1 = bufs["s1scr"]
    nb = bufs["nbcols"]
    bcols = bufs["bcols"]
    init = bufs["init"]
    hend = bufs["hend"]
    gb_tok = bufs["gb_tok"]
    x_toks = list(bufs["x_tok"])
    for l in range(L):
        g_bc, b_bc = bufs["g_bc"], bufs["b_bc"]
        bc_toks = []
        prev_bc = pb.tail.get(("mg_bc",), None)
        for which, dstt in ((0, g_bc), (1, b_bc)):
            for half in range(2):
                tm1 = pb.emit("tensor", lambda e, which=which, l=l, half=half, dstt=dstt: e.matmul(
                    bufs["ps_bc"][:, 0:512], bufs["onescol"][0:1, 0:128],
                    bufs["gbrows"][0:1, (which * L + l) * D + half * 512:
                                   (which * L + l) * D + half * 512 + 512],
                    start=True, stop=True),
                    deps=[gb_tok] + ([prev_bc] if prev_bc else []))
                prev_bc = pb.emit("scalar", lambda e, dstt=dstt, half=half: e.copy(
                    dstt[:, half * 512:half * 512 + 512], bufs["ps_bc"][:, 0:512]),
                    deps=[tm1])
                bc_toks.append(prev_bc)
        pb.tail[("mg_bc",)] = prev_bc
        lnt = emit_ln_fm(pb, x_mg, xn_fm, bufs["stats"], bufs["tmp_bf"][:, 0:D],
                         bufs["tmp_bf"], bufs["identb"], bufs["pt"], n_tc=n_tc,
                         tok_x=x_toks + bc_toks,
                         affine_wb=(g_bc, b_bc, bufs["lnscr"]))
        wl_tok = emit_weight_load_off(pb, cfg["w"][l], bufs["stage"], W, 1024, 0,
                                      deps=pb.tail.get(("mg_wfree",), ()) or (),
                                      eng_dma="sync")
        x_toks2 = []
        scan_toks = []
        for cc in range(DC):
            cu_toks = []
            for tt in range(n_tc // 4):
                mmg = mmh = None
                for dc in range(DC):
                    dep = [wl_tok, lnt]
                    if dc == 0 and ("mg_pg",) in pb.tail:
                        dep = dep + [pb.tail[("mg_pg",)]]
                    mmg = pb.emit("tensor", lambda e, l=l, cc=cc, tt=tt, dc=dc: e.matmul(
                        psg[:, 0:512],
                        W[:, dc * 2048 + cc * 128: dc * 2048 + cc * 128 + 128],
                        xn_fm[:, dc * T + tt * 512: dc * T + tt * 512 + 512],
                        start=(dc == 0), stop=(dc == DC - 1)), deps=dep)
                for dc in range(DC):
                    dep = [wl_tok, lnt]
                    if dc == 0 and ("mg_ph",) in pb.tail:
                        dep = dep + [pb.tail[("mg_ph",)]]
                    mmh = pb.emit("tensor", lambda e, l=l, cc=cc, tt=tt, dc=dc: e.matmul(
                        psu[:, 0:512],
                        W[:, dc * 2048 + 1024 + cc * 128: dc * 2048 + 1024 + cc * 128 + 128],
                        xn_fm[:, dc * T + tt * 512: dc * T + tt * 512 + 512],
                        start=(dc == 0), stop=(dc == DC - 1)), deps=dep)
                bcol = bcols[:, l * 16 + cc: l * 16 + cc + 1]
                nbcol = nb[:, l * 16 + cc: l * 16 + cc + 1]
                bcol2 = bcols[:, l * 16 + 8 + cc: l * 16 + 8 + cc + 1]
                tc1 = pb.emit("scalar", lambda e, cc=cc, tt=tt, nbcol=nbcol: e.activation(
                    cbuf[:, tt * 512:tt * 512 + 512], psg[:, 0:512], AF.Sigmoid,
                    bias=nbcol, scale=-1.0),
                    deps=[mmg, bufs["nb_tok"]] + list(pb.tail.get(("mg_cu",), ()))
                         + list(pb.tail.get(("mg_scan",), ())), strict=True)
                tz = pb.emit("scalar", lambda e, cc=cc, tt=tt, bcol=bcol: e.activation(
                    zscr[:, 0:512], psg[:, 0:512], AF.Sigmoid, bias=bcol, scale=1.0),
                    deps=[tc1, bufs["bc_tok"]], strict=True)
                pb.tail[("mg_pg",)] = tz
                tr1 = pb.emit("scalar", lambda e, bcol2=bcol2: e.activation(
                    r1[:, 0:512], psu[:, 0:512], AF.Relu, bias=bcol2, scale=1.0),
                    deps=[mmh, bufs["bc_tok"], tz], strict=True)
                ts1 = pb.emit("scalar", lambda e, bcol2=bcol2: e.activation(
                    s1[:, 0:512], psu[:, 0:512], AF.Sigmoid, bias=bcol2, scale=1.0),
                    deps=[tr1], strict=True)
                pb.tail[("mg_ph",)] = ts1
                tm1_ = pb.emit("vector", lambda e: e.tensor_scalar_min(
                    s1[:, 0:512], s1[:, 0:512], 0.5), deps=[ts1], strict=True)
                tg = pb.emit("vector", lambda e: e.tensor_tensor(
                    r1[:, 0:512], r1[:, 0:512], s1[:, 0:512], ALU.add),
                    deps=[tm1_, tr1], strict=True)
                tu = pb.emit("vector", lambda e, tt=tt: e.tensor_tensor(
                    ubuf[:, tt * 512:tt * 512 + 512], zscr[:, 0:512], r1[:, 0:512],
                    ALU.mult), deps=[tg, tz] + list(pb.tail.get(("mg_scan",), ())),
                    strict=True)
                pb.tail[("mg_cu",)] = (tu,)
                cu_toks.append(tu)
                cu_toks.append(tc1)
            icol = init[:, l * 8 + cc: l * 8 + cc + 1]
            tscan = pb.emit("vector", lambda e, cc=cc, icol=icol: e.tensor_tensor_scan(
                hbuf[:, 0:T], cbuf[:, 0:T], ubuf[:, 0:T], icol, ALU.mult, ALU.add),
                deps=cu_toks + [bufs["init_tok"]] + list(pb.tail.get(("mg_h",), ())),
                strict=True)
            scan_toks.append(tscan)
            pb.tail[("mg_scan",)] = (tscan,)
            th = pb.emit("vector", lambda e, l=l, cc=cc: e.tensor_copy(
                hend[:, l * 8 + cc: l * 8 + cc + 1], hbuf[:, T - 1:T]),
                deps=[tscan], strict=True)
            add_toks = []
            for tc2 in range(n_tc):
                ph = tc2 % 2
                tpt = pb.emit("tensor", lambda e, tc2=tc2, ph=ph: e.transpose(
                    ptF[ph][:, 0:128], hbuf[:, tc2 * 128:tc2 * 128 + 128],
                    bufs["identf"][:, 0:128]),
                    deps=[tscan] + list(pb.tail.get(("mg_pt%d" % ph,), ())))
                ta = pb.emit("vector", lambda e, tc2=tc2, cc=cc, ph=ph: e.tensor_tensor(
                    x_mg[:, tc2 * D + cc * 128: tc2 * D + cc * 128 + 128],
                    ptF[ph][:, 0:128],
                    x_mg[:, tc2 * D + cc * 128: tc2 * D + cc * 128 + 128], ALU.add),
                    deps=[tpt], strict=True)
                pb.tail[("mg_pt%d" % ph,)] = (ta,)
                add_toks.append(ta)
            pb.tail[("mg_h",)] = (add_toks[-1],) + (th,)
            x_toks2.extend(add_toks)
        pb.tail[("mg_wfree",)] = (x_toks2[-1],)
        x_toks = x_toks2
    return x_toks


def build_mingru_test(n_tc=NTC):
    nc = new_nc()
    es = ExitStack()
    T = n_tc * 128
    L = 3
    xin = nc.dram_tensor("xin", [n_tc, 128, D], F32, kind="ExternalInput").ap()
    w_d = nc.dram_tensor("w", [L, DC, 128, 2048], F32, kind="ExternalInput").ap()
    gb_d = nc.dram_tensor("gbrows", [1, 2 * L * D], BF16, kind="ExternalInput").ap()
    bc_d = nc.dram_tensor("bcols", [128, L * 16], F32, kind="ExternalInput").ap()
    init_d = nc.dram_tensor("initc", [128, L * 8], F32, kind="ExternalInput").ap()
    ident = nc.dram_tensor("ident", [128, 128], BF16, kind="ExternalInput").ap()
    identf_d = nc.dram_tensor("identf_d", [128, 128], F32, kind="ExternalInput").ap()
    ones_d = nc.dram_tensor("onesc", [1, 128], BF16, kind="ExternalInput").ap()
    xout = nc.dram_tensor("xout", [n_tc, 128, D], F32, kind="ExternalOutput").ap()
    hend_d = nc.dram_tensor("hend", [128, L * 8], F32, kind="ExternalOutput").ap()
    with es:
        pb = PB(nc, es)
        bufs = dict(
            x_mg=sb(pb, "x_mg", [128, n_tc * D], F32),
            tmp_bf=sb(pb, "tmp_bf", [128, D], BF16),
            stats=sb(pb, "stats", [128, 4 * n_tc], F32),
            identb=sb(pb, "identb", [128, 128], BF16),
            identf=sb(pb, "identf", [128, 128], F32),
            onescol=sb(pb, "onescol", [1, 128], BF16),
            gbrows=sb(pb, "gbrows_b", [1, 2 * L * D], BF16),
            bcols=sb(pb, "bcols_b", [128, L * 16], F32),
            nbcols=sb(pb, "nbcols", [128, L * 16], F32),
            init=sb(pb, "init_b", [128, L * 8], F32),
            hend=sb(pb, "hend_b", [128, L * 8], F32),
            g_bc=sb(pb, "g_bc", [128, D], F32),
            b_bc=sb(pb, "b_bc", [128, D], F32),
            lnscr=sb(pb, "lnscr", [128, D], F32),
            ps_g=ps(pb, "ps_g", [128, 512]),
            ps_u=ps(pb, "ps_u", [128, 512]),
            ps_bc=ps(pb, "ps_bc", [128, 512]),
            pt=(ps(pb, "pt0", [128, 128], BF16), ps(pb, "pt1", [128, 128], BF16)),
            ptF=(ps(pb, "ptF0", [128, 128], F32), ps(pb, "ptF1", [128, 128], F32)),
        )
        c1 = pb.dma("sync", bufs["identb"][:, :], ident)
        c2 = pb.dma("sync", bufs["identf"][:, :], identf_d)
        c3 = pb.dma("sync", bufs["onescol"][0:1, :], ones_d)
        c4 = pb.dma("sync", bufs["gbrows"][0:1, :], gb_d)
        c5 = pb.dma("sync", bufs["bcols"][:, :], bc_d)
        c6 = pb.dma("sync", bufs["init"][:, :], init_d)
        tnb = pb.emit("vector", lambda e: e.tensor_scalar(
            bufs["nbcols"][:, :], bufs["bcols"][:, :], -1.0, None, ALU.mult, ALU.bypass),
            deps=[c5])
        bufs["gb_tok"] = c4
        bufs["bc_tok"] = c5
        bufs["nb_tok"] = tnb
        bufs["init_tok"] = c6
        esA = ExitStack()
        with esA:
            saved = pb.es; pb.es = esA
            x = sb(pb, "x", [128, n_tc * D], F32)
            pb.es = saved
            x_toks = []
            for g in range(0, n_tc, 4):
                gn = min(4, n_tc - g)
                x_toks.append(pb.dma(
                    "gpsimd", x[:, g * D:(g + gn) * D].rearrange("p (t d) -> p t d", t=gn),
                    xin[g:g + gn].rearrange("t p d -> p t d")))
            ln_toks = emit_ln_tm_f32(pb, x, bufs["x_mg"], bufs["stats"],
                                     bufs["lnscr"], n_tc=n_tc, tok_x=x_toks + [c6])
        cfg = dict(n_tc=n_tc, layers=L,
                   w=[[w_d[l, c // 2][:, (c % 2) * 1024:(c % 2) * 1024 + 1024]
                       for c in range(DC * 2)] for l in range(L)])
        bufs["x_tok"] = ln_toks
        esB = ExitStack()
        with esB:
            saved = pb.es; pb.es = esB
            bufs["xn_fm"] = sb(pb, "xn_fm", [128, DC * T], BF16)
            bufs["w_bf"] = sb(pb, "w_bf", [128, DC * 2048], BF16)
            bufs["stage"] = sb(pb, "stage", [128, 2048], F32)
            bufs["cbuf"] = sb(pb, "cbuf", [128, T], F32)
            bufs["ubuf"] = sb(pb, "ubuf", [128, T], F32)
            bufs["hbuf"] = sb(pb, "hbuf", [128, T], F32)
            bufs["zscr"] = sb(pb, "zscr", [128, 512], F32)
            bufs["r1scr"] = sb(pb, "r1scr", [128, 512], F32)
            bufs["s1scr"] = sb(pb, "s1scr", [128, 512], F32)
            pb.es = saved
            mg_toks = emit_mingru(pb, bufs, cfg)
        entryC = pb.all_toks()
        esC = ExitStack()
        with esC:
            saved = pb.es; pb.es = esC
            xr = sb(pb, "xr", [128, n_tc * D], F32)
            pb.es = saved
            xr_toks = []
            for g in range(0, n_tc, 4):
                gn = min(4, n_tc - g)
                xr_toks.append(pb.dma(
                    "gpsimd", xr[:, g * D:(g + gn) * D].rearrange("p (t d) -> p t d", t=gn),
                    xin[g:g + gn].rearrange("t p d -> p t d"), deps=entryC))
            add_toks = []
            for tc in range(n_tc):
                ta = pb.emit("vector", lambda e, tc=tc: e.tensor_tensor(
                    xr[:, tc * D:(tc + 1) * D], bufs["x_mg"][:, tc * D:(tc + 1) * D],
                    xr[:, tc * D:(tc + 1) * D], ALU.add),
                    deps=[xr_toks[tc // 4]] + mg_toks[-1:], strict=True)
                add_toks.append(ta)
            st = []
            for g in range(0, n_tc, 4):
                gn = min(4, n_tc - g)
                st.append(pb.dma(
                    "gpsimd", xout[g:g + gn].rearrange("t p d -> p t d"),
                    xr[:, g * D:(g + gn) * D].rearrange("p (t d) -> p t d", t=gn),
                    deps=add_toks))
            st.append(pb.dma("sync", hend_d, bufs["hend"][:, :], deps=mg_toks))
            pb.emit("sync", lambda e: e.nop(), deps=st)
        pb.build()
    return nc



def emit_ln_stream(pb, bufs, src_aps, dst_aps, n_tc, tok0=()):
    scr = [bufs["lnt0"], bufs["lnt1"]]
    cs = bufs["cstats"]
    free = {0: (), 1: ()}
    toks = []
    for tc in range(n_tc):
        half = tc % 2
        s = scr[half]
        tl = pb.dma("gpsimd", s[:, 0:D], src_aps[tc], deps=list(tok0) + list(free[half]))
        t1 = pb.emit("scalar", lambda e, s=s: e.activation(
            bufs["sqscr"][:, 0:D], s[:, 0:D], AF.Square, accum_out=cs[:, 1:2]),
            deps=[tl] + list(pb.tail.get(("lns_sq",), ())), strict=True)
        pb.tail[("lns_sq",)] = (t1,)
        t2 = pb.emit("vector", lambda e, s=s: e.reduce_sum(cs[:, 0:1], s[:, 0:D], axis=AX.X),
                     deps=[tl] + list(pb.tail.get(("lns_cs",), ())), strict=True)
        t3 = pb.emit("vector", lambda e: e.tensor_scalar(cs[:, 2:3], cs[:, 0:1], 1.0 / D, None, ALU.mult, ALU.bypass), deps=[t2], strict=True)
        t4 = pb.emit("vector", lambda e: e.tensor_tensor(cs[:, 3:4], cs[:, 2:3], cs[:, 2:3], ALU.mult), deps=[t3], strict=True)
        t5 = pb.emit("vector", lambda e: e.scalar_tensor_tensor(cs[:, 4:5], cs[:, 1:2], 1.0 / D, cs[:, 3:4], ALU.mult, ALU.subtract), deps=[t4, t1], strict=True)
        t6 = pb.emit("vector", lambda e: e.tensor_scalar(cs[:, 4:5], cs[:, 4:5], EPS_LN, None, ALU.add, ALU.bypass), deps=[t5], strict=True)
        t7 = pb.emit("scalar", lambda e: e.activation(cs[:, 5:6], cs[:, 4:5], AF.Ln, bias=0.0, scale=1.0), deps=[t6], strict=True)
        t8 = pb.emit("scalar", lambda e: e.activation(cs[:, 6:7], cs[:, 5:6], AF.Exp, bias=0.0, scale=-0.5), deps=[t7], strict=True)
        t9 = pb.emit("vector", lambda e, s=s: e.tensor_scalar(
            s[:, 0:D], s[:, 0:D], cs[:, 2:3], cs[:, 6:7], ALU.subtract, ALU.mult),
            deps=[t8], strict=True)
        pb.tail[("lns_cs",)] = (t9,)
        ts = pb.dma("gpsimd", dst_aps[tc], s[:, 0:D], deps=[t9])
        free[half] = (ts,)
        toks.append(ts)
    return toks


def emit_retention_iter_p2(pb, bufs, cfg, ln1):
    n_tc = cfg["n_tc"]
    T = n_tc * 128
    H = 8
    HD = 128
    zi, zo, yd, x1 = cfg["zi"], cfg["zo"], cfg["yd"], cfg["x1"]
    xkv = bufs["xkv_fm"]
    it = cfg["iter"]
    entry0 = tuple(cfg.get("entry", ()))
    abuf = bufs["abuf"]
    obuf = bufs["obuf"]
    obuf2 = bufs["obuf2"]
    dmk = bufs["dmask"]
    gnw = bufs["gnw_bc"]
    gnb = bufs["gnb_bc"]
    ptB = bufs["ptB"]
    Q, K = bufs["q_fm"], bufs["k_fm"]
    QR = bufs["qrot"] if "qrot" in bufs else Q
    KR = bufs["krot"] if "krot" in bufs else K
    KH = bufs["khat"]
    V = bufs["v_tm"]
    TS1 = bufs["tscr1"]
    St = bufs["state"]
    Sbf = bufs["state_bf"]
    psA = bufs["ps_s0"]
    psB = bufs["ps_s1"]
    psO1 = bufs["ps_o1"]
    psO2 = bufs["ps_o2"]
    psSU = bufs["ps_su"]
    head_tok = entry0
    yd_toks = []
    whbs = [bufs["whb"], bufs.get("whb2", bufs["whb"])]
    wh_dbuf = whbs[1] is not whbs[0]
    whb_tok = {}
    whb_free = {0: (), 1: ()}

    def load_head_w(hh):
        wb_t = whbs[hh % 2] if wh_dbuf else whbs[0]
        dep = list(whb_free[hh % 2]) if wh_dbuf else list(head_tok)
        d = {}
        for wi, name in enumerate(("wq", "wk", "wv")):
            d[name] = emit_weight_load_off(pb, cfg[name][hh], bufs["wstage"],
                                           wb_t, 128, wi * 1024,
                                           deps=dep, eng_dma="sync")
        whb_tok[hh] = d

    load_head_w(0)
    for h in range(H):
        wl = whb_tok[h]
        wb = whbs[h % 2] if wh_dbuf else whbs[0]
        pq = {}
        for wi, (name, dst) in enumerate((("wq", Q), ("wk", K))):
            last = None
            for tt in range(T // 512):
                mm = None
                for dc2 in range(DC):
                    dep = [wl[name], ln1] + list(head_tok) + ([pq["last"]] if (dc2 == 0 and pq.get("last")) else [])
                    mm = pb.emit("tensor", lambda e, wi=wi, tt=tt, dc2=dc2, wb=wb: e.matmul(
                        psA[:, 0:512],
                        wb[:, wi * 1024 + dc2 * 128: wi * 1024 + dc2 * 128 + 128],
                        xkv[:, dc2 * T + tt * 512: dc2 * T + tt * 512 + 512],
                        start=(dc2 == 0), stop=(dc2 == DC - 1)), deps=dep)
                last = pb.emit("scalar", lambda e, dst=dst, tt=tt: e.copy(
                    dst[:, tt * 512:tt * 512 + 512], psA[:, 0:512]), deps=[mm], strict=True)
                pq[name] = last
                pq["last"] = last
        if wh_dbuf and h + 1 < H:
            load_head_w(h + 1)
        rot_toks = []
        rp = list(pb.tail.get(("ret_rot",), ()))
        tab = bufs["tab"]
        TSF = bufs["tscr1"]
        for pi, (src, dst) in enumerate(((Q, QR), (K, KR))):
            tld = [pb.dma("sync", tab[:, 0:T], cfg["tabs_d"][2 * pi],
                          deps=rp + list(head_tok) + [pq["last"]]),
                   pb.dma("sync", tab[:, T:2 * T], cfg["tabs_d"][2 * pi + 1],
                          deps=rp + list(head_tok) + [pq["last"]])]
            tsw1 = pb.dma("gpsimd", TSF[0:64, 0:T], src[64:128, 0:T],
                          deps=[pq["last"]] + rp + list(head_tok))
            tsw2 = pb.dma("gpsimd", TSF[64:128, 0:T], src[0:64, 0:T],
                          deps=[pq["last"]] + rp + list(head_tok))
            base = [pq["wq"], pq["wk"]] + tld + [tsw1, tsw2]
            tq1 = pb.emit("vector", lambda e, src=src, dst=dst, tab=tab: e.tensor_tensor(
                dst[:, 0:T], src[:, 0:T], tab[:, 0:T], ALU.mult),
                deps=base, strict=True)
            tq2 = pb.emit("vector", lambda e, tab=tab, TSF=TSF: e.tensor_tensor(
                TSF[:, 0:T], TSF[:, 0:T], tab[:, T:2 * T], ALU.mult),
                deps=base, strict=True)
            tq3 = pb.emit("vector", lambda e, dst=dst, TSF=TSF: e.tensor_tensor(
                dst[0:64, 0:T], dst[0:64, 0:T], TSF[0:64, 0:T], ALU.subtract),
                deps=[tq1, tq2], strict=True)
            tq4 = pb.emit("vector", lambda e, dst=dst, TSF=TSF: e.tensor_tensor(
                dst[64:128, 0:T], dst[64:128, 0:T], TSF[64:128, 0:T], ALU.add),
                deps=[tq3], strict=True)
            rot_toks.append(tq4)
            rp = [tq4]
        pb.tail[("ret_rot",)] = (rot_toks[-1],)
        pv = None
        kh_toks = []
        for kc in range(n_tc):
            mm = None
            for dc2 in range(DC):
                dep = [wl["wv"], ln1] + list(head_tok) + ([pv] if (dc2 == 0 and pv is not None) else [])
                mm = pb.emit("tensor", lambda e, kc=kc, dc2=dc2, wb=wb: e.matmul(
                    psB[:, 0:128],
                    xkv[:, dc2 * T + kc * 128: dc2 * T + kc * 128 + 128],
                    wb[:, 2 * 1024 + dc2 * 128: 2 * 1024 + dc2 * 128 + 128],
                    start=(dc2 == 0), stop=(dc2 == DC - 1)), deps=dep)
            pv = pb.emit("scalar", lambda e, kc=kc: e.copy(
                V[:, kc * 128:kc * 128 + 128], psB[:, 0:128]), deps=[mm], strict=True)
            tkt = pb.emit("tensor", lambda e, kc=kc, ptB=ptB: e.transpose(
                ptB[kc % 2][:, 0:128], KR[:, kc * 128:kc * 128 + 128],
                bufs["identb"][:, 0:128]),
                deps=rot_toks + list(pb.tail.get(("ret_kh%d" % (kc % 2),), ())))
            tke = pb.emit("vector", lambda e, kc=kc, h=h, ptB=ptB: e.tensor_scalar(
                KH[:, kc * 128:kc * 128 + 128], ptB[kc % 2][:, 0:128],
                bufs["ghcol"][:, h:h + 1], None, ALU.mult, ALU.bypass),
                deps=[tkt, cfg["dk_tok"]], strict=True)
            pb.tail[("ret_kh%d" % (kc % 2),)] = (tke,)
            kh_toks.append(tke)
        whb_free[h % 2] = (pv,)
        abufs = [abuf, bufs.get("abuf1", abuf)]
        obufs = [obuf, bufs.get("obuf1", obuf)]
        obuf2s = [obuf2, bufs.get("obuf21", obuf2)]
        psO1s = [psO1, bufs.get("ps_o1b", psO1)]
        gncs = bufs.get("gncs", bufs["cstats"])
        nslot = 2 if bufs.get("abuf1") is not None else 1
        tst = pb.emit("vector", lambda e: e.memset(St[:, :], 0.0), deps=list(head_tok))
        st_tok = tst
        sbf_tok = pb.emit("vector", lambda e: e.tensor_copy(Sbf[:, :], St[:, :]),
                          deps=[tst], strict=True)
        prev_A = {0: (), 1: ()}
        prev_o = {0: (), 1: ()}
        for kc in range(n_tc):
            s = kc % nslot
            ab = abufs[s]; ob = obufs[s]; ob2 = obuf2s[s]; pO1 = psO1s[s]
            cb = s * 8
            tA = pb.emit("tensor", lambda e, kc=kc: e.matmul(
                psA[:, 0:128], KR[:, kc * 128:kc * 128 + 128],
                QR[:, kc * 128:kc * 128 + 128], start=True, stop=True),
                deps=rot_toks + list(prev_A[s]) + [pq["last"]])
            tAm = pb.emit("vector", lambda e, kc=kc, h=h, ab=ab, dmk=dmk: e.tensor_tensor(
                ab[:, 0:128], psA[:, 0:128],
                dmk[:, h * 128:(h + 1) * 128], ALU.mult),
                deps=[tA, cfg["dk_tok"]] + list(pb.tail.get(("ret_ab", s), ())), strict=True)
            prev_A[s] = (tAm,)
            t1_ = pb.emit("tensor", lambda e, kc=kc, ab=ab, pO1=pO1: e.matmul(
                pO1[:, 0:128], ab[:, 0:128], V[:, kc * 128:kc * 128 + 128],
                start=True, stop=True), deps=[tAm, pv] + list(prev_o[s]))
            pb.tail[("ret_ab", s)] = (t1_,)
            t2_ = pb.emit("tensor", lambda e, kc=kc: e.matmul(
                psO2[:, 0:128], QR[:, kc * 128:kc * 128 + 128], Sbf[:, 0:128],
                start=True, stop=True), deps=[sbf_tok] + list(pb.tail.get(("ret_o2",), ()))),
            t2_ = t2_[0]
            t3_ = pb.emit("tensor", lambda e, kc=kc: e.matmul(
                psSU[:, 0:128], KH[:, kc * 128:kc * 128 + 128],
                V[:, kc * 128:kc * 128 + 128], start=True, stop=True),
                deps=[kh_toks[kc], pv] + list(pb.tail.get(("ret_su",), ())))
            to1 = pb.emit("vector", lambda e, h=h, ob=ob: e.tensor_scalar(
                ob[:, 0:128], psO2[:, 0:128], bufs["gcol"][:, h:h + 1],
                None, ALU.mult, ALU.bypass),
                deps=[t2_, cfg["dk_tok"]] + list(pb.tail.get(("ret_ob", s), ())),
                strict=True)
            pb.tail[("ret_o2",)] = (to1,)
            to2 = pb.emit("vector", lambda e, ob=ob, pO1=pO1: e.tensor_tensor(
                ob[:, 0:128], ob[:, 0:128], pO1[:, 0:128], ALU.add),
                deps=[to1, t1_], strict=True)
            prev_o[s] = (to2,)
            ts1_ = pb.emit("vector", lambda e, h=h: e.scalar_tensor_tensor(
                St[:, :], St[:, :], cfg["gC"][h], psSU[:, 0:128], ALU.mult, ALU.add),
                deps=[t3_, st_tok, sbf_tok], strict=True)
            pb.tail[("ret_su",)] = (ts1_,)
            st_tok = ts1_
            sbf_tok = pb.emit("vector", lambda e: e.tensor_copy(Sbf[:, :], St[:, :]),
                              deps=[ts1_, t2_], strict=True)
            g1 = pb.emit("vector", lambda e, ob=ob, cb=cb: e.reduce_sum(
                gncs[:, cb:cb + 1], ob[:, 0:128], axis=AX.X),
                deps=[to2] + list(pb.tail.get(("ret_gn", s), ())), strict=True)
            g2 = pb.emit("scalar", lambda e, ob=ob, cb=cb, s=s: e.activation(
                bufs["sqscr"][:, s * 128:s * 128 + 128], ob[:, 0:128], AF.Square,
                accum_out=gncs[:, cb + 1:cb + 2]),
                deps=[to2] + list(pb.tail.get(("ret_gn_sq", s), ()))
                     + list(pb.tail.get(("ret_gn", s), ())), strict=True)
            pb.tail[("ret_gn_sq", s)] = (g2,)
            g3 = pb.emit("vector", lambda e, cb=cb: e.tensor_scalar(
                gncs[:, cb + 2:cb + 3], gncs[:, cb:cb + 1], 1.0 / HD, None,
                ALU.mult, ALU.bypass), deps=[g1], strict=True)
            g4 = pb.emit("vector", lambda e, cb=cb: e.tensor_tensor(
                gncs[:, cb + 3:cb + 4], gncs[:, cb + 2:cb + 3], gncs[:, cb + 2:cb + 3],
                ALU.mult), deps=[g3], strict=True)
            g5 = pb.emit("vector", lambda e, cb=cb: e.scalar_tensor_tensor(
                gncs[:, cb + 4:cb + 5], gncs[:, cb + 1:cb + 2], 1.0 / HD,
                gncs[:, cb + 3:cb + 4], ALU.mult, ALU.subtract), deps=[g4, g2], strict=True)
            g6 = pb.emit("vector", lambda e, cb=cb: e.tensor_scalar(
                gncs[:, cb + 4:cb + 5], gncs[:, cb + 4:cb + 5], EPS_LN, None,
                ALU.add, ALU.bypass), deps=[g5], strict=True)
            g7 = pb.emit("scalar", lambda e, cb=cb: e.activation(
                gncs[:, cb + 5:cb + 6], gncs[:, cb + 4:cb + 5], AF.Ln, bias=0.0, scale=1.0),
                deps=[g6], strict=True)
            g8 = pb.emit("scalar", lambda e, cb=cb: e.activation(
                gncs[:, cb + 6:cb + 7], gncs[:, cb + 5:cb + 6], AF.Exp, bias=0.0, scale=-0.5),
                deps=[g7], strict=True)
            g9 = pb.emit("vector", lambda e, ob=ob, cb=cb: e.tensor_scalar(
                ob[:, 0:128], ob[:, 0:128], gncs[:, cb + 2:cb + 3],
                gncs[:, cb + 6:cb + 7], ALU.subtract, ALU.mult), deps=[g8], strict=True)
            pb.tail[("ret_gn", s)] = (g9,)
            g10 = pb.emit("vector", lambda e, h=h, ob=ob, gnw=gnw: e.tensor_tensor(
                ob[:, 0:128], ob[:, 0:128],
                gnw[:, h * 128:(h + 1) * 128], ALU.mult),
                deps=[g9, cfg["gn_tok"]], strict=True)
            g11 = pb.emit("vector", lambda e, h=h, ob=ob, ob2=ob2, gnb=gnb: e.tensor_tensor(
                ob2[:, 0:128], ob[:, 0:128],
                gnb[:, h * 128:(h + 1) * 128], ALU.add),
                deps=[g10] + list(pb.tail.get(("ret_st", s), ())), strict=True)
            pb.tail[("ret_ob", s)] = (g11,)
            tstore = pb.dma("gpsimd", yd[kc][:, h * 128:h * 128 + 128],
                            ob2[:, 0:128], deps=[g11])
            pb.tail[("ret_st", s)] = (tstore,)
            yd_toks.append(tstore)
        head_tok = (yd_toks[-2] if n_tc > 1 else yd_toks[-1], yd_toks[-1], sbf_tok) + tuple(rot_toks)
    return ln1, yd_toks


def emit_gb_bcast(pb, bufs, g_off, b_off, dst_g=None, dst_b=None, cols=D):
    dst_g = dst_g if dst_g is not None else bufs["g_bc"]
    dst_b = dst_b if dst_b is not None else bufs["b_bc"]
    psbc = bufs["ps_bc"]
    onesc = bufs["onescol"]
    gbr = bufs["gbrows"]
    prev = list(pb.tail.get(("gb_bc",), ())) + [pb.barrier_tok("vector")]
    toks = []
    for off, dstt in ((g_off, dst_g), (b_off, dst_b)):
        for half in range(cols // 512):
            tm1 = pb.emit("tensor", lambda e, off=off, half=half, psbc=psbc, gbr=gbr, onesc=onesc: e.matmul(
                psbc[:, 0:512], onesc[0:1, 0:128],
                gbr[0:1, off + half * 512: off + half * 512 + 512],
                start=True, stop=True),
                deps=[bufs["gb_tok"]] + prev)
            tcp = pb.emit("scalar", lambda e, dstt=dstt, half=half, psbc=psbc: e.copy(
                dstt[:, half * 512:half * 512 + 512], psbc[:, 0:512]),
                deps=[tm1])
            prev = [tcp]
            toks.append(tcp)
    pb.tail[("gb_bc",)] = (toks[-1],)
    return toks


def emit_retention_p3(pb, bufs, cfg, yd_toks, ln1_tok):
    n_tc = cfg["n_tc"]
    T = n_tc * 128
    xkv = bufs["xkv_fm"]
    wg, wo = bufs["wg_bf"], bufs["wo_bf"]
    psA = bufs["ps_s0"]
    psB = bufs["ps_s1"]
    ytile = bufs["ytile"]
    gtile = bufs["gtile"]
    yfm = bufs["yfm"]
    ztile = bufs["zt"]
    ptB = bufs["ptB"]
    x1_toks = []
    ydep = [yd_toks[-1]]
    for tc in range(n_tc):
        ty = pb.dma("gpsimd", ytile[:, 0:D], cfg["yd"][tc],
                    deps=ydep + list(pb.tail.get(("p3_y",), ())))
        tz = pb.dma("sync", ztile[:, 0:D], cfg["zi"][tc],
                    deps=list(pb.tail.get(("p3_z",), ())))
        tr_toks = []
        for dc2 in range(2):
            mm = None
            for dc in range(DC):
                dep = [cfg["wg_tok"], ln1_tok] + (list(pb.tail.get(("p3_g",), ())) if dc == 0 else [])
                mm = pb.emit("tensor", lambda e, tc=tc, dc=dc, dc2=dc2: e.matmul(
                    psA[:, 0:512], xkv[:, dc * T + tc * 128: dc * T + tc * 128 + 128],
                    wg[:, dc * 1024 + dc2 * 512: dc * 1024 + dc2 * 512 + 512],
                    start=(dc == 0), stop=(dc == DC - 1)), deps=dep)
            tg = pb.emit("scalar", lambda e: e.activation(
                gtile[:, 0:512], psA[:, 0:512], AF.Silu),
                deps=[mm] + list(pb.tail.get(("p3_gt",), ())), strict=True)
            tmul = pb.emit("vector", lambda e, dc2=dc2: e.tensor_tensor(
                ytile[:, dc2 * 512:dc2 * 512 + 512], gtile[:, 0:512],
                ytile[:, dc2 * 512:dc2 * 512 + 512], ALU.mult),
                deps=[tg, ty], strict=True)
            pb.tail[("p3_g",)] = (tmul,)
            pb.tail[("p3_gt",)] = (tmul,)
            tr_toks.append(tmul)
        tps = []
        for dc in range(DC):
            ph = dc % 2
            tt1 = pb.emit("tensor", lambda e, dc=dc, ph=ph, ptB=ptB: e.transpose(
                ptB[ph][:, 0:128], ytile[:, dc * 128:dc * 128 + 128],
                bufs["identb"][:, 0:128]),
                deps=tr_toks + list(pb.tail.get(("p3_pt%d" % ph,), ())))
            tt2 = pb.emit("scalar", lambda e, dc=dc, ph=ph, ptB=ptB: e.copy(
                yfm[:, dc * 128:dc * 128 + 128], ptB[ph][:, 0:128]), deps=[tt1],
                strict=True)
            pb.tail[("p3_pt%d" % ph,)] = (tt2,)
            tps.append(tt2)
        pb.tail[("p3_y",)] = (tps[-1],)
        st_toks = []
        for dc2 in range(2):
            mm = None
            for cf in range(DC):
                dep = [cfg["wo_tok"], tps[-1]] + (list(pb.tail.get(("p3_wo%d" % dc2,), ())) if cf == 0 else [])
                mm = pb.emit("tensor", lambda e, cf=cf, dc2=dc2: e.matmul(
                    psB[:, dc2 * 512:dc2 * 512 + 512], yfm[:, cf * 128:cf * 128 + 128],
                    wo[:, cf * 1024 + dc2 * 512: cf * 1024 + dc2 * 512 + 512],
                    start=(cf == 0), stop=(cf == DC - 1)), deps=dep)
            ta = pb.emit("vector", lambda e, dc2=dc2: e.tensor_tensor(
                ztile[:, dc2 * 512:dc2 * 512 + 512], psB[:, dc2 * 512:dc2 * 512 + 512],
                ztile[:, dc2 * 512:dc2 * 512 + 512], ALU.add),
                deps=[mm, tz], strict=True)
            pb.tail[("p3_wo%d" % dc2,)] = (ta,)
            st_toks.append(ta)
        tst = pb.dma("sync", cfg["x1"][tc], ztile[:, 0:D], deps=st_toks)
        pb.tail[("p3_z",)] = (tst,)
        x1_toks.append(tst)
    return x1_toks


def emit_retention_p4(pb, bufs, cfg, x1_toks):
    n_tc = cfg["n_tc"]
    gb = emit_gb_bcast(pb, bufs, cfg["ln2g_off"], cfg["ln2b_off"])
    w1, w2 = bufs["w1_bf"], bufs["w2_bf"]
    psHs = [bufs["ps_s1"], bufs.get("ps_s1b", bufs["ps_s1"])]
    psO = bufs["ps_o2b"]
    h1cs = [bufs["h1c"], bufs.get("h1c2", bufs["h1c"])]
    zts = [bufs["zt"], bufs.get("zt2", bufs["zt"])]
    nzs = 2 if bufs.get("zt2") is not None else 1
    nhs = 2 if bufs.get("ps_s1b") is not None else 1
    cs_w = bufs.get("p4cs", bufs["cstats"])
    sq_w = bufs.get("p4sq")
    ln_w = bufs.get("p4ln")
    tmp_w = bufs.get("p4tmp")
    w1_toks = cfg.get("w1_toks")
    w2_toks = cfg.get("w2_toks")
    ptp = bufs["pt"]
    b1v = bufs["b1cols_view"]
    b2r = bufs["b2row"]
    zo_toks = []
    slotted = cs_w is not bufs["cstats"]
    for tc in range(n_tc):
        zs = tc % nzs
        ztile = zts[zs]
        sl = (tc % 2) if slotted else 0
        cb = sl * 8
        sqv = sq_w[:, sl * D:sl * D + D] if sq_w is not None else bufs["sqscr"][:, 0:D]
        lnv = sqv if sq_w is not None else bufs["lnscr"][:, 0:D]
        tmv = tmp_w[:, sl * D:sl * D + D] if tmp_w is not None else bufs["tmp_bf"][:, 0:D]
        tz = pb.dma("sync", ztile[:, 0:D], cfg["x1"][tc],
                    deps=list(pb.tail.get(("p4_z", zs), ())) + list(pb.tail.get(("p3_z",), ()))
                         + [x1_toks[tc]])
        t1 = pb.emit("scalar", lambda e, ztile=ztile, sqv=sqv, cb=cb: e.activation(
            sqv, ztile[:, 0:D], AF.Square, accum_out=cs_w[:, cb + 1:cb + 2]),
            deps=[tz] + list(pb.tail.get(("p4_sq", sl), ())), strict=True)
        pb.tail[("p4_sq", sl)] = (t1,)
        t2 = pb.emit("vector", lambda e, ztile=ztile, cb=cb: e.reduce_sum(
            cs_w[:, cb:cb + 1], ztile[:, 0:D], axis=AX.X),
            deps=[tz] + list(pb.tail.get(("p4_cs", sl), ())), strict=True)
        t3 = pb.emit("vector", lambda e, cb=cb: e.tensor_scalar(
            cs_w[:, cb + 2:cb + 3], cs_w[:, cb:cb + 1], 1.0 / D, None, ALU.mult, ALU.bypass),
            deps=[t2], strict=True)
        t4 = pb.emit("vector", lambda e, cb=cb: e.tensor_tensor(
            cs_w[:, cb + 3:cb + 4], cs_w[:, cb + 2:cb + 3], cs_w[:, cb + 2:cb + 3], ALU.mult),
            deps=[t3], strict=True)
        t5 = pb.emit("vector", lambda e, cb=cb: e.scalar_tensor_tensor(
            cs_w[:, cb + 4:cb + 5], cs_w[:, cb + 1:cb + 2], 1.0 / D, cs_w[:, cb + 3:cb + 4],
            ALU.mult, ALU.subtract), deps=[t4, t1], strict=True)
        t6 = pb.emit("vector", lambda e, cb=cb: e.tensor_scalar(
            cs_w[:, cb + 4:cb + 5], cs_w[:, cb + 4:cb + 5], EPS_LN, None, ALU.add, ALU.bypass),
            deps=[t5], strict=True)
        t7 = pb.emit("scalar", lambda e, cb=cb: e.activation(
            cs_w[:, cb + 5:cb + 6], cs_w[:, cb + 4:cb + 5], AF.Ln, bias=0.0, scale=1.0),
            deps=[t6], strict=True)
        t8 = pb.emit("scalar", lambda e, cb=cb: e.activation(
            cs_w[:, cb + 6:cb + 7], cs_w[:, cb + 5:cb + 6], AF.Exp, bias=0.0, scale=-0.5),
            deps=[t7], strict=True)
        t9 = pb.emit("vector", lambda e, ztile=ztile, lnv=lnv, cb=cb: e.tensor_scalar(
            lnv, ztile[:, 0:D], cs_w[:, cb + 2:cb + 3], cs_w[:, cb + 6:cb + 7],
            ALU.subtract, ALU.mult),
            deps=[t8, t5] + list(pb.tail.get(("p4_ln", sl), ())), strict=True)
        t10 = pb.emit("vector", lambda e, lnv=lnv: e.tensor_tensor(
            lnv, lnv, bufs["g_bc"][:, 0:D], ALU.mult),
            deps=[t9] + list(gb), strict=True)
        t11 = pb.emit("vector", lambda e, lnv=lnv, tmv=tmv: e.tensor_tensor(
            tmv, lnv, bufs["b_bc"][:, 0:D], ALU.add),
            deps=[t10] + list(pb.tail.get(("p4_tmp", sl), ())), strict=True)
        pb.tail[("p4_ln", sl)] = (t11,)
        if sq_w is not None:
            pb.tail[("p4_sq", sl)] = (t1, t11)
        xnb = bufs.get("p4xn")
        xnv = xnb[:, sl * D: sl * D + D] if xnb is not None else bufs["xn2t"][:, 0:D]
        tps = []
        for dc in range(DC):
            ph = dc % 2
            tt1 = pb.emit("tensor", lambda e, dc=dc, ph=ph, ptp=ptp, tmv=tmv: e.transpose(
                ptp[ph][:, 0:128], tmv[:, dc * 128:dc * 128 + 128],
                bufs["identb"][:, 0:128]),
                deps=[t11] + list(pb.tail.get(("p4_pt%d" % ph,), ()))
                     + list(pb.tail.get(("p4_xnr", sl), ())))
            tt2 = pb.emit("scalar", lambda e, dc=dc, ph=ph, ptp=ptp, xnv=xnv: e.copy(
                xnv[:, dc * 128:dc * 128 + 128], ptp[ph][:, 0:128]), deps=[tt1],
                strict=True)
            pb.tail[("p4_pt%d" % ph,)] = (tt2,)
            tps.append(tt2)
        pb.tail[("p4_tmp", sl)] = (tps[-1],)
        xn_tok = tps[-1]
        mm_last = None
        for hf in range(32):
            hs = hf % nhs
            psH = psHs[hs]
            h1c = h1cs[hs]
            mm = None
            for dc in range(DC):
                wdep = ([w1_toks[dc * 4 + hf // 8]] if w1_toks is not None
                        else [cfg["w1_tok"]])
                dep = wdep + [xn_tok] + (list(pb.tail.get(("p4_h", hs), ())) if dc == 0 else [])
                mm = pb.emit("tensor", lambda e, tc=tc, hf=hf, dc=dc, psH=psH, xnv=xnv: e.matmul(
                    psH[:, 0:128],
                    w1[:, dc * 4096 + hf * 128: dc * 4096 + hf * 128 + 128],
                    xnv[:, dc * 128: dc * 128 + 128],
                    start=(dc == 0), stop=(dc == DC - 1)), deps=dep)
            th = pb.emit("scalar", lambda e, hf=hf, psH=psH, h1c=h1c: e.activation(
                h1c[:, 0:128], psH[:, 0:128], AF.Silu,
                bias=b1v[:, hf:hf + 1], scale=1.0),
                deps=[mm, cfg["b1_tok"]] + list(pb.tail.get(("p4_h2", hs), ())), strict=True)
            pb.tail[("p4_h", hs)] = (th,)
            wmms = []
            for dc2 in range(2):
                w2dep = ([w2_toks[hf]] if w2_toks is not None else [cfg["w2_tok"]])
                wmm = pb.emit("tensor", lambda e, hf=hf, dc2=dc2, h1c=h1c: e.matmul(
                    psO[:, dc2 * 512:dc2 * 512 + 512],
                    h1c[:, 0:128],
                    w2[:, hf * 1024 + dc2 * 512: hf * 1024 + dc2 * 512 + 512],
                    start=(hf == 0), stop=False),
                    deps=([th] + w2dep + (list(pb.tail.get(("p4_o",), ())) if hf == 0 else [])))
                wmms.append(wmm)
            pb.tail[("p4_h2", hs)] = (wmms[-1],)
            mm_last = wmms[-1]
        pb.tail[("p4_xnr", sl)] = (mm_last,)
        x0aps = cfg.get("x0")
        if x0aps is not None:
            x0s = bufs["x0scr"]
            tx0 = pb.dma("gpsimd", x0s[:, 0:D], x0aps[tc],
                         deps=list(pb.tail.get(("p4_x0",), ())))
        ev_toks = []
        for dc2 in range(2):
            bmm = pb.emit("tensor", lambda e, dc2=dc2: e.matmul(
                psO[:, dc2 * 512:dc2 * 512 + 512], bufs["onescol"][0:1, 0:128],
                b2r[0:1, dc2 * 512:dc2 * 512 + 512],
                start=False, stop=True), deps=[mm_last, cfg["b2_tok"]])
            ta = pb.emit("vector", lambda e, dc2=dc2, tc=tc, ztile=ztile: e.tensor_tensor(
                ztile[:, dc2 * 512:dc2 * 512 + 512],
                psO[:, dc2 * 512:dc2 * 512 + 512],
                ztile[:, dc2 * 512:dc2 * 512 + 512], ALU.add),
                deps=[bmm, tz], strict=True)
            if x0aps is not None:
                ta = pb.emit("vector", lambda e, dc2=dc2, x0s=x0s, ztile=ztile: e.tensor_tensor(
                    ztile[:, dc2 * 512:dc2 * 512 + 512], x0s[:, dc2 * 512:dc2 * 512 + 512],
                    ztile[:, dc2 * 512:dc2 * 512 + 512], ALU.add),
                    deps=[ta, tx0], strict=True)
            ev_toks.append(ta)
        if x0aps is not None:
            pb.tail[("p4_x0",)] = tuple(ev_toks)
        pb.tail[("p4_o",)] = tuple(ev_toks)
        tst = pb.dma("sync", cfg["zo"][tc], ztile[:, 0:D], deps=ev_toks)
        pb.tail[("p4_z", zs)] = (tst,)
        zo_toks.append(tst)
    return zo_toks


def build_retention_test(n_tc=32, iters=1, outer_ln=False, final_add=False,
                         it_base=0):
    nc = new_nc()
    es = ExitStack()
    T = n_tc * 128
    H = 8
    zin = nc.dram_tensor("zin", [n_tc, 128, D], F32, kind="ExternalInput").ap()
    x0_d = nc.dram_tensor("x0", [n_tc, 128, D], F32, kind="ExternalInput").ap() if final_add else None
    z0_d = nc.dram_tensor("z0_scr", [n_tc, 128, D], F32).ap() if outer_ln else None
    wq_d = nc.dram_tensor("wq", [iters, H, DC, 128, 128], F32, kind="ExternalInput").ap()
    wk_d = nc.dram_tensor("wk", [iters, H, DC, 128, 128], F32, kind="ExternalInput").ap()
    wv_d = nc.dram_tensor("wv", [iters, H, DC, 128, 128], F32, kind="ExternalInput").ap()
    wg_d = nc.dram_tensor("wg", [iters, DC, 128, D], F32, kind="ExternalInput").ap()
    wo_d = nc.dram_tensor("wo", [iters, DC, 128, D], F32, kind="ExternalInput").ap()
    w1_d = nc.dram_tensor("w1", [iters, DC, 128, 4096], F32, kind="ExternalInput").ap()
    w2_d = nc.dram_tensor("w2", [iters, 32, 128, D], F32, kind="ExternalInput").ap()
    gb_d = nc.dram_tensor("gbrows", [1, iters * 6 * D], BF16, kind="ExternalInput").ap()
    b1_d = nc.dram_tensor("b1cols", [128, iters * 32], F32, kind="ExternalInput").ap()
    b2_d = nc.dram_tensor("b2rows", [1, iters * D], BF16, kind="ExternalInput").ap()
    tabs_d = nc.dram_tensor("xpos", [4, 128, T], BF16, kind="ExternalInput").ap()
    dmask_d = nc.dram_tensor("dmask", [128, H * 128], F32, kind="ExternalInput").ap()
    gcols_d = nc.dram_tensor("gcols", [128, 2 * H], F32, kind="ExternalInput").ap()
    ident = nc.dram_tensor("ident", [128, 128], BF16, kind="ExternalInput").ap()
    ones_d = nc.dram_tensor("onesc", [1, 128], BF16, kind="ExternalInput").ap()
    zout = nc.dram_tensor("zout", [n_tc, 128, D], F32, kind="ExternalOutput").ap()
    yd_d = nc.dram_tensor("yd_scr", [n_tc, 128, D], BF16).ap()
    x1_d = nc.dram_tensor("x1_scr", [n_tc, 128, D], F32).ap()
    z1_d = nc.dram_tensor("z1_scr", [n_tc, 128, D], F32).ap()

    with es:
        pb = PB(nc, es)
        bufs = dict(
            stats=sb(pb, "stats", [128, 4 * n_tc], F32),
            tmp_bf=sb(pb, "tmp_bf", [128, D], BF16),
            identb=sb(pb, "identb", [128, 128], BF16),
            onescol=sb(pb, "onescol", [1, 128], BF16),
            gbrows=sb(pb, "gbrows_b", [1, 6 * D], BF16),
            b1cols=sb(pb, "b1cols_b", [128, iters * 32], F32),
            b2rows=sb(pb, "b2rows_b", [1, iters * D], BF16),
            g_bc=sb(pb, "g_bc", [128, D], F32),
            b_bc=sb(pb, "b_bc", [128, D], F32),
            gnw_bc=sb(pb, "gnw_bc", [128, H * 128], F32),
            gnb_bc=sb(pb, "gnb_bc", [128, H * 128], F32),
            lnscr=sb(pb, "lnscr", [128, D], F32),
            dram_scr=sb(pb, "dram_scr", [128, 2 * D], F32),
            zt=sb(pb, "zt", [128, D], F32),
            wstage=sb(pb, "wstage", [128, 1024], F32),
            sqscr=sb(pb, "sqscr", [128, D], F32),
            cstats=sb(pb, "cstats", [128, 8], F32),
            gcol=sb(pb, "gcol", [128, H], F32),
            ghcol=sb(pb, "ghcol", [128, H], F32),
            dmask=sb(pb, "dmask_b", [128, H * 128], F32),
        )
        c1 = pb.dma("sync", bufs["identb"][:, :], ident)
        c2 = pb.dma("sync", bufs["onescol"][0:1, :], ones_d)
        c3 = pb.dma("sync", bufs["gbrows"][0:1, :], gb_d[0:1, 0:6 * D])
        c4 = pb.dma("sync", bufs["b1cols"][:, :], b1_d)
        c5 = pb.dma("sync", bufs["b2rows"][0:1, :], b2_d)
        c6 = pb.dma("sync", bufs["gcol"][:, :], gcols_d[:, 0:H])
        c7 = pb.dma("sync", bufs["ghcol"][:, :], gcols_d[:, H:2 * H])
        c8 = pb.dma("sync", bufs["dmask"][:, :], dmask_d)
        bufs["gb_tok"] = c3
        const_tok = c8
        gammas = 1.0 - np.exp(np.linspace(np.log(1.0 / 32), np.log(1.0 / 512), H))
        entry = [c1, c2, c3, c4, c5, c6, c7, c8]
        if outer_ln:
            es0 = ExitStack()
            with es0:
                sv0 = pb.es; pb.es = es0
                bufs["lnt0"] = sb(pb, "lnt0", [128, D], F32)
                bufs["lnt1"] = sb(pb, "lnt1", [128, D], F32)
                pb.es = sv0
                ln0 = emit_ln_stream(pb, bufs, [zin[t] for t in range(n_tc)],
                                     [z0_d[t] for t in range(n_tc)], n_tc, tok0=entry)
            del bufs["lnt0"], bufs["lnt1"]
            entry = entry + [ln0[-1]]
            zi_aps = [z0_d[t] for t in range(n_tc)]
        else:
            zi_aps = [zin[t] for t in range(n_tc)]
        for it in range(iters):
            if it > 0:
                c3 = pb.dma("sync", bufs["gbrows"][0:1, :],
                            gb_d[0:1, it * 6 * D:(it + 1) * 6 * D], deps=entry)
                bufs["gb_tok"] = c3
            zo_aps = [zout[t] for t in range(n_tc)] if it == iters - 1 else [z1_d[t] for t in range(n_tc)]
            cfg = dict(
                n_tc=n_tc, iter=it,
                zi=zi_aps, zo=zo_aps,
                yd=[yd_d[t] for t in range(n_tc)],
                x1=[x1_d[t] for t in range(n_tc)],
                wq=[[wq_d[it, h, c] for c in range(DC)] for h in range(H)],
                wk=[[wk_d[it, h, c] for c in range(DC)] for h in range(H)],
                wv=[[wv_d[it, h, c] for c in range(DC)] for h in range(H)],
                ln1g_off=0 * D, ln1b_off=1 * D,
                ln2g_off=2 * D, ln2b_off=3 * D,
                gng_off=4 * D, gnb_off=5 * D,
                gC=[float(g) ** 128 for g in gammas],
                dk_tok=const_tok, tab_tok=None,
                b1_tok=c4, b2_tok=c5,
                entry=list(entry),
            )
            es1 = ExitStack()
            with es1:
                sv = pb.es; pb.es = es1
                bufs["pt"] = (ps(pb, "ptP1a%d" % it, [128, 128], BF16),
                              ps(pb, "ptP1b%d" % it, [128, 128], BF16))
                bufs["ps_bc"] = ps(pb, "psbc1%d" % it, [128, 512])
                pb.es = sv
                gn_toks = []
                prevgn = list(pb.tail.get(("gb_bc",), ())) + [pb.barrier_tok("vector")]
                for h in range(H):
                    for which, dstt in ((cfg["gng_off"], bufs["gnw_bc"]),
                                        (cfg["gnb_off"], bufs["gnb_bc"])):
                        psbcl = bufs["ps_bc"]
                        tm1 = pb.emit("tensor", lambda e, which=which, h=h, psbcl=psbcl: e.matmul(
                            psbcl[:, 0:128], bufs["onescol"][0:1, 0:128],
                            bufs["gbrows"][0:1, which + h * 128: which + h * 128 + 128],
                            start=True, stop=True), deps=[bufs["gb_tok"]] + prevgn)
                        tcp = pb.emit("scalar", lambda e, dstt=dstt, h=h, psbcl=psbcl: e.copy(
                            dstt[:, h * 128:(h + 1) * 128], psbcl[:, 0:128]),
                            deps=[tm1])
                        prevgn = [tcp]
                        gn_toks.append(tcp)
                pb.tail[("gb_bc",)] = (gn_toks[-1],)
                cfg["gn_tok"] = gn_toks[-1]
                gb = emit_gb_bcast(pb, bufs, cfg["ln1g_off"], cfg["ln1b_off"])
                esKV = ExitStack()
                svv = pb.es; pb.es = esKV
                xkv = sb(pb, "xkv%d" % it, [128, DC * T], BF16)
                pb.es = svv
                bufs["xkv_fm"] = xkv
                ln1 = emit_ln_fm(pb, None, xkv, bufs["stats"], bufs["tmp_bf"][:, 0:D],
                                 bufs["tmp_bf"], bufs["identb"], bufs["pt"], n_tc=n_tc,
                                 tok_x=list(gb) + list(entry),
                                 affine_wb=(bufs["g_bc"], bufs["b_bc"], bufs["lnscr"]),
                                 src_dram=zi_aps, dram_scr=bufs["dram_scr"], dst_stride=T)
            es2 = ExitStack()
            with es2:
                sv = pb.es; pb.es = es2
                for nm, shp, dt_ in (("q_fm", [128, T], BF16), ("k_fm", [128, T], BF16),
                                     ("qrot", [128, T], BF16), ("krot", [128, T], BF16),
                                     ("khat", [128, T], BF16), ("v_tm", [128, T], BF16),
                                     ("tscr1", [128, T], BF16),
                                     ("whb", [128, 3 * 1024], BF16),
                                     ("abuf", [128, 128], BF16),
                                     ("obuf", [128, 128], F32),
                                     ("obuf2", [128, 128], BF16),
                                     ("state", [128, 128], F32),
                                     ("state_bf", [128, 128], BF16),
                                     ("tab", [128, 2 * T], BF16)):
                    bufs[nm] = sb(pb, nm + "_%d" % it, shp, dt_)
                bufs["ps_s0"] = ps(pb, "psA%d" % it, [128, 512])
                bufs["ps_s1"] = ps(pb, "psB%d" % it, [128, 512])
                bufs["ps_o1"] = ps(pb, "psO1%d" % it, [128, 128])
                bufs["ps_o2"] = ps(pb, "psO2%d" % it, [128, 128])
                bufs["ps_su"] = ps(pb, "psSU%d" % it, [128, 128])
                bufs["ptB"] = (ps(pb, "ptB0%d" % it, [128, 128], BF16),
                               ps(pb, "ptB1%d" % it, [128, 128], BF16))
                pb.es = sv
                cfg["tabs_d"] = [tabs_d[j] for j in range(4)]
                _, yd_toks = None, None
                ln1_, yd_toks = emit_retention_iter_p2(pb, bufs, cfg, ln1)
            es3 = ExitStack()
            with es3:
                sv = pb.es; pb.es = es3
                bufs["wg_bf"] = sb(pb, "wg_bf%d" % it, [128, DC * D], BF16)
                bufs["wo_bf"] = sb(pb, "wo_bf%d" % it, [128, DC * D], BF16)
                bufs["ytile"] = sb(pb, "ytile%d" % it, [128, D], BF16)
                bufs["gtile"] = sb(pb, "gtile%d" % it, [128, 512], BF16)
                bufs["yfm"] = sb(pb, "yfm%d" % it, [128, D], BF16)
                bufs["ps_s0"] = ps(pb, "psA3%d" % it, [128, 512])
                bufs["ps_s1"] = ps(pb, "psB3%d" % it, [128, 1024])
                bufs["ptB"] = (ps(pb, "ptB30%d" % it, [128, 128], BF16),
                               ps(pb, "ptB31%d" % it, [128, 128], BF16))
                pb.es = sv
                _, cfg["wg_tok"] = emit_weight_load(pb, [wg_d[it, c] for c in range(DC)],
                                                    bufs["wstage"], bufs["wg_bf"], DC, 1024,
                                                    deps=[yd_toks[-1]], eng_dma="sync")
                _, cfg["wo_tok"] = emit_weight_load(pb, [wo_d[it, c] for c in range(DC)],
                                                    bufs["wstage"], bufs["wo_bf"], DC, 1024,
                                                    deps=[], eng_dma="sync")
                x1_toks = emit_retention_p3(pb, bufs, cfg, yd_toks, ln1)
            esKV.close()
            es4 = ExitStack()
            with es4:
                sv = pb.es; pb.es = es4
                if final_add and it == iters - 1:
                    bufs["x0scr"] = sb(pb, "x0scr_%d" % it, [128, D], F32)
                    cfg["x0"] = [x0_d[t] for t in range(n_tc)]
                bufs["xn2t"] = sb(pb, "xn2t_%d" % it, [128, D], BF16)
                bufs["w1_bf"] = sb(pb, "w1_bf%d" % it, [128, DC * 4096], BF16)
                bufs["w2_bf"] = sb(pb, "w2_bf%d" % it, [128, 32 * 1024], BF16)
                bufs["h1c"] = sb(pb, "h1c%d" % it, [128, 128], BF16)
                bufs["ps_s1"] = ps(pb, "psH4%d" % it, [128, 512])
                bufs["ps_bc"] = ps(pb, "psbc4%d" % it, [128, 512])
                bufs["pt"] = (ps(pb, "ptP4a%d" % it, [128, 128], BF16),
                              ps(pb, "ptP4b%d" % it, [128, 128], BF16))
                bufs["ps_o2b"] = ps(pb, "psO4%d" % it, [128, 1024])
                pb.es = sv
                cfg["b1_tok"] = c4
                bufs["b1cols_it"] = bufs["b1cols"]
                _, cfg["w1_tok"] = emit_weight_load(
                    pb, [w1_d[it, c // 4][:, (c % 4) * 1024:(c % 4) * 1024 + 1024]
                         for c in range(DC * 4)],
                    bufs["wstage"], bufs["w1_bf"], DC * 4, 1024,
                    deps=[x1_toks[-1]], eng_dma="gpsimd")
                _, cfg["w2_tok"] = emit_weight_load(
                    pb, [w2_d[it, c] for c in range(32)],
                    bufs["wstage"], bufs["w2_bf"], 32, 1024,
                    deps=[], eng_dma="gpsimd")
                cfg["b1_base"] = it * 32
                bufs["b1cols_view"] = bufs["b1cols"][:, it * 32:(it + 1) * 32]
                bufs["b2row"] = bufs["b2rows"][0:1, it * D:(it + 1) * D]
                zo_toks = emit_retention_p4(pb, bufs, cfg, x1_toks)
            zi_aps = zo_aps
            entry = pb.all_toks()
        pb.emit("sync", lambda e: e.nop(), deps=zo_toks)
        pb.build()
    return nc



_PROG_CACHE = {}

def _prog(name, fn):
    if name not in _PROG_CACHE:
        _PROG_CACHE[name] = fn()
    return _PROG_CACHE[name]


def _halves(x):
    out = []
    for c in range(8):
        b, h = c // 2, c % 2
        out.append(np.ascontiguousarray(x[b, h * 2048:(h + 1) * 2048].reshape(16, 128, 1024)))
    return out


def _unhalves(res, key):
    x = np.empty((4, 4096, 1024), np.float32)
    for c in range(8):
        b, h = c // 2, c % 2
        x[b, h * 2048:(h + 1) * 2048] = np.asarray(res[c][key]).reshape(2048, 1024)
    return x


def kernel(**z):
    import math
    ONES = np.ones((1, 128), BF)
    B = 4
    x = np.asarray(z["L0_attn_a_X"], np.float32)

    H8, HD = 8, 128
    perm = np.concatenate([np.arange(0, HD, 2), np.arange(1, HD, 2)])
    WQ, WK, WV = z["L0_attn_a_WQ"], z["L0_attn_a_WK"], z["L0_attn_a_WV"]
    T4 = 4096
    sv = (np.arange(0, HD, 2) + 0.4 * HD) / (1.4 * HD)
    power = (np.arange(T4, dtype=np.float64) / 512)[:, None]
    scale = sv[None, :] ** power
    inv_freq = 1.0 / 10000.0 ** (np.arange(64, dtype=np.float64) / 64)
    ang = np.arange(T4, dtype=np.float64)[:, None] * inv_freq[None, :]
    sin_, cos_ = np.sin(ang), np.cos(ang)
    tabs = np.stack([np.ascontiguousarray(
        np.concatenate([a.T, a.T], 0).astype(BF))
        for a in (cos_ * scale, sin_ * scale, cos_ / scale, sin_ / scale)])
    gammas = 1.0 - np.exp(np.linspace(np.log(1 / 32), np.log(1 / 512), H8))
    pp = np.arange(128)
    dmask = np.concatenate([np.where(pp[None, :] >= pp[:, None],
                                     g ** (pp[None, :] - pp[:, None]).astype(np.float64), 0.0).astype(np.float32)
                            for g in gammas], 1)
    gcols = np.concatenate([np.stack([g ** (pp + 1) for g in gammas], 1),
                            np.stack([g ** (127 - pp) for g in gammas], 1)], 1).astype(np.float32)
    gbrows = np.concatenate([np.concatenate(
        [z["L0_attn_a_ln1_w"][i], z["L0_attn_a_ln1_b"][i],
         z["L0_attn_a_ln2_w"][i], z["L0_attn_a_ln2_b"][i],
         z["L0_attn_a_gn_w"][i], z["L0_attn_a_gn_b"][i]]) for i in range(2)]
        ).astype(BF)[None, :]
    imr = dict(
        wq=np.ascontiguousarray(WQ[:, :, :, perm].reshape(2, H8, 8, 128, 128)),
        wk=np.ascontiguousarray(WK[:, :, :, perm].reshape(2, H8, 8, 128, 128)),
        wv=np.ascontiguousarray(WV.reshape(2, H8, 8, 128, 128)),
        wg=np.ascontiguousarray(z["L0_attn_a_WG"].reshape(2, 8, 128, D)),
        wo=np.ascontiguousarray(z["L0_attn_a_WO"].reshape(2, 8, 128, D)),
        w1=np.ascontiguousarray(z["L0_attn_a_ffn_w1"].reshape(2, 8, 128, 4096)),
        w2=np.ascontiguousarray(z["L0_attn_a_ffn_w2"].reshape(2, 32, 128, D)),
        gbrows=gbrows,
        b1cols=np.ascontiguousarray(np.concatenate(
            [z["L0_attn_a_ffn_b1"][i].reshape(32, 128).T for i in range(2)], 1)),
        b2rows=z["L0_attn_a_ffn_b2"].ravel().astype(BF)[None, :],
        xpos=tabs, dmask=dmask, gcols=gcols,
        ident=IDENT_BF, onesc=ONES)
    RET = _prog("RET", lambda: build_retention_test(n_tc=32, iters=2,
                                                    outer_ln=True, final_add=True))
    in_maps = []
    for c in range(8):
        b = c // 2
        im = dict(imr)
        im["zin"] = np.ascontiguousarray(x[b].reshape(32, 128, 1024))
        im["x0"] = im["zin"]
        in_maps.append(im)
    res = run_bass_kernel_spmd(RET, in_maps, core_ids=list(range(8)))
    xcur = np.stack([np.asarray(res.results[2 * b]["zout"]).reshape(4096, 1024)
                     for b in range(B)])

    ML0 = _prog("ML0", lambda: build_moe_l0_test(n_tc=16))
    im_m = dict(
        gwT=np.ascontiguousarray(z["L0_ffn_gate_w"].T.reshape(8, 128, 8)
                                 .transpose(1, 0, 2).reshape(128, 64)),
        gu=np.ascontiguousarray(z["L0_ffn_gate_up_proj"].reshape(8, 8, 128, 2048)),
        down=np.ascontiguousarray(z["L0_ffn_down_proj"].reshape(8, 8, 128, 1024)),
        ident=IDENT_BF)

    def run_moe_l0(xc):
        hs = _halves(xc)
        ims = [dict(im_m, xin=hs[c]) for c in range(8)]
        r = run_bass_kernel_spmd(ML0, ims, core_ids=list(range(8)))
        return _unhalves(r.results, "xout")

    ppp, fff = np.arange(128)[:, None], np.arange(512)[None, :]
    cmask = np.concatenate([np.where(ppp + idx * 128 > fff, -1e30, 0).astype(np.float32)
                            for idx in range(4)], 1)

    def attn_im_common(style):
        im = dict(ident=IDENT_BF, onesc=ONES, cmask=cmask)
        if style == "tanh":
            Wq, Wk, Wv, Wo = z["L1_attn_a_Wq"], z["L1_attn_a_Wk"], z["L1_attn_a_Wv"], z["L1_attn_a_Wo"]
            bq, bk, bv, bo = z["L1_attn_a_bq"], z["L1_attn_a_bk"], z["L1_attn_a_bv"], z["L1_attn_a_bo"]
            bqk = np.zeros((64, 32), np.float32)
            for h in range(16):
                bqk[:, 2 * h] = bq[h * 64:(h + 1) * 64]
                bqk[:, 2 * h + 1] = bk[h * 64:(h + 1) * 64]
            im["bqk"] = bqk
            im["bvo"] = np.concatenate([bv, bo]).astype(BF)[None, :]
        else:
            Wq = np.ascontiguousarray(z["L0_attn_b_Wq"].T)
            Wk = np.ascontiguousarray(z["L0_attn_b_Wk"].T)
            Wv = np.ascontiguousarray(z["L0_attn_b_Wv"].T)
            Wo = np.ascontiguousarray(z["L0_attn_b_Wp"].T)
            for n2 in "qkv":
                im["bT" + n2] = np.ascontiguousarray(z["L0_attn_b_B" + n2].T.reshape(8, 128, 8))
                im["aT" + n2] = np.ascontiguousarray(z["L0_attn_b_A" + n2].T)
        im["wq"] = np.ascontiguousarray(Wq.reshape(8, 128, 16, 64).transpose(2, 0, 1, 3))
        im["wk"] = np.ascontiguousarray(Wk.reshape(8, 128, 16, 64).transpose(2, 0, 1, 3))
        im["wv"] = np.ascontiguousarray(Wv.reshape(8, 128, 16, 64).transpose(2, 0, 1, 3))
        im["wo"] = np.ascontiguousarray(Wo.reshape(8, 128, 1024))
        return im

    AT = _prog("AT", lambda: build_attn_test(nq_tc=16, style="tanh"))
    AA = _prog("AA", lambda: build_attn_test(nq_tc=16, style="alibi"))

    def run_attn(prog, xc, style):
        imc = attn_im_common(style)
        zeros_h = np.zeros((16, 128, 1024), np.float32)
        ims = []
        for c in range(8):
            b, h = c // 2, c % 2
            im = dict(imc)
            im["xin"] = np.ascontiguousarray(xc[b, h * 2048:(h + 1) * 2048].reshape(16, 128, 1024))
            im["xprev"] = (np.ascontiguousarray(xc[b, 0:2048].reshape(16, 128, 1024))
                           if h == 1 else zeros_h)
            km = np.zeros((128, 32), np.float32)
            if h == 0:
                km[:, 0:16] = -1e30
            im["kmask"] = km
            base = h * 2048 - 2048
            im["kpos"] = (base + np.arange(32)[None, :] * 128
                          + np.arange(128)[:, None]).astype(np.float32)
            ims.append(im)
        r = run_bass_kernel_spmd(prog, ims, core_ids=list(range(8)))
        return _unhalves(r.results, "xout")

    ML1 = _prog("ML1", lambda: build_moe_l1_test(n_tc=16))
    im_f = dict(
        gwT=np.ascontiguousarray(z["L1_ffn_router_w"].reshape(8, 128, 8)
                                 .transpose(1, 0, 2).reshape(128, 64)),
        ebias=np.ascontiguousarray(np.broadcast_to(z["L1_ffn_expert_bias"], (128, 8))),
        gu=np.ascontiguousarray(np.concatenate([z["L1_ffn_w1"], z["L1_ffn_w3"]], axis=2)
                                .reshape(8, 8, 128, 2048)),
        down=np.ascontiguousarray(z["L1_ffn_w2"].reshape(8, 8, 128, 1024)),
        sgu=np.ascontiguousarray(np.concatenate([z["L1_ffn_sw1"], z["L1_ffn_sw3"]], axis=1)
                                 .reshape(8, 128, 2048)),
        sdown=np.ascontiguousarray(z["L1_ffn_sw2"].reshape(8, 128, 1024)),
        ident=IDENT_BF)

    def run_moe_l1(xc):
        hs = _halves(xc)
        ims = [dict(im_f, xin=hs[c]) for c in range(8)]
        r = run_bass_kernel_spmd(ML1, ims, core_ids=list(range(8)))
        return _unhalves(r.results, "xout")

    MG = _prog("MG", lambda: build_mingru_test(n_tc=16))
    im_g = dict(
        w=np.ascontiguousarray(z["L1_attn_b_W"].reshape(3, 8, 128, 2048)),
        gbrows=np.concatenate([z["L1_attn_b_ln_gamma"].ravel(),
                               z["L1_attn_b_ln_beta"].ravel()]).astype(BF)[None, :],
        bcols=np.ascontiguousarray(z["L1_attn_b_b"].reshape(3, 16, 128)
                                   .transpose(2, 0, 1).reshape(128, 48)),
        ident=IDENT_BF, identf_d=np.eye(128, dtype=np.float32), onesc=ONES)

    def run_mingru(xc):
        ims = []
        for c in range(8):
            b = c // 2
            im = dict(im_g, xin=np.ascontiguousarray(xc[b, 0:2048].reshape(16, 128, 1024)),
                      initc=np.full((128, 24), 0.5, np.float32))
            ims.append(im)
        r1 = run_bass_kernel_spmd(MG, ims, core_ids=list(range(8)))
        ims2 = []
        for c in range(8):
            b = c // 2
            im = dict(im_g, xin=np.ascontiguousarray(xc[b, 2048:4096].reshape(16, 128, 1024)),
                      initc=np.asarray(r1.results[2 * b]["hend"]))
            ims2.append(im)
        r2 = run_bass_kernel_spmd(MG, ims2, core_ids=list(range(8)))
        out = np.empty((4, 4096, 1024), np.float32)
        for b in range(B):
            out[b, 0:2048] = np.asarray(r1.results[2 * b]["xout"]).reshape(2048, 1024)
            out[b, 2048:4096] = np.asarray(r2.results[2 * b]["xout"]).reshape(2048, 1024)
        return out

    xcur = run_moe_l0(xcur)
    xcur = run_attn(AA, xcur, "alibi")
    xcur = run_moe_l0(xcur)
    xcur = run_attn(AT, xcur, "tanh")
    xcur = run_moe_l1(xcur)
    xcur = run_mingru(xcur)
    xcur = run_moe_l1(xcur)
    return xcur.astype(np.float32)



CCK = mybir.CollectiveComputeKind
PAIRS = [[0, 1], [2, 3], [4, 5], [6, 7]]


def emit_allgather(pb, ins_ap, outs_ap, deps=()):
    return pb.emit("gpsimd", lambda e: e.collective_compute(
        "AllGather", ALU.bypass, PAIRS, [ins_ap], [outs_ap]),
        deps=deps, coll=True)


def emit_allreduce(pb, ins_ap, outs_ap, deps=()):
    return pb.emit("gpsimd", lambda e: e.collective_compute(
        "AllReduce", ALU.add, PAIRS, [ins_ap], [outs_ap]),
        deps=deps, coll=True)


def build_fused(nq=16, upto=8):
    nc = new_nc()
    es = ExitStack()
    T = nq * 128
    S_tc = 2 * nq
    S = S_tc * 128
    H8 = 8
    H16, dh = 16, 64

    def din(name, shape, dt_=F32):
        return nc.dram_tensor(name, list(shape), dt_, kind="ExternalInput").ap()

    def dscr(name, shape, dt_=F32):
        return nc.dram_tensor(name, list(shape), dt_).ap()

    xin = din("xin", [S_tc, 128, D])
    ident = din("ident", [128, 128], BF16)
    identf = din("identf_d", [128, 128])
    ones_d = din("onesc", [1, 128], BF16)
    cmask_d = din("cmask", [128, 4 * 512], BF16)
    hflags = din("hflags", [128, 4])
    r = {k: din("r_" + k, s, d) for k, s, d in (
        ("wq", [2, H8, DC, 128, 128], F32), ("wk", [2, H8, DC, 128, 128], F32),
        ("wv", [2, H8, DC, 128, 128], F32),
        ("wg", [2, DC, 128, D], F32), ("wo", [2, DC, 128, D], F32),
        ("w1", [2, DC, 128, 4096], F32), ("w2", [2, 32, 128, D], F32),
        ("gbrows", [1, 2 * 6 * D], BF16), ("b1cols", [128, 2 * 32], F32),
        ("b2rows", [1, 2 * D], BF16), ("xpos", [4, 128, S], BF16),
        ("dmask", [128, H8 * 128], F32), ("gcols", [128, 2 * H8], F32))}
    m0 = {k: din("m0_" + k, s, d) for k, s, d in (
        ("gwT", [128, DC * 8], F32), ("gu", [8, 2 * DC, 128, 1024], F32),
        ("down", [8, DC, 128, 1024], F32))}
    aa = {k: din("aa_" + k, s, d) for k, s, d in (
        ("wq", [H16, DC, 128, dh], F32), ("wk", [H16, DC, 128, dh], F32),
        ("wv", [H16, DC, 128, dh], F32), ("wo", [DC, 128, D], F32),
        ("bTq", [DC, 128, 8], F32), ("bTk", [DC, 128, 8], F32), ("bTv", [DC, 128, 8], F32),
        ("aTq", [8, D], F32), ("aTk", [8, D], F32), ("aTv", [8, D], F32),
        ("kmask", [128, S_tc], F32), ("kpos", [128, S_tc], F32),
        ("coef", [128, 2 * S_tc], F32))}
    at = {k: din("at_" + k, s, d) for k, s, d in (
        ("wq", [H16, DC, 128, dh], F32), ("wk", [H16, DC, 128, dh], F32),
        ("wv", [H16, DC, 128, dh], F32), ("wo", [DC, 128, D], F32),
        ("bqk", [64, 2 * H16], F32), ("bvo", [1, 2 * D], BF16),
        ("kmask", [128, S_tc], F32))}
    m1 = {k: din("m1_" + k, s, d) for k, s, d in (
        ("gwT", [128, DC * 8], F32), ("gu", [8, 2 * DC, 128, 1024], F32),
        ("down", [8, DC, 128, 1024], F32), ("sgu", [2 * DC, 128, 1024], F32),
        ("sdown", [DC, 128, 1024], F32), ("ebias", [128, 8], F32))}
    g = {k: din("g_" + k, s, d) for k, s, d in (
        ("w", [3, DC, 128, 2048], F32), ("gbrows", [1, 6 * D], BF16),
        ("bcols", [128, 48], F32))}
    r_z0 = dscr("r_z0", [S_tc, 128, D])
    r_z1 = dscr("r_z1", [S_tc, 128, D])
    r_zf = dscr("r_zf", [S_tc, 128, D])
    r_yd = dscr("r_yd", [S_tc, 128, D], BF16)
    r_x1 = dscr("r_x1", [S_tc, 128, D])
    ag_own = dscr("ag_own", [T, D])
    ag_full = dscr("ag_full", [S, D])
    kvsrc = dscr("kvsrc", [S_tc, 128, D])
    p_sp = dscr("p_sp", [DC, 128, T])
    xsp = dscr("xsp", [nq, 128, D])
    gcin = dscr("gcin", [3, 128, 8])
    gcout = dscr("gcout", [3, 128, 8])
    xout = nc.dram_tensor("xout", [S_tc, 128, D], F32, kind="ExternalOutput").ap()

    with es:
        pb = PB(nc, es)
        bP = dict(
            stats=sb(pb, "stats", [128, 4 * S_tc], F32),
            tmp_bf=sb(pb, "tmp_bf", [128, D], BF16),
            identb=sb(pb, "identb", [128, 128], BF16),
            identfb=sb(pb, "identfb", [128, 128], F32),
            onescol=sb(pb, "onescol", [1, 128], BF16),
            zt=sb(pb, "zt", [128, D], F32),
            cstats=sb(pb, "cstats", [128, 8], F32),
            wstage=sb(pb, "wstage", [128, 1024], F32),
            rcol=sb(pb, "rcol", [128, 8], F32),
            fcols=sb(pb, "fcols", [128, 4], F32),
        )
        cst = [pb.dma("sync", bP["identb"][:, :], ident),
               pb.dma("sync", bP["identfb"][:, :], identf),
               pb.dma("sync", bP["onescol"][0:1, :], ones_d),
               pb.dma("sync", bP["fcols"][:, :], hflags)]
        c_tok = cst[-1]
        gammas = 1.0 - np.exp(np.linspace(np.log(1.0 / 32), np.log(1.0 / 512), H8))
        ret_toks = None
        pb.tail[("fused_entry",)] = tuple(cst)
        esR = ExitStack()
        with esR:
            svR = pb.es; pb.es = esR
            bR = dict(bP)
            bR["gbrows"] = sb(pb, "r_gbrows_b", [1, 6 * D], BF16)
            bR["b1cols"] = sb(pb, "r_b1cols_b", [128, 2 * 32], F32)
            bR["b2rows"] = sb(pb, "r_b2rows_b", [1, 2 * D], BF16)
            bR["gcol"] = sb(pb, "r_gcol", [128, H8], F32)
            bR["ghcol"] = sb(pb, "r_ghcol", [128, H8], F32)
            bR["g_bc"] = sb(pb, "r_g_bc", [128, D], F32)
            bR["b_bc"] = sb(pb, "r_b_bc", [128, D], F32)
            bR["dram_scr"] = sb(pb, "r_dram_scr", [128, 2 * D], F32)
            bR["sqscr"] = sb(pb, "r_sqscr", [128, D], F32)
            bR["lnscr"] = sb(pb, "r_lnscr", [128, D], F32)
            pb.es = svR
            rc = [pb.dma("sync", bR["gbrows"][0:1, :], r["gbrows"][0:1, 0:6 * D]),
                  pb.dma("sync", bR["b1cols"][:, :], r["b1cols"]),
                  pb.dma("sync", bR["b2rows"][0:1, :], r["b2rows"]),
                  pb.dma("sync", bR["gcol"][:, :], r["gcols"][:, 0:H8]),
                  pb.dma("sync", bR["ghcol"][:, :], r["gcols"][:, H8:2 * H8])]
            bR["gb_tok"] = rc[0]
            entry = cst + rc
            es0 = ExitStack()
            with es0:
                sv0 = pb.es; pb.es = es0
                bR["lnt0"] = sb(pb, "lnt0", [128, D], F32)
                bR["lnt1"] = sb(pb, "lnt1", [128, D], F32)
                pb.es = sv0
                ln0 = emit_ln_stream(pb, bR, [xin[t] for t in range(S_tc)],
                                     [r_z0[t] for t in range(S_tc)], S_tc, tok0=entry)
            del bR["lnt0"], bR["lnt1"]
            entry = entry + [ln0[-1]]
            zi_aps = [r_z0[t] for t in range(S_tc)]
            for it in range(2):
                if it > 0:
                    c3b = pb.dma("sync", bR["gbrows"][0:1, :],
                                 r["gbrows"][0:1, 6 * D:12 * D], deps=entry)
                    bR["gb_tok"] = c3b
                zo_aps = [r_zf[t] for t in range(S_tc)] if it == 1 else [r_z1[t] for t in range(S_tc)]
                cfg = dict(
                    n_tc=S_tc, iter=it, zi=zi_aps, zo=zo_aps,
                    yd=[r_yd[t] for t in range(S_tc)],
                    x1=[r_x1[t] for t in range(S_tc)],
                    wq=[[r["wq"][it, h, c] for c in range(DC)] for h in range(H8)],
                    wk=[[r["wk"][it, h, c] for c in range(DC)] for h in range(H8)],
                    wv=[[r["wv"][it, h, c] for c in range(DC)] for h in range(H8)],
                    ln1g_off=0, ln1b_off=D, ln2g_off=2 * D, ln2b_off=3 * D,
                    gng_off=4 * D, gnb_off=5 * D,
                    gC=[float(gm) ** 128 for gm in gammas],
                    b1_tok=rc[1], b2_tok=rc[2],
                    entry=list(entry),
                    tabs_d=[r["xpos"][j] for j in range(4)],
                )
                esKV = ExitStack()
                svv = pb.es; pb.es = esKV
                xkv = sb(pb, "r_xkv%d" % it, [128, DC * S], BF16)
                pb.es = svv
                bR["xkv_fm"] = xkv
                esG = ExitStack()
                svG = pb.es; pb.es = esG
                bR["gnw_bc"] = sb(pb, "r_gnw_bc%d" % it, [128, H8 * 128], F32)
                bR["gnb_bc"] = sb(pb, "r_gnb_bc%d" % it, [128, H8 * 128], F32)
                bR["dmask"] = sb(pb, "r_dmask_b%d" % it, [128, H8 * 128], F32)
                pb.es = svG
                dkt = pb.dma("sync", bR["dmask"][:, :], r["dmask"], deps=list(entry))
                cfg["dk_tok"] = dkt
                es1 = ExitStack()
                with es1:
                    sv = pb.es; pb.es = es1
                    bR["pt"] = (ps(pb, "ptP1a%d" % it, [128, 128], BF16),
                                ps(pb, "ptP1b%d" % it, [128, 128], BF16))
                    bR["ps_bc"] = ps(pb, "psbc1%d" % it, [128, 512])
                    pb.es = sv
                    gn_toks = []
                    prevgn = list(pb.tail.get(("gb_bc",), ())) + [pb.barrier_tok("vector")]
                    for h in range(H8):
                        for which, dstt in ((cfg["gng_off"], bR["gnw_bc"]),
                                            (cfg["gnb_off"], bR["gnb_bc"])):
                            psbcl = bR["ps_bc"]
                            gbr_l = bR["gbrows"]
                            tm1 = pb.emit("tensor", lambda e, which=which, h=h, psbcl=psbcl, gbr_l=gbr_l: e.matmul(
                                psbcl[:, 0:128], bP["onescol"][0:1, 0:128],
                                gbr_l[0:1, which + h * 128: which + h * 128 + 128],
                                start=True, stop=True), deps=[bR["gb_tok"]] + prevgn)
                            tcp = pb.emit("scalar", lambda e, dstt=dstt, h=h, psbcl=psbcl: e.copy(
                                dstt[:, h * 128:(h + 1) * 128], psbcl[:, 0:128]),
                                deps=[tm1])
                            prevgn = [tcp]
                            gn_toks.append(tcp)
                    pb.tail[("gb_bc",)] = (gn_toks[-1],)
                    cfg["gn_tok"] = gn_toks[-1]
                    gb = emit_gb_bcast(pb, bR, cfg["ln1g_off"], cfg["ln1b_off"])
                    ln1 = emit_ln_fm(pb, None, xkv, bR["stats"], bR["tmp_bf"][:, 0:D],
                                     bR["tmp_bf"], bR["identb"], bR["pt"], n_tc=S_tc,
                                     tok_x=list(gb) + list(entry),
                                     affine_wb=(bR["g_bc"], bR["b_bc"], bR["lnscr"]),
                                     src_dram=zi_aps, dram_scr=bR["dram_scr"], dst_stride=S)
                es2 = ExitStack()
                with es2:
                    sv = pb.es; pb.es = es2
                    for nm, shp, dt_ in (("q_fm", [128, S], BF16), ("k_fm", [128, S], BF16),
                                         ("khat", [128, S], BF16), ("v_tm", [128, S], BF16),
                                         ("tscr1", [128, S], BF16),
                                         ("whb", [128, 3 * 1024], BF16),
                                         ("whb2", [128, 3 * 1024], BF16),
                                         ("abuf", [128, 128], BF16),
                                         ("abuf1", [128, 128], BF16),
                                         ("obuf", [128, 128], F32),
                                         ("obuf1", [128, 128], F32),
                                         ("obuf2", [128, 128], BF16),
                                         ("obuf21", [128, 128], BF16),
                                         ("gncs", [128, 16], F32),
                                         ("state", [128, 128], F32),
                                         ("state_bf", [128, 128], BF16),
                                         ("tab", [128, 2 * S], BF16)):
                        bR[nm] = sb(pb, "r_" + nm + "_%d" % it, shp, dt_)
                    bR["ps_s0"] = ps(pb, "psA%d" % it, [128, 512])
                    bR["ps_s1"] = ps(pb, "psB%d" % it, [128, 512])
                    bR["ps_o1"] = ps(pb, "psO1%d" % it, [128, 128])
                    bR["ps_o1b"] = ps(pb, "psO1b%d" % it, [128, 128])
                    bR["ps_o2"] = ps(pb, "psO2%d" % it, [128, 128])
                    bR["ps_su"] = ps(pb, "psSU%d" % it, [128, 128])
                    bR["ptB"] = (ps(pb, "ptB0%d" % it, [128, 128], BF16),
                                 ps(pb, "ptB1%d" % it, [128, 128], BF16))
                    pb.es = sv
                    _, yd_toks = emit_retention_iter_p2(pb, bR, cfg, ln1)
                esG.close()
                es3 = ExitStack()
                with es3:
                    sv = pb.es; pb.es = es3
                    bR["wg_bf"] = sb(pb, "r_wg_bf%d" % it, [128, DC * D], BF16)
                    bR["wo_bf"] = sb(pb, "r_wo_bf%d" % it, [128, DC * D], BF16)
                    bR["ytile"] = sb(pb, "r_ytile%d" % it, [128, D], BF16)
                    bR["gtile"] = sb(pb, "r_gtile%d" % it, [128, 512], BF16)
                    bR["yfm"] = sb(pb, "r_yfm%d" % it, [128, D], BF16)
                    bR["ps_s0"] = ps(pb, "psA3%d" % it, [128, 512])
                    bR["ps_s1"] = ps(pb, "psB3%d" % it, [128, 1024])
                    bR["ptB"] = (ps(pb, "ptB30%d" % it, [128, 128], BF16),
                                 ps(pb, "ptB31%d" % it, [128, 128], BF16))
                    pb.es = sv
                    _, cfg["wg_tok"] = emit_weight_load(pb, [r["wg"][it, c] for c in range(DC)],
                                                        bR["wstage"], bR["wg_bf"], DC, 1024,
                                                        deps=[yd_toks[-1]], eng_dma="sync")
                    _, cfg["wo_tok"] = emit_weight_load(pb, [r["wo"][it, c] for c in range(DC)],
                                                        bR["wstage"], bR["wo_bf"], DC, 1024,
                                                        deps=[], eng_dma="sync")
                    x1_toks = emit_retention_p3(pb, bR, cfg, yd_toks, ln1)
                esKV.close()
                es4 = ExitStack()
                with es4:
                    sv = pb.es; pb.es = es4
                    if it == 1:
                        bR["x0scr"] = sb(pb, "r_x0scr", [128, D], F32)
                        cfg["x0"] = [xin[t] for t in range(S_tc)]
                    bR["xn2t"] = sb(pb, "r_xn2t_%d" % it, [128, D], BF16)
                    bR["p4xn"] = sb(pb, "r_p4xn_%d" % it, [128, 2 * D], BF16)
                    bR["p4sq"] = sb(pb, "r_p4sq_%d" % it, [128, 2 * D], F32)
                    bR["p4tmp"] = sb(pb, "r_p4tmp_%d" % it, [128, 2 * D], BF16)
                    bR["p4cs"] = sb(pb, "r_p4cs_%d" % it, [128, 16], F32)
                    bR["zt2"] = sb(pb, "r_zt2_%d" % it, [128, D], F32)
                    bR["wstage2"] = bR["dram_scr"][:, 0:1024]
                    bR["w1_bf"] = sb(pb, "r_w1_bf%d" % it, [128, DC * 4096], BF16)
                    bR["w2_bf"] = sb(pb, "r_w2_bf%d" % it, [128, 32 * 1024], BF16)
                    bR["h1c"] = sb(pb, "r_h1c%d" % it, [128, 128], BF16)
                    bR["h1c2"] = sb(pb, "r_h1c2%d" % it, [128, 128], BF16)
                    bR["ps_s1"] = ps(pb, "psH4%d" % it, [128, 512])
                    bR["ps_s1b"] = ps(pb, "psH4b%d" % it, [128, 512])
                    bR["ps_bc"] = ps(pb, "psbc4%d" % it, [128, 512])
                    bR["pt"] = (ps(pb, "ptP4a%d" % it, [128, 128], BF16),
                                ps(pb, "ptP4b%d" % it, [128, 128], BF16))
                    bR["ps_o2b"] = ps(pb, "psO4%d" % it, [128, 1024])
                    pb.es = sv
                    w12_chunks = ([r["w1"][it, c // 4][:, (c % 4) * 1024:(c % 4) * 1024 + 1024]
                                   for c in range(DC * 4)]
                                  + [r["w2"][it, c] for c in range(32)])
                    w12_dst = ([bR["w1_bf"][:, c * 1024:(c + 1) * 1024] for c in range(32)]
                               + [bR["w2_bf"][:, c * 1024:(c + 1) * 1024] for c in range(32)])
                    w12_order = []
                    for q_ in range(4):
                        w12_order += [dc * 4 + q_ for dc in range(DC)]
                        w12_order += [32 + q_ * 8 + j for j in range(8)]
                    w12_toks = emit_weight_load_mq(
                        pb, w12_chunks, [bR["wstage"], bR["wstage2"]], None, 1024,
                        deps=[x1_toks[-1]], queues=("gpsimd", "sync"),
                        order=w12_order, dst_list=w12_dst)
                    cfg["w1_toks"] = w12_toks[0:32]
                    cfg["w1_tok"] = cfg["w1_toks"][-1]
                    cfg["w2_toks"] = w12_toks[32:64]
                    cfg["w2_tok"] = cfg["w2_toks"][-1]
                    bR["b1cols_view"] = bR["b1cols"][:, it * 32:(it + 1) * 32]
                    bR["b2row"] = bR["b2rows"][0:1, it * D:(it + 1) * D]
                    zo_toks = emit_retention_p4(pb, bR, cfg, x1_toks)
                zi_aps = zo_aps
                entry = pb.all_toks()
            ret_toks = zo_toks
        ctxd = dict(T=T, S=S, nq=nq, S_tc=S_tc, upto=upto,
                    xin=xin, r_zf=r_zf, r_z1=r_z1, r_x1=r_x1,
                    xout=xout, m0=m0, aa=aa, at=at, m1=m1, g=g,
                    cmask_d=cmask_d, ret_toks=ret_toks)
        _fused_rest(pb, bP, ctxd)
        pb.build()
    return nc


def _fused_rest(pb, bP, C):
    nq = C["nq"]; T = C["T"]; S = C["S"]; S_tc = C["S_tc"]
    c_tok = pb.tail[("fused_entry",)][-1]
    xa = [C["r_zf"][t] for t in range(S_tc)]
    xb = [C["r_z1"][t] for t in range(S_tc)]
    xg_aps = None

    def fused_moe_s(which, xi, xo, in_toks):
        toks = list(in_toks)
        TGT = 16 if S_tc % 16 == 0 else 8
        for tg in range(S_tc // TGT):
            esM = ExitStack()
            with esM:
                sv = pb.es; pb.es = esM
                b = dict(bP)
                b["x"] = sb(pb, which + "x%d" % tg, [128, TGT * D], F32)
                b["xn_fm"] = sb(pb, which + "xn%d" % tg, [128, DC * TGT * 128], BF16)
                b["gwT"] = sb(pb, which + "gwT%d" % tg, [128, DC * 8], BF16)
                st1 = sb(pb, which + "st1%d" % tg, [128, 1024], F32)
                st2 = sb(pb, which + "st2%d" % tg, [128, 1024], F32)
                b["stages"] = [st1, st2]
                b["stages2"] = b["stages"]
                gu0 = sb(pb, which + "gu0%d" % tg, [128, DC * 2048], BF16)
                gu1 = sb(pb, which + "gu1%d" % tg, [128, DC * 2048], BF16)
                b["gu_bf0"] = gu0
                b["gu_bf1"] = gu1
                b["dn_bf0"] = sb(pb, which + "dn%d" % tg, [128, DC * 1024], BF16)
                b["hbuf0"] = sb(pb, which + "h0%d" % tg, [128, 4096], BF16)
                b["hbuf1"] = b["hbuf0"]
                b["hf_major"] = True
                b["sg"] = sb(pb, which + "sg%d" % tg, [128, 512], F32)
                b["routed_tm"] = sb(pb, which + "ro%d" % tg, [128, TGT * 8], F32)
                b["rscr"] = sb(pb, which + "rs%d" % tg, [128, 64], F32)
                b["bias_bc"] = sb(pb, which + "eb%d" % tg, [128, 8], F32)
                b["ps_r"] = ps(pb, which + "psr%d" % tg, [128, 8])
                b["ps_g"] = ps(pb, which + "psg%d" % tg, [128, 512])
                b["ps_u"] = ps(pb, which + "psu%d" % tg, [128, 512])
                b["ps_d"] = ps(pb, which + "psd%d" % tg, [128, 1024])
                b["pt"] = (ps(pb, which + "pt0%d" % tg, [128, 128], BF16),
                           ps(pb, which + "pt1%d" % tg, [128, 128], BF16))
                pb.es = sv
                entry = pb.all_toks()
                l1 = which.startswith("f")
                md = C["m1"] if l1 else C["m0"]
                x_toks = []
                prevd = ()
                for i in range(TGT):
                    t = pb.dma("gpsimd", b["x"][:, i * D:(i + 1) * D], xi[tg * TGT + i],
                               deps=toks[-1:] + list(prevd) + list(entry))
                    prevd = (t,)
                    x_toks.append(t)
                tgw = pb.dma("sync", b["rscr"][:, 0:64], md["gwT"], deps=entry)
                tgwb = pb.emit("vector", lambda e, b=b: e.tensor_copy(
                    b["gwT"][:, :], b["rscr"][:, 0:64]), deps=[tgw], strict=True)
                extra = [tgwb]
                if l1:
                    tbias = pb.dma("sync", b["bias_bc"][:, :], md["ebias"], deps=entry)
                    b["bias_tok"] = tbias
                    extra.append(tbias)
                fm_last = emit_ln_fm(pb, b["x"], b["xn_fm"], b["stats"],
                                     b["tmp_bf"][:, 0:D], b["tmp_bf"], b["identb"],
                                     b["pt"], n_tc=TGT,
                                     tok_x=x_toks + extra + list(entry))
                gu_d, dn_d = md["gu"], md["down"]
                wap = {
                    "gu": [[gu_d[e_i, cc] for cc in range(DC * 2)] for e_i in range(8)],
                    "down": [[dn_d[e_i, cc] for cc in range(DC)] for e_i in range(8)],
                }
                if l1:
                    wap["sgu"] = [md["sgu"][cc] for cc in range(DC * 2)]
                    wap["sdown"] = [md["sdown"][cc] for cc in range(DC)]
                mtoks = emit_moe_l0(pb, b, wap, n_tc=TGT, fm_toks=[fm_last],
                                    renorm=not l1, shared=l1, l1=l1)
                st = []
                prevd = ()
                for i in range(TGT):
                    t = pb.dma("gpsimd", xo[tg * TGT + i], b["x"][:, i * D:(i + 1) * D],
                               deps=mtoks + list(prevd))
                    prevd = (t,)
                    st.append(t)
                toks = st
        return toks

    def fused_attn_s(which, xi, xo, in_toks):
        style = "alibi" if which == "aa" else "tanh"
        ad = C[which]
        tanh = style == "tanh"
        lora = style == "alibi"
        KVT = S
        esA = ExitStack()
        with esA:
            sv = pb.es; pb.es = esA
            b = dict(bP)
            kposb = sb(pb, which + "kpos", [128, S_tc], F32)
            kmaskb = sb(pb, which + "kmask", [128, S_tc], F32)
            cmaskb = sb(pb, which + "cmask", [128, 4 * 512], BF16)
            bqk = sb(pb, which + "bqk", [64, 2 * 16], F32)
            bvo = sb(pb, which + "bvo", [1, D], BF16)
            b.update(kpos=kposb, kmask=kmaskb, cmask=cmaskb)
            pb.es = sv
            esKV2 = ExitStack()
            svk = pb.es; pb.es = esKV2
            xkv_fm = sb(pb, which + "xkv", [128, DC * KVT], BF16)
            pb.es = svk
            b["xkv_fm"] = xkv_fm
            entry = pb.all_toks()
            ctoks = [pb.dma("sync", kmaskb[:, :],
                            ad["kmask"] if lora else C["at"]["kmask"], deps=entry),
                     pb.dma("sync", cmaskb[:, :], C["cmask_d"], deps=entry)]
            if lora:
                ctoks.append(pb.dma("sync", kposb[:, :], ad["kpos"], deps=entry))
            if tanh:
                ctoks += [pb.dma("sync", bqk[0:64, :], ad["bqk"], deps=entry),
                          pb.dma("sync", bvo[0:1, :], ad["bvo"][0:1, 0:D], deps=entry)]
                b["bqk"] = bqk
                b["bqk_tok"] = ctoks[-2]
                b["bv_rows"] = bvo[0:1, 0:D]
            esPT = ExitStack()
            svp = pb.es; pb.es = esPT
            b["pt"] = (ps(pb, which + "ptA0", [128, 128], BF16),
                       ps(pb, which + "ptA1", [128, 128], BF16))
            b["dram_scr"] = sb(pb, which + "dscr", [128, 2 * D], F32)
            pb.es = svp
            ln1 = emit_ln_fm(pb, None, xkv_fm, b["stats"], b["tmp_bf"][:, 0:D],
                             b["tmp_bf"], b["identb"], b["pt"], n_tc=S_tc,
                             tok_x=list(in_toks) + list(entry),
                             dst_off=0, dst_stride=KVT,
                             src_dram=xi, dram_scr=b["dram_scr"], dma_eng="gpsimd")
            esPT.close()
            esB = ExitStack()
            svx = pb.es; pb.es = esA
            ctx_tm = sb(pb, which + "ctx", [128, S_tc * 1024], BF16, side="right")
            pb.es = svx
            with esB:
                svb = pb.es; pb.es = esB
                qh_ = sb(pb, which + "qh0", [64, KVT], BF16)
                kh_ = sb(pb, which + "kh0", [64, KVT], BF16)
                vh_ = sb(pb, which + "vh0", [128, S_tc * 65], BF16)
                wh_ = sb(pb, which + "wh0", [128, 3 * 512], BF16)
                bias_ = sb(pb, which + "bias0", [128, S_tc], F32)
                b2 = dict(
                    qh0=qh_, qh1=qh_, kh0=kh_, kh1=kh_, vh0=vh_, vh1=vh_,
                    wh0=wh_, wh1=wh_,
                    pb0=sb(pb, which + "pb0", [128, 512], BF16),
                    pb1=sb(pb, which + "pb1", [128, 512], BF16),
                    pb2=sb(pb, which + "pb2", [128, 512], BF16),
                    bias0=bias_, bias1=bias_,
                )
                if lora:
                    for n2 in ("q", "k", "v"):
                        b2["t1_" + n2] = sb(pb, which + "t1" + n2, [8, KVT], BF16)
                        b2["bT_" + n2] = sb(pb, which + "bT" + n2, [128, 64], BF16)
                        b2["aT_" + n2] = sb(pb, which + "aT" + n2, [8, D], BF16)
                b2["ps_s0"] = ps(pb, which + "pss0", [128, 512])
                b2["ps_s1"] = ps(pb, which + "pss1", [128, 512])
                b2["ps_s2"] = ps(pb, which + "pss2", [128, 512])
                b2["ps_o"] = [ps(pb, which + "pso%d" % i, [128, 65]) for i in range(4)]
                b2["ps_q"] = ps(pb, which + "psq", [128, 512])
                pb.es = svb
                b.update(b2)
                b["ctx_tm"] = ctx_tm
                b["xkv_tok"] = ln1
                b["const_tok"] = ctoks[-1]
                b["entry_tok"] = pb.all_toks()
                lw = []
                if lora:
                    for n2 in ("q", "k", "v"):
                        tb1 = emit_weight_load_off(pb, [ad["bT" + n2][cc] for cc in range(DC)],
                                                   b["wstage"], b["bT_" + n2], 8, 0,
                                                   deps=list(entry), eng_dma="sync")
                        ta1 = pb.dma("sync", b["wstage"][0:8, 0:D], ad["aT" + n2], deps=[tb1])
                        ta2 = pb.emit("scalar", lambda e, n2=n2, b=b: e.copy(
                            b["aT_" + n2][0:8, 0:D], b["wstage"][0:8, 0:D]), deps=[ta1])
                        pb.tail[("stage", b["wstage"].name)] = ta2
                        lw.append(ta2)
                b["lw_tok"] = tuple(lw)
                slopes = ([2.0 ** (-0.5 - 0.5 * i) for i in range(16)] if lora else None)
                cfg = dict(nq_tc=S_tc, heads=16, dh=64, tanh=tanh, alibi=lora, lora=lora,
                           prefix=False, single_hbuf=True, slopes=slopes,
                           wq=[[ad["wq"][h, cc] for cc in range(DC)] for h in range(16)],
                           wk=[[ad["wk"][h, cc] for cc in range(DC)] for h in range(16)],
                           wv=[[ad["wv"][h, cc] for cc in range(DC)] for h in range(16)])
                attn_toks = emit_attn(pb, b, cfg)
            esKV2.close()
            esC = ExitStack()
            with esC:
                svc = pb.es; pb.es = esC
                b["pt"] = (ps(pb, which + "ptC0", [128, 128], BF16),
                           ps(pb, which + "ptC1", [128, 128], BF16))
                b["ps_q"] = ps(pb, which + "psqC", [128, 512])
                ctx_fm = sb(pb, which + "ctxfm", [128, DC * KVT], BF16)
                wo_bf = sb(pb, which + "wo", [128, DC * 1024], BF16)
                projscr = sb(pb, which + "pscr", [128, 1024], F32)
                boC = sb(pb, which + "boC", [1, D], BF16)
                pb.es = svc
                entryC = pb.all_toks()
                b["projscr"] = projscr
                tbo = (pb.dma("sync", boC[0:1, :], ad["bvo"][0:1, D:2 * D],
                              deps=list(entryC)) if tanh else None)
                _, wo_last = emit_weight_load(pb, [ad["wo"][cc] for cc in range(DC)],
                                              b["wstage"], wo_bf, DC, 1024,
                                              deps=list(entryC), eng_dma="sync")
                fm_toks = emit_tm_to_fm(pb, ctx_tm, ctx_fm, b["identb"], b["pt"],
                                        n_tc=S_tc, deps_per_tile=None, src_cols=1024,
                                        dst_cols=KVT,
                                        extra_deps=[attn_toks[-1]] + list(entryC))
                pr = emit_proj_residual(
                    pb, b, ctx_fm, wo_bf, S_tc, bP["zt"],
                    act=(AF.Tanh if tanh else None),
                    bias_row=(boC[0:1, 0:D] if tanh else None),
                    src_tok=[fm_toks[-1]], w_tok=[wo_last],
                    x_tok=None, x_dram=xi, xo_dram=xo)
        return pr[-1:]

    def fused_mingru_s(xi, xo, in_toks):
        L = 3
        mg = [C["r_x1"][t] for t in range(S_tc)]
        esG = ExitStack()
        with esG:
            sv = pb.es; pb.es = esG
            b = dict(bP)
            b["lnt0"] = sb(pb, "g_lnt0", [128, D], F32)
            b["lnt1"] = sb(pb, "g_lnt1", [128, D], F32)
            b["gbrows"] = sb(pb, "g_gbrows_b", [1, 6 * D], BF16)
            b["g_bc"] = sb(pb, "g_g_bc", [128, D], F32)
            b["b_bc"] = sb(pb, "g_b_bc", [128, D], F32)
            b["dram_scr"] = sb(pb, "g_dram_scr", [128, 2 * D], F32)
            b["sqscr"] = sb(pb, "g_sqscr", [128, D], F32)
            b["lnscr"] = sb(pb, "g_lnscr", [128, D], F32)
            b["bcols"] = sb(pb, "g_bcols_b", [128, 48], F32)
            b["nbcols"] = sb(pb, "g_nbcols", [128, 48], F32)
            b["xn_fm"] = sb(pb, "g_xn_fm", [128, DC * S // 2], BF16)
            b["h_all"] = sb(pb, "g_h_all", [128, DC * S // 2], BF16)
            b["hmid"] = sb(pb, "g_hmid", [128, 8], F32)
            b["w_bf"] = sb(pb, "g_w_bf", [128, 2 * DC * 128], BF16)
            b["stage"] = sb(pb, "g_stage", [128, 512], F32)
            b["cbuf"] = sb(pb, "g_cbuf", [128, S // 2], F32)
            b["ubuf"] = sb(pb, "g_ubuf", [128, S // 2], F32)
            b["zscr"] = sb(pb, "g_zscr", [128, 512], F32)
            b["r1scr"] = sb(pb, "g_r1", [128, 512], F32)
            b["s1scr"] = sb(pb, "g_s1", [128, 512], F32)
            b["ps_g"] = ps(pb, "g_psg", [128, 512])
            b["ps_u"] = ps(pb, "g_psu", [128, 512])
            b["ps_bc"] = ps(pb, "g_psbc", [128, 512])
            b["pt"] = (ps(pb, "g_pt0", [128, 128], BF16), ps(pb, "g_pt1", [128, 128], BF16))
            pb.es = sv
            entry = pb.all_toks()
            gc1 = pb.dma("sync", b["gbrows"][0:1, :], C["g"]["gbrows"], deps=entry)
            gc2 = pb.dma("sync", b["bcols"][:, :], C["g"]["bcols"], deps=entry)
            tnb = pb.emit("vector", lambda e, b=b: e.tensor_scalar(
                b["nbcols"][:, :], b["bcols"][:, :], -1.0, None, ALU.mult, ALU.bypass),
                deps=[gc2], strict=True)
            b["gb_tok"] = gc1
            ln0 = emit_ln_stream(pb, b, xi, mg, S_tc, tok0=list(in_toks) + list(entry))
            cur = [ln0[-1]]
            HS_tc = S_tc // 2
            HS = S // 2
            for l in range(L):
                gb = emit_gb_bcast(pb, b, (0 + l) * D, (L + l) * D)
                for th in range(2):
                    tbase = th * HS_tc
                    lnt = emit_ln_fm(pb, None, b["xn_fm"], b["stats"], b["sqscr"][:, 0:D],
                                     b["tmp_bf"], b["identb"], b["pt"], n_tc=HS_tc,
                                     tok_x=cur + list(gb), dst_stride=HS,
                                     affine_wb=(b["g_bc"], b["b_bc"], b["lnscr"]),
                                     src_dram=[mg[tbase + t] for t in range(HS_tc)],
                                     dram_scr=b["dram_scr"], dma_eng="gpsimd")
                    scan_toks = []
                    wl_prev = ()
                    for cc in range(DC):
                        wch = [C["g"]["w"][l, dc][:, cc * 128:cc * 128 + 128] for dc in range(DC)]
                        wch += [C["g"]["w"][l, dc][:, 1024 + cc * 128:1024 + cc * 128 + 128]
                                for dc in range(DC)]
                        wl_tok = emit_weight_load_off(pb, wch, b["stage"], b["w_bf"], 128, 0,
                                                      deps=list(wl_prev) or cur, eng_dma="sync")
                        cu_toks = []
                        mmg = mmh = None
                        for tt in range(HS // 512):
                            for dc in range(DC):
                                dep = [wl_tok, lnt] + (list(pb.tail.get(("mgs_pg",), ())) if dc == 0 else [])
                                mmg = pb.emit("tensor", lambda e, cc=cc, tt=tt, dc=dc, b=b: e.matmul(
                                    b["ps_g"][:, 0:512],
                                    b["w_bf"][:, dc * 128: dc * 128 + 128],
                                    b["xn_fm"][:, dc * HS + tt * 512: dc * HS + tt * 512 + 512],
                                    start=(dc == 0), stop=(dc == DC - 1)), deps=dep)
                            for dc in range(DC):
                                dep = [wl_tok, lnt] + (list(pb.tail.get(("mgs_ph",), ())) if dc == 0 else [])
                                mmh = pb.emit("tensor", lambda e, cc=cc, tt=tt, dc=dc, b=b: e.matmul(
                                    b["ps_u"][:, 0:512],
                                    b["w_bf"][:, DC * 128 + dc * 128: DC * 128 + dc * 128 + 128],
                                    b["xn_fm"][:, dc * HS + tt * 512: dc * HS + tt * 512 + 512],
                                    start=(dc == 0), stop=(dc == DC - 1)), deps=dep)
                            bcol = b["bcols"][:, l * 16 + cc: l * 16 + cc + 1]
                            nbcol = b["nbcols"][:, l * 16 + cc: l * 16 + cc + 1]
                            bcol2 = b["bcols"][:, l * 16 + 8 + cc: l * 16 + 8 + cc + 1]
                            tc1 = pb.emit("scalar", lambda e, tt=tt, nbcol=nbcol, b=b: e.activation(
                                b["cbuf"][:, tt * 512:tt * 512 + 512], b["ps_g"][:, 0:512],
                                AF.Sigmoid, bias=nbcol, scale=-1.0),
                                deps=[mmg, tnb] + list(pb.tail.get(("mgs_cu",), ()))
                                     + list(pb.tail.get(("mgs_scan",), ())), strict=True)
                            tz = pb.emit("scalar", lambda e, tt=tt, bcol=bcol, b=b: e.activation(
                                b["zscr"][:, 0:512], b["ps_g"][:, 0:512], AF.Sigmoid,
                                bias=bcol, scale=1.0), deps=[tc1, gc2], strict=True)
                            pb.tail[("mgs_pg",)] = (tz,)
                            tr1 = pb.emit("scalar", lambda e, bcol2=bcol2, b=b: e.activation(
                                b["r1scr"][:, 0:512], b["ps_u"][:, 0:512], AF.Relu,
                                bias=bcol2, scale=1.0), deps=[mmh, gc2, tz], strict=True)
                            ts1 = pb.emit("scalar", lambda e, bcol2=bcol2, b=b: e.activation(
                                b["s1scr"][:, 0:512], b["ps_u"][:, 0:512], AF.Sigmoid,
                                bias=bcol2, scale=1.0), deps=[tr1], strict=True)
                            pb.tail[("mgs_ph",)] = (ts1,)
                            tm1_ = pb.emit("vector", lambda e, b=b: e.tensor_scalar_min(
                                b["s1scr"][:, 0:512], b["s1scr"][:, 0:512], 0.5),
                                deps=[ts1], strict=True)
                            tg_ = pb.emit("vector", lambda e, b=b: e.tensor_tensor(
                                b["r1scr"][:, 0:512], b["r1scr"][:, 0:512], b["s1scr"][:, 0:512],
                                ALU.add), deps=[tm1_, tr1], strict=True)
                            tu = pb.emit("vector", lambda e, tt=tt, b=b: e.tensor_tensor(
                                b["ubuf"][:, tt * 512:tt * 512 + 512], b["zscr"][:, 0:512],
                                b["r1scr"][:, 0:512], ALU.mult),
                                deps=[tg_, tz] + list(pb.tail.get(("mgs_scan",), ())), strict=True)
                            pb.tail[("mgs_cu",)] = (tu,)
                            cu_toks += [tc1, tu]
                        init = 0.5 if th == 0 else b["hmid"][:, cc:cc + 1]
                        hdep = list(pb.tail.get(("mgs_h",), ())) +                                (list(pb.tail.get(("mgs_hm",), ())) if th == 1 else [])
                        tscan = pb.emit("vector", lambda e, cc=cc, b=b, init=init: e.tensor_tensor_scan(
                            b["h_all"][:, cc * HS:(cc + 1) * HS], b["cbuf"][:, 0:HS],
                            b["ubuf"][:, 0:HS], init, ALU.mult, ALU.add),
                            deps=cu_toks + hdep, strict=True)
                        pb.tail[("mgs_scan",)] = (tscan,)
                        if th == 0:
                            thm = pb.emit("vector", lambda e, cc=cc, b=b: e.tensor_copy(
                                b["hmid"][:, cc:cc + 1], b["h_all"][:, cc * HS + HS - 1: cc * HS + HS]),
                                deps=[tscan], strict=True)
                            pb.tail[("mgs_hm",)] = pb.tail.get(("mgs_hm",), ()) + (thm,)
                            scan_toks.append(thm)
                        scan_toks.append(tscan)
                        wl_prev = (mmg, mmh)
                    if th == 1:
                        pb.tail[("mgs_hm",)] = ()
                    nxt = []
                    rm_prev = ()
                    for tc in range(HS_tc):
                        tcg = tbase + tc
                        tl = pb.dma("gpsimd", bP["zt"][:, 0:D], mg[tcg],
                                    deps=scan_toks + list(rm_prev))
                        tprev = tl
                        for cc in range(DC):
                            phh = cc % 2
                            tpt = pb.emit("tensor", lambda e, tc=tc, cc=cc, phh=phh, b=b: e.transpose(
                                b["pt"][phh][:, 0:128],
                                b["h_all"][:, cc * HS + tc * 128: cc * HS + tc * 128 + 128],
                                b["identb"][:, 0:128]),
                                deps=[scan_toks[-1]] + list(pb.tail.get(("mgs_pt%d" % phh,), ())))
                            ta = pb.emit("vector", lambda e, tc=tc, cc=cc, phh=phh, b=b: e.tensor_tensor(
                                bP["zt"][:, cc * 128:(cc + 1) * 128], b["pt"][phh][:, 0:128],
                                bP["zt"][:, cc * 128:(cc + 1) * 128], ALU.add),
                                deps=[tpt, tl], strict=True)
                            pb.tail[("mgs_pt%d" % phh,)] = (ta,)
                            tprev = ta
                        ts = pb.dma("gpsimd", mg[tcg], bP["zt"][:, 0:D], deps=[tprev])
                        rm_prev = (ts,)
                        nxt.append(ts)
                    cur = nxt[-1:]
            fin = []
            fprev = ()
            for tc in range(S_tc):
                tl1 = pb.dma("gpsimd", bP["zt"][:, 0:D], mg[tc], deps=cur + list(fprev))
                tl2 = pb.dma("sync", b["lnscr"][:, 0:D], xi[tc], deps=cur + list(fprev))
                ta = pb.emit("vector", lambda e, tc=tc, b=b: e.tensor_tensor(
                    bP["zt"][:, 0:D], b["lnscr"][:, 0:D], bP["zt"][:, 0:D], ALU.add),
                    deps=[tl1, tl2], strict=True)
                ts = pb.dma("gpsimd", xo[tc], bP["zt"][:, 0:D], deps=[ta])
                fprev = (ts,)
                fin.append(ts)
        return fin[-1:]

    upto = C.get("upto", 8)
    t = list(C["ret_toks"])
    cur = xa
    steps = [("m0a", fused_moe_s), ("aa", fused_attn_s), ("m0b", fused_moe_s),
             ("at", fused_attn_s), ("f1a", fused_moe_s), ("mg", None),
             ("f1b", fused_moe_s)]
    done = 1
    for name, fn in steps:
        if done >= upto:
            break
        nxt = xb if cur is xa else xa
        if name == "mg":
            t = fused_mingru_s(cur, nxt, t)
        else:
            t = fn(name, cur, nxt, t)
        cur = nxt
        done += 1
    st = []
    prevd = ()
    for tc in range(S_tc):
        tl = pb.dma("gpsimd", bP["zt"][:, 0:D], cur[tc], deps=list(t) + list(prevd))
        ts = pb.dma("sync", C["xout"][tc], bP["zt"][:, 0:D], deps=[tl])
        prevd = (ts, tl)
        st.append(ts)
    pb.emit("sync", lambda e: e.nop(), deps=st)


def _fused_host_inputs(z, x_full, nq=16):
    import math
    S = 2 * nq * 128
    S_tc = 2 * nq
    H8 = 8
    perm = np.concatenate([np.arange(0, 128, 2), np.arange(1, 128, 2)])
    sv = (np.arange(0, 128, 2) + 0.4 * 128) / (1.4 * 128)
    power = (np.arange(S, dtype=np.float64) / 512)[:, None]
    scale = sv[None, :] ** power
    inv_freq = 1.0 / 10000.0 ** (np.arange(64, dtype=np.float64) / 64)
    ang = np.arange(S, dtype=np.float64)[:, None] * inv_freq[None, :]
    sin_, cos_ = np.sin(ang), np.cos(ang)
    tabs = np.stack([np.ascontiguousarray(np.concatenate([a.T, a.T], 0).astype(BF))
                     for a in (cos_ * scale, sin_ * scale, cos_ / scale, sin_ / scale)])
    gammas = 1.0 - np.exp(np.linspace(np.log(1 / 32), np.log(1 / 512), H8))
    pp = np.arange(128)
    dmask = np.concatenate([np.where(pp[None, :] >= pp[:, None],
                                     g ** (pp[None, :] - pp[:, None]).astype(np.float64), 0.0).astype(np.float32)
                            for g in gammas], 1)
    gcols = np.concatenate([np.stack([g ** (pp + 1) for g in gammas], 1),
                            np.stack([g ** (127 - pp) for g in gammas], 1)], 1).astype(np.float32)
    base = dict(
        ident=IDENT_BF, identf_d=np.eye(128, dtype=np.float32),
        onesc=np.ones((1, 128), BF),
        r_wq=np.ascontiguousarray(z["L0_attn_a_WQ"][:, :, :, perm].reshape(2, H8, 8, 128, 128)),
        r_wk=np.ascontiguousarray(z["L0_attn_a_WK"][:, :, :, perm].reshape(2, H8, 8, 128, 128)),
        r_wv=np.ascontiguousarray(z["L0_attn_a_WV"].reshape(2, H8, 8, 128, 128)),
        r_wg=np.ascontiguousarray(z["L0_attn_a_WG"].reshape(2, 8, 128, D)),
        r_wo=np.ascontiguousarray(z["L0_attn_a_WO"].reshape(2, 8, 128, D)),
        r_w1=np.ascontiguousarray(z["L0_attn_a_ffn_w1"].reshape(2, 8, 128, 4096)),
        r_w2=np.ascontiguousarray(z["L0_attn_a_ffn_w2"].reshape(2, 32, 128, D)),
        r_gbrows=np.concatenate([np.concatenate(
            [z["L0_attn_a_ln1_w"][i], z["L0_attn_a_ln1_b"][i],
             z["L0_attn_a_ln2_w"][i], z["L0_attn_a_ln2_b"][i],
             z["L0_attn_a_gn_w"][i], z["L0_attn_a_gn_b"][i]]) for i in range(2)]
            ).astype(BF)[None, :],
        r_b1cols=np.ascontiguousarray(np.concatenate(
            [z["L0_attn_a_ffn_b1"][i].reshape(32, 128).T for i in range(2)], 1)),
        r_b2rows=z["L0_attn_a_ffn_b2"].ravel().astype(BF)[None, :],
        r_xpos=tabs, r_dmask=dmask, r_gcols=gcols,
        m0_gwT=np.ascontiguousarray(z["L0_ffn_gate_w"].T.reshape(8, 128, 8)
                                    .transpose(1, 0, 2).reshape(128, 64)).astype(np.float32),
        m0_gu=np.ascontiguousarray(z["L0_ffn_gate_up_proj"].reshape(8, 8, 128, 2, 8, 128)
                                   .transpose(0, 3, 4, 2, 1, 5).reshape(8, 16, 128, 1024)),
        m0_down=np.ascontiguousarray(z["L0_ffn_down_proj"].reshape(8, 8, 128, 1024)),
        m1_gwT=np.ascontiguousarray(z["L1_ffn_router_w"].reshape(8, 128, 8)
                                    .transpose(1, 0, 2).reshape(128, 64)).astype(np.float32),
        m1_gu=np.ascontiguousarray(np.concatenate([z["L1_ffn_w1"], z["L1_ffn_w3"]], axis=2)
                                   .reshape(8, 8, 128, 2, 8, 128)
                                   .transpose(0, 3, 4, 2, 1, 5).reshape(8, 16, 128, 1024)),
        m1_down=np.ascontiguousarray(z["L1_ffn_w2"].reshape(8, 8, 128, 1024)),
        m1_sgu=np.ascontiguousarray(np.concatenate([z["L1_ffn_sw1"], z["L1_ffn_sw3"]], axis=1)
                                    .reshape(8, 128, 2, 8, 128)
                                    .transpose(2, 3, 1, 0, 4).reshape(16, 128, 1024)),
        m1_sdown=np.ascontiguousarray(z["L1_ffn_sw2"].reshape(8, 128, 1024)),
        m1_ebias=np.ascontiguousarray(np.broadcast_to(z["L1_ffn_expert_bias"], (128, 8))).astype(np.float32),
        g_w=np.ascontiguousarray(z["L1_attn_b_W"].reshape(3, 8, 128, 2048)),
        g_gbrows=np.concatenate([z["L1_attn_b_ln_gamma"].ravel(),
                                 z["L1_attn_b_ln_beta"].ravel()]).astype(BF)[None, :],
        g_bcols=np.ascontiguousarray(z["L1_attn_b_b"].reshape(3, 16, 128)
                                     .transpose(2, 0, 1).reshape(128, 48)),
    )
    ppp, fff = np.arange(128)[:, None], np.arange(512)[None, :]
    base["cmask"] = np.concatenate(
        [np.where(ppp + idx * 128 > fff, -1e30, 0) for idx in range(4)], 1).astype(BF)
    for pre, Wq, Wk, Wv, Wo in (
        ("aa_", np.ascontiguousarray(z["L0_attn_b_Wq"].T), np.ascontiguousarray(z["L0_attn_b_Wk"].T),
         np.ascontiguousarray(z["L0_attn_b_Wv"].T), np.ascontiguousarray(z["L0_attn_b_Wp"].T)),
        ("at_", z["L1_attn_a_Wq"], z["L1_attn_a_Wk"], z["L1_attn_a_Wv"], z["L1_attn_a_Wo"])):
        base[pre + "wq"] = np.ascontiguousarray(Wq.reshape(8, 128, 16, 64).transpose(2, 0, 1, 3))
        base[pre + "wk"] = np.ascontiguousarray(Wk.reshape(8, 128, 16, 64).transpose(2, 0, 1, 3))
        base[pre + "wv"] = np.ascontiguousarray(Wv.reshape(8, 128, 16, 64).transpose(2, 0, 1, 3))
        base[pre + "wo"] = np.ascontiguousarray(Wo.reshape(8, 128, 1024))
    for n2 in "qkv":
        base["aa_bT" + n2] = np.ascontiguousarray(z["L0_attn_b_B" + n2].T.reshape(8, 128, 8))
        base["aa_aT" + n2] = np.ascontiguousarray(z["L0_attn_b_A" + n2].T)
    bqk = np.zeros((64, 32), np.float32)
    for h in range(16):
        bqk[:, 2 * h] = z["L1_attn_a_bq"][h * 64:(h + 1) * 64]
        bqk[:, 2 * h + 1] = z["L1_attn_a_bk"][h * 64:(h + 1) * 64]
    base["at_bqk"] = bqk
    base["at_bvo"] = np.concatenate([z["L1_attn_a_bv"], z["L1_attn_a_bo"]]).astype(BF)[None, :]
    base["aa_coef"] = np.zeros((128, 2 * S_tc), np.float32)
    base["aa_kmask"] = np.zeros((128, S_tc), np.float32)
    base["at_kmask"] = np.zeros((128, S_tc), np.float32)
    base["aa_kpos"] = (np.arange(S_tc)[None, :] * 128
                       + np.arange(128)[:, None]).astype(np.float32)
    in_maps = []
    for c in range(8):
        b, h = c // 2, c % 2
        im = dict(base)
        im["xin"] = np.ascontiguousarray(x_full[b].reshape(S_tc, 128, 1024))
        hf = np.zeros((128, 4), np.float32)
        hf[:, 0] = 1.0 - h
        hf[:, 1] = h
        im["hflags"] = hf
        in_maps.append(im)
    return in_maps


_FUSED_CACHE = {}


def kernel_fused(**z):
    nq = 16
    x_full = np.asarray(z["L0_attn_a_X"], np.float32)
    if nq not in _FUSED_CACHE:
        _FUSED_CACHE[nq] = build_fused(nq=nq)
    nc = _FUSED_CACHE[nq]
    in_maps = _fused_host_inputs(z, x_full, nq=nq)
    res = run_bass_kernel_spmd(nc, in_maps, core_ids=list(range(8)))
    out = np.stack([np.asarray(res.results[2 * b]["xout"]).reshape(2 * nq * 128, 1024)
                    for b in range(4)])
    return out


kernel_multi = kernel
kernel = kernel_fused
